# version 08
# baseline (speedup 1.0000x reference)
import contextlib
import numpy as np
import ml_dtypes
import concourse.bass as bass
import concourse.mybir as mybir
from concourse.bass_utils import run_bass_kernel_spmd

F32 = mybir.dt.float32
BF16 = mybir.dt.bfloat16
AF = mybir.ActivationFunctionType
ALU = mybir.AluOpType
AX = mybir.AxisListType
BF = ml_dtypes.bfloat16

L = 2048
D = 1024
H = 4
DK = 256
NT = L // 128
NC = D // 128
NCH = L // 32
E = 8
FF = 4096
NF = FF // 128
SLAB = 1024
NSLAB = L // SLAB

DEBUG_TAPS = set()
STOP_AFTER = None


class _Cut(Exception):
    pass


class Ctx:
    def __init__(self, nc):
        self.nc = nc
        self.sem = {}
        self.cnt = {}
        self.dbg = []
        self.pu_last = {}
        self.stack = contextlib.ExitStack()

    def new_sem(self, name):
        self.sem[name] = self.stack.enter_context(self.nc.semaphore(name))
        self.cnt[name] = 0

    def inc(self, instr, name, n=1):
        self.cnt[name] += n
        instr.then_inc(self.sem[name], n)
        return self.cnt[name]

    def wait(self, eng, name, v=None):
        eng.wait_ge(self.sem[name], self.cnt[name] if v is None else v)

    def sb(self, name, shape, dtype):
        return self.stack.enter_context(self.nc.sbuf_tensor(name, shape, dtype))

    def pu_read(self, unit, semname, v=None):
        self.pu_last[unit] = (semname, self.cnt[semname] if v is None else v)

    def pu_guard(self, eng, unit):
        rec = self.pu_last.get(unit)
        if rec is not None:
            self.wait(eng, rec[0], rec[1])

    def tap(self, name, src_ap, semname):
        if name not in DEBUG_TAPS:
            return
        dram = self.nc.dram_tensor("dbg_" + name, list(src_ap.shape), src_ap.dtype,
                                   kind="ExternalOutput").ap()
        self.dbg.append((name, dram, src_ap, semname, self.cnt[semname]))

    def checkpoint(self, name):
        if STOP_AFTER != name:
            return False
        with self.nc.Block(f"cut_{name}") as blk:
            @blk.sync
            def _(sy):
                for s in ("pe", "dv", "ac", "gp", "din", "dwt", "dout"):
                    self.wait(sy, s)
        return True

    def flush_taps(self):
        if not self.dbg:
            return
        taps = self.dbg
        self.dbg = []
        self._tapn = getattr(self, "_tapn", 0) + 1
        with self.nc.Block(f"taps{self._tapn}") as blk:
            @blk.sync
            def _(sy):
                for name, dram, src, semname, v in taps:
                    sy.wait_ge(self.sem[semname], v)
                    self.inc(sy.dma_start(out=dram, in_=src), "dout", 16)


def build_program():
    nc = bass.Bass("TRN2", target_bir_lowering=False)
    c = Ctx(nc)

    def din(name, shape, dtype=F32):
        return nc.dram_tensor(name, list(shape), dtype, kind="ExternalInput").ap()

    x_hbm = din("x", (L, D))
    Wq_h = din("Wq", (D, D)); Wk_h = din("Wk", (D, D)); Wv_h = din("Wv", (D, D))
    Wb_h = din("Wb", (128, 32))
    cq_h = din("conv_q", (128, 32)); ck_h = din("conv_k", (128, 32)); cv_h = din("conv_v", (128, 32))
    firl_h = din("fir_long", (128, 512)); firs_h = din("fir_short", (128, 40))
    W1_h = din("W1", (1040, D)); b1_h = din("b1", (128, 8))
    W2_h = din("W2", (128, 32))
    bbT_h = din("bbT", (4, 4)); b2c_h = din("b2c", (4, 1))
    ltm_h = din("log_temp", (1, 4)); crl_h = din("crl", (1, 4))
    rmsw_h = din("rms_w", (1, DK))
    Wo_h = din("Wo", (D, D))
    rw_h = din("router_w", (128, 64))
    mwv_h = din("m_wv", (E * D, FF)); msv_h = din("m_sv", (128, E * NF))
    mw_h = din("m_w", (E * D, FF)); ms_h = din("m_s", (128, E * NF))
    mw1_h = din("m_w1", (E * FF, D)); ms1_h = din("m_s1", (E, D))
    identb_h = din("ident_bf", (128, 128), BF16)
    identf_h = din("ident_f32", (128, 128))
    masks_h = din("masks_c", (128, 128))
    clamp_h = din("clamp", (128, 4))
    onesc_h = din("ones_col", (128, 1), BF16)
    onesr_h = din("ones_row", (1, 128))
    onesrbf_h = din("ones_row_bf", (1, 128), BF16)
    cb_h = din("cbias", (128, 4))

    out_hbm = nc.dram_tensor("out", [L, D], F32, kind="ExternalOutput").ap()
    wvt_d = nc.dram_tensor("wvt_d", [E * NF * 128, NC * 128], BF16).ap()
    wt_d = nc.dram_tensor("wt_d", [E * NF * 128, NC * 128], BF16).ap()
    w1t_d = nc.dram_tensor("w1t_d", [E * NF * 128, D], BF16).ap()

    for s in ("din", "dwt", "dpp", "dout", "pe", "dv", "ac", "gp",
               "dw0", "dw1", "dw2", "dw3", "ds1", "dx0", "dx1"):
        c.new_sem(s)

    pu = [c.stack.enter_context(nc.psum_tensor(f"pu{i}", [128, 512], F32)) for i in range(8)]
    pub = [p[:].bitcast(BF16) for p in pu]

    identb = c.sb("identb", [128, 128], BF16)
    identf = c.sb("identf", [128, 128], F32)
    masks = c.sb("masks", [128, 128], F32)
    clampt = c.sb("clampt", [128, 4], F32)
    onescol = c.sb("onescol", [128, 1], BF16)
    onesrow = c.sb("onesrow", [1, 128], F32)
    cb = c.sb("cb", [128, 4], F32)
    cq_sb = c.sb("cq_sb", [128, 32], F32)
    ck_sb = c.sb("ck_sb", [128, 32], F32)
    cv_sb = c.sb("cv_sb", [128, 32], F32)
    firs_sb = c.sb("firs_sb", [128, 40], F32)
    firl_sb = c.sb("firl_sb", [128, 512], F32)
    b1_sb = c.sb("b1_sb", [128, 8], F32)
    bbT_sb = c.sb("bbT_sb", [4, 4], F32)
    b2c_sb = c.sb("b2c_sb", [4, 1], F32)
    gb_sb = c.sb("gb_sb", [4, 4], F32)
    ltm_sb = c.sb("ltm_sb", [1, 4], F32)
    crl_sb = c.sb("crl_sb", [1, 4], F32)
    rmsw_row = c.sb("rmsw_row", [1, DK], F32)
    w1s_sb = c.sb("w1s_sb", [16, D], BF16)
    w2_sb = c.sb("w2_sb", [128, 32], BF16)
    rw_sb = c.sb("rw_sb", [128, 64], BF16)
    wb_sb = c.sb("wb_sb", [128, 32], BF16)
    sv_sb = c.sb("sv_sb", [128, E * NF], F32)
    s_sb = c.sb("s_sb", [128, E * NF], F32)
    invtemp4 = c.sb("invtemp4", [4, 4], F32)
    crb = c.sb("crb", [128, 4], F32)
    rmsw_bc = c.sb("rmsw_bc", [128, DK], F32)
    onesrow_bf = c.sb("onesrow_bf", [1, 128], BF16)

    es_o = contextlib.ExitStack()
    es_x = contextlib.ExitStack()
    o_fm = es_o.enter_context(nc.sbuf_tensor("o_fm", [128, NC * L], BF16))
    x_ln_fm = es_x.enter_context(nc.sbuf_tensor("x_ln_fm", [128, NC * L], BF16))

    with nc.Block("ph0") as blk:
        @blk.sync
        def _(sy):
            for dst, src in [
                (identb[:], identb_h), (identf[:], identf_h), (masks[:], masks_h),
                (clampt[:], clamp_h), (onescol[:], onesc_h), (onesrow[:], onesr_h),
                (cb[:], cb_h),
                (cq_sb[:], cq_h), (ck_sb[:], ck_h), (cv_sb[:], cv_h),
                (firs_sb[:], firs_h), (firl_sb[:], firl_h), (b1_sb[:], b1_h),
                (bbT_sb[:], bbT_h), (b2c_sb[:], b2c_h),
                (ltm_sb[:], ltm_h), (crl_sb[:], crl_h), (rmsw_row[:], rmsw_h),
                (sv_sb[:], msv_h), (s_sb[:], ms_h),
                (onesrow_bf[:], onesrbf_h),
            ]:
                c.inc(sy.dma_start(out=dst, in_=src, allow_slow_non_contiguous=True),
                      "din", 16)

        @blk.gpsimd
        def _(g):
            c.inc(g.dma_start(out=w1s_sb[:], in_=W1_h[1024:1040, :]), "dwt", 16)
            c.inc(g.dma_start(out=w2_sb[:], in_=W2_h), "dwt", 16)
            c.inc(g.dma_start(out=rw_sb[:], in_=rw_h), "dwt", 16)
            c.inc(g.dma_start(out=wb_sb[:], in_=Wb_h), "dwt", 16)

    v_consts = c.cnt["din"]

    with nc.Block("ph0b") as blk:
        @blk.scalar
        def _(s):
            c.wait(s, "din", v_consts)
            c.inc(s.activation(ltm_sb[:], ltm_sb[:], AF.Exp), "ac")
            c.wait(s, "ac")
            c.inc(s.activation(ltm_sb[:], ltm_sb[:], AF.Ln, bias=cb[0:1, 0:1]), "ac")
            c.inc(s.activation(crl_sb[:], crl_sb[:], AF.Sigmoid), "ac")

        @blk.vector
        def _(v):
            c.wait(v, "ac")
            c.inc(v.tensor_scalar(ltm_sb[:], ltm_sb[:], 1e-4, None, op0=ALU.add), "dv")
            c.wait(v, "dv")
            c.inc(v.reciprocal(ltm_sb[:], ltm_sb[:]), "dv")
            c.wait(v, "dv")
            v.tensor_scalar(crl_sb[:], crl_sb[:], 0.5, None, op0=ALU.mult)
            c.inc(v.tensor_scalar(gb_sb[:], bbT_sb[:], b2c_sb[:, 0:1], None, op0=ALU.add), "dv")

        @blk.tensor
        def _(t):
            c.wait(t, "dv")
            t.matmul(pu[0][0:4, 0:4], onesrow[0:1, 0:4], ltm_sb[0:1, :], start=True, stop=True)
            t.matmul(pu[1][:, 0:4], onesrow[0:1, :], crl_sb[0:1, :], start=True, stop=True)
            c.inc(t.matmul(pu[2][:, 0:DK], onesrow[0:1, :], rmsw_row[0:1, :], start=True, stop=True), "pe")

        @blk.vector
        def _(v):
            c.wait(v, "pe")
            v.tensor_copy(invtemp4[:], pu[0][0:4, 0:4])
            v.tensor_copy(crb[:], pu[1][:, 0:4])
            c.inc(v.tensor_copy(rmsw_bc[:], pu[2][:, 0:DK]), "dv")
            for u in (0, 1, 2):
                c.pu_read(u, "dv")

    with contextlib.ExitStack() as ph1:
        x_rot = [ph1.enter_context(nc.sbuf_tensor(f"x_rot{i}", [128, D], F32)) for i in range(2)]
        xln_rot = [ph1.enter_context(nc.sbuf_tensor(f"xln_rot{i}", [128, D], BF16)) for i in range(2)]
        ln_scr = ph1.enter_context(nc.sbuf_tensor("ln_scr", [128, 8], F32))
        sq_junk = ph1.enter_context(nc.sbuf_tensor("sq_junk", [128, D], BF16))

        ln_done = []
        sq_done = []
        tr_done = []
        ev_done = []
        for ti in range(NT):
            with nc.Block(f"ln1_{ti}") as blk:
                @blk.sync
                def _(sy, ti=ti):
                    if ti >= 2:
                        c.wait(sy, "dv", ln_done[ti - 2])
                        c.wait(sy, "ac", sq_done[ti - 2])
                    c.inc(sy.dma_start(out=x_rot[ti % 2][:],
                                       in_=x_hbm[ti * 128:(ti + 1) * 128, :]),
                          "dx0" if ti % 2 == 0 else "dx1", 16)

                xs = "dx0" if ti % 2 == 0 else "dx1"
                v_x = c.cnt[xs]

                @blk.vector
                def _(v, ti=ti, v_x=v_x, xs=xs):
                    c.wait(v, xs, v_x)
                    c.inc(v.tensor_reduce(ln_scr[:, 0:1], x_rot[ti % 2][:], axis=AX.X, op=ALU.add), "dv")

                v_sum = c.cnt["dv"]

                @blk.scalar
                def _(s, ti=ti, v_x=v_x, xs=xs):
                    c.wait(s, xs, v_x)
                    sq_done.append(c.inc(s.activation(sq_junk[:], x_rot[ti % 2][:], AF.Square,
                                                      accum_out=ln_scr[:, 1:2]), "ac"))

                v_sq = c.cnt["ac"]

                @blk.vector
                def _(v, v_sq=v_sq, v_sum=v_sum):
                    c.wait(v, "ac", v_sq)
                    c.wait(v, "dv", v_sum)
                    v.tensor_tensor(ln_scr[:, 2:3], ln_scr[:, 0:1], ln_scr[:, 0:1], op=ALU.mult)
                    v.drain()
                    c.inc(v.scalar_tensor_tensor(ln_scr[:, 3:4], ln_scr[:, 2:3], -1.0 / D,
                                                 ln_scr[:, 1:2], op0=ALU.mult, op1=ALU.add), "dv")

                v_var = c.cnt["dv"]

                @blk.scalar
                def _(s, v_var=v_var):
                    c.wait(s, "dv", v_var)
                    c.inc(s.activation(ln_scr[:, 4:5], ln_scr[:, 3:4], AF.Sqrt,
                                       scale=1.0 / D, bias=cb[:, 2:3]), "ac")

                v_std = c.cnt["ac"]

                @blk.vector
                def _(v, ti=ti, v_std=v_std):
                    c.wait(v, "ac", v_std)
                    v.reciprocal(ln_scr[:, 5:6], ln_scr[:, 4:5])
                    v.tensor_scalar(ln_scr[:, 6:7], ln_scr[:, 0:1], -1.0 / D, None, op0=ALU.mult)
                    v.drain()
                    if ti >= 2:
                        c.wait(v, "pe", tr_done[ti - 2])
                    ln_done.append(c.inc(v.tensor_scalar(xln_rot[ti % 2][:], x_rot[ti % 2][:],
                                                         ln_scr[:, 6:7], ln_scr[:, 5:6],
                                                         op0=ALU.add, op1=ALU.mult), "dv"))

                v_xln = c.cnt["dv"]

                @blk.tensor
                def _(t, ti=ti, v_xln=v_xln):
                    c.wait(t, "dv", v_xln)
                    if ti >= 2:
                        c.wait(t, "dv", ev_done[ti - 2])
                    for k in range(NC):
                        pb = pub[ti % 2] if k < 4 else pub[2 + ti % 2]
                        mm = t.transpose(pb[:, (k % 4) * 128:(k % 4) * 128 + 128],
                                         xln_rot[ti % 2][:, k * 128:(k + 1) * 128], identb[:])
                    tr_done.append(c.inc(mm, "pe"))

                v_tr = c.cnt["pe"]

                @blk.vector
                def _(v, ti=ti, v_tr=v_tr):
                    c.wait(v, "pe", v_tr)
                    for k in range(NC):
                        pb = pub[ti % 2] if k < 4 else pub[2 + ti % 2]
                        cp = v.tensor_copy(x_ln_fm[:, k * L + ti * 128: k * L + ti * 128 + 128],
                                           pb[:, (k % 4) * 128:(k % 4) * 128 + 128])
                    ev_done.append(c.inc(cp, "dv"))

    v_xln_all = c.cnt["dv"]
    for u in range(4):
        c.pu_read(u, "dv", v_xln_all)
    c.tap("x_ln_fm", x_ln_fm[:], "dv")
    c.flush_taps()
    if c.checkpoint("ln1"):
        return nc

    class G:
        pass
    g = G()
    g.nc = nc
    _locals = dict(
        pu=pu, pub=pub, identb=identb, identf=identf, masks=masks, clampt=clampt,
        onescol=onescol, onesrow=onesrow, onesrow_bf=onesrow_bf, cq_sb=cq_sb,
        ck_sb=ck_sb, cv_sb=cv_sb, firs_sb=firs_sb, firl_sb=firl_sb, b1_sb=b1_sb,
        gb_sb=gb_sb, ltm_sb=ltm_sb, crl_sb=crl_sb, w1s_sb=w1s_sb, w2_sb=w2_sb,
        rw_sb=rw_sb, wb_sb=wb_sb, sv_sb=sv_sb, s_sb=s_sb, invtemp4=invtemp4,
        crb=crb, rmsw_bc=rmsw_bc, cb=cb, x_ln_fm=x_ln_fm, o_fm=o_fm, x_hbm=x_hbm,
        Wq_h=Wq_h, Wk_h=Wk_h, Wv_h=Wv_h, Wo_h=Wo_h, W1_h=W1_h, mwv_h=mwv_h,
        mw_h=mw_h, mw1_h=mw1_h, ms1_h=ms1_h, out_hbm=out_hbm)
    for nm, val in _locals.items():
        setattr(g, nm, val)

    g.wvt_d, g.wt_d, g.w1t_d = wvt_d, wt_d, w1t_d
    g.pp_done = {}
    g.pp_todo = [(e, f) for e in range(E) for f in range(NF)]

    g.x2_tm = c.stack.enter_context(nc.sbuf_tensor("x2_tm", [128, NT * D], BF16, side="right"))

    try:
        for head in range(H):
            if attn_head(c, g, head):
                return nc
    except _Cut:
        return nc

    es_x.close()

    wo_residual(c, g)
    c.tap("x2_tm", g.x2_tm[:], "dv")
    c.flush_taps()
    if c.checkpoint("wo"):
        return nc
    es_o.close()

    emit_prepass(c, g, len(g.pp_todo), "rest")
    g.pp_total = c.cnt["dpp"]

    moe(c, g)

    with nc.Block("fin") as blk:
        @blk.sync
        def _(sy):
            c.wait(sy, "dout")

    return nc


def emit_prepass(c, g, n, tag):
    todo = g.pp_todo[:n]
    del g.pp_todo[:n]
    if not todo:
        return
    with g.nc.Block(f"pp_{tag}") as blk:
        @blk.gpsimd
        def _(gg):
            for e, f in todo:
                row = (e * NF + f) * 128
                c.inc(gg.dma_start(
                    out=g.wvt_d[row:row + 128, :],
                    in_=g.mwv_h[e * D:(e + 1) * D, f * 128:(f + 1) * 128].rearrange(
                        "(cc p) w -> p cc w", p=128)), "dpp", 16)
                c.inc(gg.dma_start(
                    out=g.wt_d[row:row + 128, :],
                    in_=g.mw_h[e * D:(e + 1) * D, f * 128:(f + 1) * 128].rearrange(
                        "(cc p) w -> p cc w", p=128)), "dpp", 16)
                c.inc(gg.dma_start(
                    out=g.w1t_d[row:row + 128, :],
                    in_=g.mw1_h[e * FF + f * 128: e * FF + (f + 1) * 128, :]), "dpp", 16)
                g.pp_done[(e, f)] = c.cnt["dpp"]


def fence(c, nc, name):
    with nc.Block(name) as blk:
        @blk.tensor
        def _(t):
            for s in ("dv", "ac", "gp", "din", "dwt", "dout"):
                c.wait(t, s)

        @blk.vector
        def _(v):
            for s in ("pe", "ac", "gp", "din", "dwt", "dout"):
                c.wait(v, s)

        @blk.scalar
        def _(s_):
            for s in ("pe", "dv", "gp", "din", "dwt", "dout"):
                c.wait(s_, s)

        @blk.gpsimd
        def _(gg):
            for s in ("pe", "dv", "ac", "dout"):
                c.wait(gg, s)

        @blk.sync
        def _(sy):
            for s in ("pe", "dv", "ac", "gp", "dout", "dpp"):
                c.wait(sy, s)


def tr_fm_to_tm(c, g, name, src_fn, dst_fn, n=16, units=(6, 7)):
    ngrp = (n + 7) // 8
    for grp in range(ngrp):
        lo, hi = grp * 8, min(grp * 8 + 8, n)
        u = units[grp % len(units)]
        with g.nc.Block(f"{name}_tr{grp}") as blk:
            @blk.tensor
            def _(t, grp=grp, lo=lo, hi=hi, u=u):
                if grp == 0:
                    c.wait(t, "dv")
                    c.wait(t, "ac")
                c.pu_guard(t, u)
                pb = g.pub[u]
                for tt in range(lo, hi):
                    mm = t.transpose(pb[:, (tt - lo) * 128:(tt - lo + 1) * 128],
                                     src_fn(tt), g.identb[:])
                c.inc(mm, "pe")

            v_mm = c.cnt["pe"]

            @blk.vector
            def _(v, grp=grp, lo=lo, hi=hi, u=u, v_mm=v_mm):
                c.wait(v, "pe", v_mm)
                pb = g.pub[u]
                for tt in range(lo, hi):
                    cp = v.tensor_copy(dst_fn(tt), pb[:, (tt - lo) * 128:(tt - lo + 1) * 128])
                c.inc(cp, "dv")
                c.pu_read(u, "dv")
    return c.cnt["dv"]


def attn_head(c, g, head):
    nc = g.nc
    fence(c, nc, f"hd{head}_fence")

    with contextlib.ExitStack() as hs:
        def sb(name, shape, dtype):
            return hs.enter_context(nc.sbuf_tensor(f"h{head}_{name}", shape, dtype))

        q_n_fm = sb("q_n_fm", [128, 2 * L], BF16)
        k_tm = sb("k_tm", [128, NT * DK], BF16)
        v_tm = sb("v_tm", [128, NT * DK], BF16)
        fs_tm = sb("fs_tm", [128, NT * DK], BF16)
        fl_tm = sb("fl_tm", [128, NT * DK], BF16)
        delta_tm = sb("delta_tm", [128, NT * DK], BF16)
        Wt_sb = sb("Wt_sb", [128, 2 * NCH * 32], BF16)
        ALt_sb = sb("ALt_sb", [128, 512], BF16)
        iTb = sb("iTb", [128, 512], BF16)

        with contextlib.ExitStack() as pa:
            def sba(name, shape, dtype):
                return pa.enter_context(nc.sbuf_tensor(f"h{head}a_{name}", shape, dtype))

            wslab = sba("wslab", [128, NC * DK], BF16)
            P_buf = sba("P_buf", [128, 3 + L], BF16)
            cacc = sba("cacc", [128, L], BF16)
            beta_bf = sba("beta_bf", [1, L], BF16)
            rsq = sba("rsq", [1, 512], F32)
            k_n_fm = sba("k_n_fm", [128, 2 * L], BF16)
            iT = sba("iT", [128, 512], BF16)
            iT2 = sba("iT2", [128, 512], BF16)
            Pt = sba("Pt", [128, 512], BF16)
            Qt = sba("Qt", [128, 512], BF16)
            Pt2 = sba("Pt2", [128, 512], BF16)
            Qt2 = sba("Qt2", [128, 512], BF16)
            q2s = P_buf[:, 0:1024]
            b_cb = sba("b_cb", [128, 512], BF16)
            b_rb = sba("b_rb", [128, 512], BF16)
            qfac = sba("qfac", [128, 512], BF16)

            for ts4 in range(4):
                u = ts4 % 2
                with nc.Block(f"hd{head}_bt{ts4}") as blk:
                    @blk.tensor
                    def _(t, ts4=ts4, u=u):
                        c.pu_guard(t, u)
                        for k in range(NC):
                            mm = t.matmul(g.pu[u][0:1, 0:512],
                                          g.wb_sb[:, k * 4 + head: k * 4 + head + 1],
                                          g.x_ln_fm[:, k * L + ts4 * 512: k * L + ts4 * 512 + 512],
                                          start=(k == 0), stop=(k == NC - 1))
                        c.inc(mm, "pe")

                    v_mm = c.cnt["pe"]

                    @blk.scalar
                    def _(s, ts4=ts4, u=u, v_mm=v_mm):
                        c.wait(s, "pe", v_mm)
                        cp = s.activation(beta_bf[0:1, ts4 * 512: ts4 * 512 + 512],
                                          g.pu[u][0:1, 0:512], AF.Sigmoid)
                        c.inc(cp, "ac")
                        c.pu_read(u, "ac")

            def proj_part(part, csb, cc, dst, dst_off):
                gc = 2 * head + cc
                for ts4 in range(4):
                    u = ts4 % 2
                    with nc.Block(f"hd{head}_{part}{cc}_mm{ts4}") as blk:
                        @blk.tensor
                        def _(t, ts4=ts4, u=u):
                            c.pu_guard(t, u)
                            for k in range(NC):
                                mm = t.matmul(g.pu[u][:],
                                              wslab[:, k * DK + cc * 128: k * DK + cc * 128 + 128],
                                              g.x_ln_fm[:, k * L + ts4 * 512: k * L + ts4 * 512 + 512],
                                              start=(k == 0), stop=(k == NC - 1))
                            c.inc(mm, "pe")

                        v_mm = c.cnt["pe"]

                        @blk.scalar
                        def _(s, ts4=ts4, u=u, v_mm=v_mm):
                            c.wait(s, "pe", v_mm)
                            if ts4 == 0:
                                c.wait(s, "dv")
                            cp = s.activation(P_buf[:, 3 + ts4 * 512: 3 + ts4 * 512 + 512],
                                              g.pu[u][:], AF.Copy)
                            c.inc(cp, "ac")
                            c.pu_read(u, "ac")

                v_evac = c.cnt["ac"]
                with nc.Block(f"hd{head}_{part}{cc}_conv") as blk:
                    @blk.vector
                    def _(v, gc=gc, v_evac=v_evac):
                        c.wait(v, "ac", v_evac)
                        v.memset(P_buf[:, 0:3], 0.0)
                        v.drain()
                        v.tensor_scalar(cacc[:], P_buf[:, 0:L], csb[:, gc * 4: gc * 4 + 1],
                                        None, op0=ALU.mult)
                        for j in range(1, 4):
                            v.drain()
                            mm = v.scalar_tensor_tensor(cacc[:], P_buf[:, j: j + L],
                                                        csb[:, gc * 4 + j: gc * 4 + j + 1],
                                                        cacc[:], op0=ALU.mult, op1=ALU.add)
                        c.inc(mm, "dv")

                    v_conv = c.cnt["dv"]

                    @blk.scalar
                    def _(s, v_conv=v_conv, dst=dst, dst_off=dst_off):
                        c.wait(s, "dv", v_conv)
                        c.wait(s, "pe")
                        c.inc(s.activation(dst[:, dst_off: dst_off + L], cacc[:], AF.Silu), "ac")

            def l2norm(part, dst_fm):
                for ts4 in range(4):
                    ua, ub = 2 + ts4 % 2, 4 + ts4 % 2
                    with nc.Block(f"hd{head}_l2{part}_{ts4}") as blk:
                        @blk.scalar
                        def _(s, ts4=ts4):
                            c.wait(s, "ac")
                            c.wait(s, "dv")
                            for cc in range(2):
                                cp = s.activation(q2s[:, cc * 512: cc * 512 + 512],
                                                  dst_fm[:, cc * L + ts4 * 512: cc * L + ts4 * 512 + 512],
                                                  AF.Square)
                            c.inc(cp, "ac")

                        v_sq = c.cnt["ac"]

                        @blk.tensor
                        def _(t, ts4=ts4, ua=ua, v_sq=v_sq):
                            c.wait(t, "ac", v_sq)
                            c.pu_guard(t, ua)
                            for cc in range(2):
                                mm = t.matmul(g.pu[ua][0:1, 0:512], g.onescol[:],
                                              q2s[:, cc * 512: cc * 512 + 512],
                                              start=(cc == 0), stop=(cc == 1))
                            c.inc(mm, "pe")

                        v_ss = c.cnt["pe"]

                        @blk.scalar
                        def _(s, ts4=ts4, ua=ua, v_ss=v_ss):
                            c.wait(s, "pe", v_ss)
                            c.inc(s.activation(rsq[0:1, :], g.pu[ua][0:1, 0:512], AF.Ln,
                                               bias=g.cb[0:1, 1:2]), "ac")
                            c.wait(s, "ac")
                            cp = s.activation(rsq[0:1, :], rsq[0:1, :], AF.Exp, scale=-0.5)
                            c.inc(cp, "ac")
                            c.pu_read(ua, "ac")

                        v_rs = c.cnt["ac"]

                        @blk.tensor
                        def _(t, ts4=ts4, ub=ub, v_rs=v_rs):
                            c.wait(t, "ac", v_rs)
                            c.pu_guard(t, ub)
                            c.inc(t.matmul(g.pu[ub][:], g.onesrow[:], rsq[0:1, :],
                                           start=True, stop=True), "pe")

                        v_bc = c.cnt["pe"]

                        @blk.vector
                        def _(v, ts4=ts4, ub=ub, v_bc=v_bc):
                            c.wait(v, "pe", v_bc)
                            for cc in range(2):
                                sl = dst_fm[:, cc * L + ts4 * 512: cc * L + ts4 * 512 + 512]
                                cp = v.tensor_tensor(sl, sl, g.pu[ub][:], op=ALU.mult)
                            c.inc(cp, "dv")
                            c.pu_read(ub, "dv")

            def load_wslab(W_h, gate_pe):
                with nc.Block(f"hd{head}_wld{W_h.name}") as blk:
                    @blk.gpsimd
                    def _(gg):
                        c.wait(gg, "pe", gate_pe)
                        c.inc(gg.dma_start(
                            out=wslab[:],
                            in_=W_h[:, head * DK:(head + 1) * DK].rearrange(
                                "(c p) o -> p c o", p=128)), "dwt", 16)
                return c.cnt["dwt"]

            if c.checkpoint(f"h{head}beta"):
                return True
            emit_prepass(c, g, 48, f"h{head}q")
            v_w = load_wslab(g.Wq_h, 0)
            with nc.Block(f"hd{head}_qw") as blk:
                @blk.tensor
                def _(t, v_w=v_w):
                    c.wait(t, "dwt", v_w)
            proj_part("q", g.cq_sb, 0, q_n_fm, 0)
            proj_part("q", g.cq_sb, 1, q_n_fm, L)
            if c.checkpoint(f"h{head}qproj"):
                return True
            l2norm("q", q_n_fm)
            if c.checkpoint(f"h{head}ql2"):
                return True
            emit_prepass(c, g, 48, f"h{head}k")
            v_w = load_wslab(g.Wk_h, c.cnt["pe"])
            with nc.Block(f"hd{head}_kw") as blk:
                @blk.tensor
                def _(t, v_w=v_w):
                    c.wait(t, "dwt", v_w)
            proj_part("k", g.ck_sb, 0, k_n_fm, 0)
            proj_part("k", g.ck_sb, 1, k_n_fm, L)
            l2norm("k", k_n_fm)
            if c.checkpoint(f"h{head}kl2"):
                return True
            for cc in range(2):
                tr_fm_to_tm(c, g, f"hd{head}_ktm{cc}",
                            lambda tt, cc=cc: k_n_fm[:, cc * L + tt * 128: cc * L + tt * 128 + 128],
                            lambda tt, cc=cc: k_tm[:, tt * DK + cc * 128: tt * DK + cc * 128 + 128])

            c.tap(f"h{head}_q_n_fm", q_n_fm[:], "dv")
            c.tap(f"h{head}_k_n_fm", k_n_fm[:], "dv")
            c.flush_taps()
            if c.checkpoint(f"h{head}ktm"):
                return True

            try:
                _dp = delta_prep
            except NameError:
                pass
            delta_prep(c, g, head, dict(
                q_n_fm=q_n_fm, k_n_fm=k_n_fm, k_tm=k_tm, beta_bf=beta_bf,
                Wt_sb=Wt_sb, ALt_sb=ALt_sb, iTb=iTb,
                iT=iT, iT2=iT2, Pt=Pt, Qt=Qt, Pt2=Pt2, Qt2=Qt2,
                b_cb=b_cb, b_rb=b_rb, qfac=qfac))

            c.tap(f"h{head}_iTb", iTb[:], "dv")
            c.tap(f"h{head}_Wt_sb", Wt_sb[:], "dv")
            c.tap(f"h{head}_ALt_sb", ALt_sb[:], "dv")
            c.flush_taps()
            if c.checkpoint(f"h{head}a"):
                return True
            fence(c, nc, f"hd{head}_fa")

        with contextlib.ExitStack() as pb:
            def sbb(name, shape, dtype):
                return pb.enter_context(nc.sbuf_tensor(f"h{head}b_{name}", shape, dtype))

            wslab = sbb("wslab", [128, NC * DK], BF16)
            P_buf = sbb("P_buf", [128, 3 + L], BF16)
            cacc = sbb("cacc", [128, L], BF16)
            v_fm = sbb("v_fm", [128, 63 + L], BF16)
            flA = sbb("flA", [128, L], BF16)

            emit_prepass(c, g, 48, f"h{head}v")
            with nc.Block(f"hd{head}_vw") as blk:
                @blk.gpsimd
                def _(gg):
                    c.wait(gg, "pe")
                    c.inc(gg.dma_start(
                        out=wslab[:],
                        in_=g.Wv_h[:, head * DK:(head + 1) * DK].rearrange(
                            "(c p) o -> p c o", p=128)), "dwt", 16)

                v_w = c.cnt["dwt"]

                @blk.tensor
                def _(t, v_w=v_w):
                    c.wait(t, "dwt", v_w)

            for cc in range(2):
                gc = 2 * head + cc
                with nc.Block(f"hd{head}_v{cc}_pad") as blk:
                    @blk.vector
                    def _(v):
                        c.inc(v.memset(v_fm[:, 0:63], 0.0), "dv")

                for ts4 in range(4):
                    u = ts4 % 2
                    with nc.Block(f"hd{head}_v{cc}_mm{ts4}") as blk:
                        @blk.tensor
                        def _(t, ts4=ts4, u=u, cc=cc):
                            c.pu_guard(t, u)
                            for k in range(NC):
                                mm = t.matmul(g.pu[u][:],
                                              wslab[:, k * DK + cc * 128: k * DK + cc * 128 + 128],
                                              g.x_ln_fm[:, k * L + ts4 * 512: k * L + ts4 * 512 + 512],
                                              start=(k == 0), stop=(k == NC - 1))
                            c.inc(mm, "pe")

                        v_mm = c.cnt["pe"]

                        @blk.scalar
                        def _(s, ts4=ts4, u=u, v_mm=v_mm):
                            c.wait(s, "pe", v_mm)
                            if ts4 == 0:
                                c.wait(s, "dv")
                            cp = s.activation(P_buf[:, 3 + ts4 * 512: 3 + ts4 * 512 + 512],
                                              g.pu[u][:], AF.Copy)
                            c.inc(cp, "ac")
                            c.pu_read(u, "ac")

                v_evac = c.cnt["ac"]
                with nc.Block(f"hd{head}_v{cc}_conv") as blk:
                    @blk.vector
                    def _(v, gc=gc, v_evac=v_evac):
                        c.wait(v, "ac", v_evac)
                        v.memset(P_buf[:, 0:3], 0.0)
                        v.drain()
                        v.tensor_scalar(cacc[:], P_buf[:, 0:L], g.cv_sb[:, gc * 4: gc * 4 + 1],
                                        None, op0=ALU.mult)
                        for j in range(1, 4):
                            v.drain()
                            mm = v.scalar_tensor_tensor(cacc[:], P_buf[:, j: j + L],
                                                        g.cv_sb[:, gc * 4 + j: gc * 4 + j + 1],
                                                        cacc[:], op0=ALU.mult, op1=ALU.add)
                        c.inc(mm, "dv")

                    v_conv = c.cnt["dv"]

                    @blk.scalar
                    def _(s, v_conv=v_conv):
                        c.wait(s, "dv", v_conv)
                        c.wait(s, "pe")
                        c.inc(s.activation(v_fm[:, 63:63 + L], cacc[:], AF.Silu), "ac")

                tr_fm_to_tm(c, g, f"hd{head}_vtm{cc}",
                            lambda tt: v_fm[:, 63 + tt * 128: 63 + tt * 128 + 128],
                            lambda tt, cc=cc: v_tm[:, tt * DK + cc * 128: tt * DK + cc * 128 + 128])

                with nc.Block(f"hd{head}_fir{cc}") as blk:
                    @blk.vector
                    def _(v, gc=gc):
                        c.wait(v, "ac")
                        c.wait(v, "pe")
                        v.tensor_scalar(P_buf[:, 0:L], v_fm[:, 59:59 + L],
                                        g.firs_sb[:, gc * 5: gc * 5 + 1], None, op0=ALU.mult)
                        for j in range(1, 5):
                            v.drain()
                            mm = v.scalar_tensor_tensor(P_buf[:, 0:L], v_fm[:, 59 + j: 59 + j + L],
                                                        g.firs_sb[:, gc * 5 + j: gc * 5 + j + 1],
                                                        P_buf[:, 0:L], op0=ALU.mult, op1=ALU.add)
                        c.inc(mm, "dv")
                        for blk4 in range(4):
                            dst = flA if blk4 == 0 else cacc
                            j0 = blk4 * 16
                            v.drain()
                            v.tensor_scalar(dst[:], v_fm[:, j0: j0 + L],
                                            g.firl_sb[:, gc * 64 + j0: gc * 64 + j0 + 1],
                                            None, op0=ALU.mult)
                            for j in range(j0 + 1, j0 + 16):
                                v.drain()
                                v.scalar_tensor_tensor(dst[:], v_fm[:, j: j + L],
                                                       g.firl_sb[:, gc * 64 + j: gc * 64 + j + 1],
                                                       dst[:], op0=ALU.mult, op1=ALU.add)
                            if blk4 > 0:
                                v.drain()
                                mm = v.tensor_tensor(flA[:], flA[:], cacc[:], op=ALU.add)
                        c.inc(mm, "dv")

                tr_fm_to_tm(c, g, f"hd{head}_fstm{cc}",
                            lambda tt: P_buf[:, tt * 128: tt * 128 + 128],
                            lambda tt, cc=cc: fs_tm[:, tt * DK + cc * 128: tt * DK + cc * 128 + 128])
                tr_fm_to_tm(c, g, f"hd{head}_fltm{cc}",
                            lambda tt: flA[:, tt * 128: tt * 128 + 128],
                            lambda tt, cc=cc: fl_tm[:, tt * DK + cc * 128: tt * DK + cc * 128 + 128])

            c.tap(f"h{head}_v_tm", v_tm[:], "dv")
            c.tap(f"h{head}_fs_tm", fs_tm[:], "dv")
            c.tap(f"h{head}_fl_tm", fl_tm[:], "dv")
            c.flush_taps()
            if c.checkpoint(f"h{head}b"):
                return True
            fence(c, nc, f"hd{head}_fb")

        with contextlib.ExitStack() as sc_s:
            S_sb = sc_s.enter_context(nc.sbuf_tensor(f"h{head}_S", [128, 512], F32))
            S_bf = sc_s.enter_context(nc.sbuf_tensor(f"h{head}_Sbf", [128, 512], BF16))
            u_sc = sc_s.enter_context(nc.sbuf_tensor(f"h{head}_usc", [128, DK], BF16))
            delta_scan(c, g, head, dict(
                q_n_fm=q_n_fm, k_tm=k_tm, v_tm=v_tm, Wt_sb=Wt_sb, ALt_sb=ALt_sb,
                iTb=iTb, S_sb=S_sb, S_bf=S_bf, u_sc=u_sc, delta_tm=delta_tm))
            fence(c, nc, f"hd{head}_fs2")

        c.tap(f"h{head}_delta_tm", delta_tm[:], "dv")
        c.flush_taps()

        if c.checkpoint(f"h{head}scan"):
            return True

        attn_post(c, g, head, dict(
            v_tm=v_tm, fs_tm=fs_tm, fl_tm=fl_tm, delta_tm=delta_tm))

        if c.checkpoint(f"h{head}post"):
            return True

    fence(c, nc, f"hd{head}_fend")
    return False


def delta_prep(c, g, head, T):
    nc = g.nc
    q_n_fm, k_n_fm, k_tm = T["q_n_fm"], T["k_n_fm"], T["k_tm"]
    beta_bf = T["beta_bf"]
    Wt_sb, ALt_sb, iTb = T["Wt_sb"], T["ALt_sb"], T["iTb"]
    iT, iT2, Pt, Qt, Pt2, Qt2 = T["iT"], T["iT2"], T["Pt"], T["Qt"], T["Pt2"], T["Qt2"]
    b_cb, b_rb, qfac = T["b_cb"], T["b_rb"], T["qfac"]

    def sl(buf, i, w=32):
        j = i % 4
        return buf[32 * j:32 * j + 32, (i // 4) * w:(i // 4) * w + w]

    def mask_bc(col):
        return g.masks[:, col:col + 32].unsqueeze(1).broadcast_to([128, 16, 32])

    def r3(ap):
        return ap.rearrange("p (gr n) -> p gr n", n=32)

    with nc.Block(f"hd{head}_bfac") as blk:
        @blk.tensor
        def _(t):
            c.wait(t, "ac")
            c.pu_guard(t, 2)
            c.pu_guard(t, 3)
            for j in range(4):
                rhs = beta_bf[0:1, :].rearrange("o (gr n) -> o gr n", n=32)[:, j::4, :]
                mm = t.matmul(g.pu[2][32 * j:32 * j + 32, :], g.onesrow_bf[0:1, 0:32],
                              rhs, start=True, stop=True, tile_position=(0, 32 * j))
            c.inc(mm, "pe")
            for i in range(NCH):
                j = i % 4
                mm = t.matmul(g.pu[3][32 * j:32 * j + 32, (i // 4) * 32:(i // 4) * 32 + 32],
                              beta_bf[0:1, i * 32: i * 32 + 32], g.onesrow_bf[0:1, 0:32],
                              start=True, stop=True, tile_position=(0, 32 * j))
            c.inc(mm, "pe")

        v_b = c.cnt["pe"]

        @blk.vector
        def _(v, v_b=v_b):
            c.wait(v, "pe", v_b)
            v.tensor_copy(b_rb[:], g.pu[3][:])
            v.tensor_tensor(r3(b_cb[:]), r3(g.pu[2][:]), mask_bc(0), op=ALU.mult)
            c.inc(v.tensor_tensor(r3(qfac[:]), r3(g.pu[3][:]), mask_bc(32), op=ALU.mult), "dv")
            c.pu_read(2, "dv")
            c.pu_read(3, "dv")

    if c.checkpoint(f"h{head}bfac"):
        raise _Cut()
    with nc.Block(f"hd{head}_kkt") as blk:
        @blk.tensor
        def _(t):
            c.wait(t, "dv")
            c.pu_guard(t, 0)
            for i in range(NCH):
                j = i % 4
                for cc in range(2):
                    mm = t.matmul(sl(g.pu[0], i),
                                  k_n_fm[:, cc * L + i * 32: cc * L + i * 32 + 32],
                                  k_n_fm[:, cc * L + i * 32: cc * L + i * 32 + 32],
                                  start=(cc == 0), stop=(cc == 1), tile_position=(0, 32 * j))
            c.inc(mm, "pe")

        v_kkt = c.cnt["pe"]

        @blk.vector
        def _(v, v_kkt=v_kkt):
            c.wait(v, "pe", v_kkt)
            c.wait(v, "dv")
            v.tensor_tensor(Pt[:], g.pu[0][:], b_cb[:], op=ALU.mult)
            v.tensor_tensor(Qt[:], g.pu[0][:], qfac[:], op=ALU.mult)
            v.drain()
            c.inc(v.tensor_tensor(r3(iT[:]), r3(Pt[:]), mask_bc(96), op=ALU.add), "dv")
            c.pu_read(0, "dv")

    if c.checkpoint(f"h{head}kkt"):
        raise _Cut()
    P_cur, Q_cur, i_cur = Pt, Qt, iT
    P_nxt, Q_nxt, i_nxt = Pt2, Qt2, iT2
    for it in range(1, 5):
        with nc.Block(f"hd{head}_inv{it}") as blk:
            @blk.tensor
            def _(t, it=it, P_cur=P_cur, Q_cur=Q_cur):
                c.wait(t, "dv")
                c.pu_guard(t, 0)
                c.pu_guard(t, 1)
                if it < 4:
                    for i in range(NCH):
                        j = i % 4
                        mm = t.matmul(sl(g.pu[0], i), sl(Q_cur, i), sl(P_cur, i),
                                      start=True, stop=True, tile_position=(32 * j, 32 * j))
                    c.inc(mm, "pe")
                for i in range(NCH):
                    j = i % 4
                    mm = t.matmul(sl(g.pu[1], i), sl(P_cur, i), sl(Q_cur, i),
                                  start=True, stop=True, tile_position=(32 * j, 32 * j))
                c.inc(mm, "pe")

            v_sq = c.cnt["pe"]

            @blk.vector
            def _(v, it=it, P_nxt=P_nxt, Q_nxt=Q_nxt, v_sq=v_sq):
                c.wait(v, "pe", v_sq)
                if it < 4:
                    v.tensor_copy(P_nxt[:], g.pu[0][:])
                c.inc(v.tensor_copy(Q_nxt[:], g.pu[1][:]), "dv")
                c.pu_read(0, "dv")
                c.pu_read(1, "dv")

            v_pq = c.cnt["dv"]

            @blk.tensor
            def _(t, Q_nxt=Q_nxt, i_cur=i_cur, v_pq=v_pq):
                c.wait(t, "dv", v_pq)
                c.pu_guard(t, 2)
                for i in range(NCH):
                    j = i % 4
                    mm = t.matmul(sl(g.pu[2], i), sl(Q_nxt, i), sl(i_cur, i),
                                  start=True, stop=True, tile_position=(32 * j, 32 * j))
                c.inc(mm, "pe")

            v_up = c.cnt["pe"]

            @blk.vector
            def _(v, i_cur=i_cur, i_nxt=i_nxt, v_up=v_up):
                c.wait(v, "pe", v_up)
                c.wait(v, "dv")
                c.inc(v.tensor_tensor(i_nxt[:], g.pu[2][:], i_cur[:], op=ALU.add), "dv")
                c.pu_read(2, "dv")

        P_cur, P_nxt = P_nxt, P_cur
        Q_cur, Q_nxt = Q_nxt, Q_cur
        i_cur, i_nxt = i_nxt, i_cur

    if c.checkpoint(f"h{head}inv"):
        raise _Cut()
    with nc.Block(f"hd{head}_itb") as blk:
        @blk.vector
        def _(v, i_cur=i_cur):
            c.wait(v, "dv")
            c.inc(v.tensor_tensor(iTb[:], i_cur[:], b_rb[:], op=ALU.mult), "dv")

    v_itb = c.cnt["dv"]

    if c.checkpoint(f"h{head}itb"):
        raise _Cut()
    with nc.Block(f"hd{head}_alt") as blk:
        @blk.tensor
        def _(t):
            c.pu_guard(t, 0)
            for i in range(NCH):
                j = i % 4
                for cc in range(2):
                    mm = t.matmul(sl(g.pu[0], i),
                                  k_n_fm[:, cc * L + i * 32: cc * L + i * 32 + 32],
                                  q_n_fm[:, cc * L + i * 32: cc * L + i * 32 + 32],
                                  start=(cc == 0), stop=(cc == 1), tile_position=(0, 32 * j))
            c.inc(mm, "pe")

        v_al = c.cnt["pe"]

        @blk.vector
        def _(v, v_al=v_al):
            c.wait(v, "pe", v_al)
            c.inc(v.tensor_tensor(r3(ALt_sb[:]), r3(g.pu[0][:]), mask_bc(64), op=ALU.mult), "dv")
            c.pu_read(0, "dv")

    if c.checkpoint(f"h{head}alt"):
        raise _Cut()
    for hf in range(2):
        for sbi in range(4):
            u = 1 + (hf * 4 + sbi) % 2
            with nc.Block(f"hd{head}_wt{hf}_{sbi}") as blk:
                @blk.tensor
                def _(t, hf=hf, sbi=sbi, u=u, v_itb=v_itb):
                    if hf == 0 and sbi == 0:
                        c.wait(t, "dv", v_itb)
                    c.pu_guard(t, u)
                    prev_j = None
                    for ii in range(16):
                        i = sbi * 16 + ii
                        j = i % 4
                        if prev_j is not None and j != prev_j:
                            t.drain()
                        prev_j = j
                        mm = t.matmul(g.pu[u][:, ii * 32: ii * 32 + 32],
                                      k_tm[32 * j:32 * j + 32,
                                           (i // 4) * DK + hf * 128: (i // 4) * DK + hf * 128 + 128],
                                      sl(iTb, i),
                                      start=True, stop=True, tile_position=(32 * j, 0))
                    c.inc(mm, "pe")

                v_w = c.cnt["pe"]

                @blk.vector
                def _(v, hf=hf, sbi=sbi, u=u, v_w=v_w):
                    c.wait(v, "pe", v_w)
                    c.inc(v.tensor_scalar(
                        Wt_sb[:, hf * (NCH * 32) + sbi * 512: hf * (NCH * 32) + sbi * 512 + 512],
                        g.pu[u][:], -1.0, None, op0=ALU.mult), "dv")
                    c.pu_read(u, "dv")


def delta_scan(c, g, head, T):
    nc = g.nc
    q_n_fm, k_tm, v_tm = T["q_n_fm"], T["k_tm"], T["v_tm"]
    Wt_sb, ALt_sb, iTb = T["Wt_sb"], T["ALt_sb"], T["iTb"]
    S_sb, S_bf, u_sc, delta_tm = T["S_sb"], T["S_bf"], T["u_sc"], T["delta_tm"]

    with nc.Block(f"hd{head}_scan_init") as blk:
        @blk.vector
        def _(v):
            v.memset(S_sb[:], 0.0)
            c.inc(v.memset(S_bf[:], 0.0), "dv")

    v_sadd = c.cnt["dv"]
    for i in range(NCH):
        j = i % 4
        tt = i // 4
        uo = 6 + tt % 2
        with nc.Block(f"hd{head}_scan{i}") as blk:
            @blk.tensor
            def _(t, i=i, j=j, tt=tt, v_sadd=v_sadd):
                c.wait(t, "dv", v_sadd)
                if i == 0:
                    c.pu_guard(t, 5)
                    c.pu_guard(t, 3)
                t.matmul(g.pu[5][32 * j:32 * j + 32, 0:DK],
                         iTb[32 * j:32 * j + 32, (i // 4) * 32: (i // 4) * 32 + 32],
                         v_tm[32 * j:32 * j + 32, tt * DK: tt * DK + DK],
                         start=True, stop=False, tile_position=(32 * j, 32 * j))
                for hf in range(2):
                    mm = t.matmul(g.pu[5][32 * j:32 * j + 32, 0:DK],
                                  Wt_sb[:, hf * (NCH * 32) + i * 32: hf * (NCH * 32) + i * 32 + 32],
                                  S_bf[:, hf * DK: hf * DK + DK],
                                  start=False, stop=(hf == 1), tile_position=(0, 32 * j))
                c.inc(mm, "pe")

            v_ws = c.cnt["pe"]

            @blk.vector
            def _(v, i=i, j=j, v_ws=v_ws):
                c.wait(v, "pe", v_ws)
                c.inc(v.tensor_copy(u_sc[32 * j:32 * j + 32, :],
                                    g.pu[5][32 * j:32 * j + 32, 0:DK]), "dv")

            v_ut = c.cnt["dv"]

            @blk.tensor
            def _(t, i=i, j=j, tt=tt, uo=uo, v_ut=v_ut):
                c.wait(t, "dv", v_ut)
                if j == 0:
                    c.pu_guard(t, uo)
                for hf in range(2):
                    t.matmul(g.pu[uo][32 * j:32 * j + 32, 0:DK],
                             q_n_fm[:, hf * L + i * 32: hf * L + i * 32 + 32],
                             S_bf[:, hf * DK: hf * DK + DK],
                             start=(hf == 0), stop=False, tile_position=(0, 32 * j))
                mm = t.matmul(g.pu[uo][32 * j:32 * j + 32, 0:DK],
                              ALt_sb[32 * j:32 * j + 32, tt * 32: tt * 32 + 32],
                              u_sc[32 * j:32 * j + 32, :],
                              start=False, stop=True, tile_position=(32 * j, 32 * j))
                c.inc(mm, "pe")
                t.drain()
                for hf in range(2):
                    mm = t.matmul(g.pu[3][:, hf * DK: hf * DK + DK],
                                  k_tm[32 * j:32 * j + 32, tt * DK + hf * 128: tt * DK + hf * 128 + 128],
                                  u_sc[32 * j:32 * j + 32, :],
                                  start=True, stop=True, tile_position=(32 * j, 0))
                c.inc(mm, "pe")

            v_osd = c.cnt["pe"]

            @blk.vector
            def _(v, i=i, j=j, tt=tt, uo=uo, v_osd=v_osd):
                c.wait(v, "pe", v_osd)
                v.tensor_tensor(S_sb[:], g.pu[3][:], S_sb[:], op=ALU.add)
                v.drain()
                cp = v.tensor_copy(S_bf[:], S_sb[:])
                if j == 3:
                    cp = v.tensor_copy(delta_tm[:, tt * DK: tt * DK + DK], g.pu[uo][:, 0:DK])
                    c.pu_read(uo, "dv", c.cnt["dv"] + 1)
                c.inc(cp, "dv")

            v_sadd = c.cnt["dv"]
    c.pu_read(5, "dv", v_sadd)
    c.pu_read(3, "dv", v_sadd)


def attn_post(c, g, head, T):
    nc = g.nc
    v_tm, fs_tm, fl_tm, delta_tm = T["v_tm"], T["fs_tm"], T["fl_tm"], T["delta_tm"]
    tensors = (fs_tm, fl_tm, delta_tm, v_tm)

    with contextlib.ExitStack() as ps:
        def sb(name, shape, dtype):
            return ps.enter_context(nc.sbuf_tensor(f"h{head}p_{name}", shape, dtype))

        stats_fm = sb("stats_fm", [16, L], BF16)
        raw = sb("raw", [128, NT * 12], F32)
        stats_tm = sb("stats_tm", [128, NT * 16], F32)
        st_junk = sb("st_junk", [128, DK], BF16)
        st_scr = sb("st_scr", [128, 8], F32)
        h1h = sb("h1h", [128, NC * 512], BF16)
        w1m = sb("w1m", [128, NC * D], BF16)
        logits_sb = sb("logits_sb", [4, L], F32)
        p_scr = sb("p_scr", [128, NT * 4], F32)
        sm_scr = sb("sm_scr", [128, 8], F32)
        o_acc = sb("o_acc", [128, DK], F32)
        o_n = sb("o_n", [128, DK], BF16)

        v_scan = c.cnt["dv"]
        with nc.Block(f"hd{head}_stats_raw") as blk:
            @blk.vector
            def _(v):
                c.wait(v, "dv", v_scan)
                for ti in range(NT):
                    for x in range(4):
                        src = tensors[x][:, ti * DK: ti * DK + DK]
                        v.tensor_reduce(raw[:, ti * 12 + x: ti * 12 + x + 1], src,
                                        axis=AX.X, op=ALU.add)
                        mm = v.tensor_reduce(raw[:, ti * 12 + 8 + x: ti * 12 + 8 + x + 1], src,
                                             axis=AX.X, op=ALU.add, apply_absolute_value=True)
                c.inc(mm, "dv")

            @blk.scalar
            def _(s):
                c.wait(s, "dv", v_scan)
                for ti in range(NT):
                    for x in range(4):
                        src = tensors[x][:, ti * DK: ti * DK + DK]
                        if ti or x:
                            s.drain()
                        mm = s.activation(st_junk[:], src, AF.Square,
                                          accum_out=raw[:, ti * 12 + 4 + x: ti * 12 + 4 + x + 1])
                c.inc(mm, "ac")

        v_raw_v = c.cnt["dv"]
        v_raw_s = c.cnt["ac"]

        with nc.Block(f"hd{head}_stats_drv") as blk:
            @blk.vector
            def _(v):
                c.wait(v, "dv", v_raw_v)
                c.wait(v, "ac", v_raw_s)
                for ti in range(NT):
                    sums = raw[:, ti * 12: ti * 12 + 4]
                    sqs = raw[:, ti * 12 + 4: ti * 12 + 8]
                    abss = raw[:, ti * 12 + 8: ti * 12 + 12]
                    st = stats_tm[:, ti * 16: ti * 16 + 16].rearrange("p (x f) -> p f x", f=4)
                    v.tensor_scalar(st[:, 0:1, :], sums, 1.0 / DK, None, op0=ALU.mult)
                    mm = v.tensor_scalar(st[:, 2:3, :], abss, 1.0 / DK, None, op0=ALU.mult)
                    v.tensor_tensor(st_scr[:, 0:4], sums, sums, op=ALU.mult)
                    v.drain()
                    v.scalar_tensor_tensor(st_scr[:, 4:8], st_scr[:, 0:4], -1.0 / DK, sqs,
                                           op0=ALU.mult, op1=ALU.add)
                    v.drain()
                    mm = v.tensor_scalar(st[:, 1:2, :], st_scr[:, 4:8], 1.0 / DK, None,
                                         op0=ALU.mult)
                c.inc(mm, "dv")

            v_drv = c.cnt["dv"]

            @blk.scalar
            def _(s, v_drv=v_drv):
                c.wait(s, "dv", v_drv)
                for ti in range(NT):
                    sqs = raw[:, ti * 12 + 4: ti * 12 + 8]
                    st = stats_tm[:, ti * 16: ti * 16 + 16].rearrange("p (x f) -> p f x", f=4)
                    mm = s.activation(st[:, 3:4, :], sqs, AF.Sqrt)
                c.inc(mm, "ac")

        v_stats = c.cnt["ac"]

        for grp in range(4):
            u = grp % 2
            with nc.Block(f"hd{head}_sttr{grp}") as blk:
                @blk.tensor
                def _(t, grp=grp, u=u):
                    if grp == 0:
                        c.wait(t, "ac", v_stats)
                        c.wait(t, "dv")
                    c.pu_guard(t, u)
                    for q in range(4):
                        ti = grp * 4 + q
                        mm = t.transpose(g.pu[u][0:16, q * 128: q * 128 + 128],
                                         stats_tm[:, ti * 16: ti * 16 + 16], g.identf[:])
                    c.inc(mm, "pe")

                v_tr = c.cnt["pe"]

                @blk.vector
                def _(v, grp=grp, u=u, v_tr=v_tr):
                    c.wait(v, "pe", v_tr)
                    c.inc(v.tensor_copy(stats_fm[:, grp * 512: grp * 512 + 512],
                                        g.pu[u][0:16, :]), "dv")
                    c.pu_read(u, "dv")

        emit_prepass(c, g, 48, f"h{head}p")
        with nc.Block(f"hd{head}_w1ld") as blk:
            @blk.gpsimd
            def _(gg):
                c.wait(gg, "pe")
                c.wait(gg, "dv")
                c.inc(gg.dma_start(out=w1m[:],
                                   in_=g.W1_h[0:1024, :].rearrange("(c p) o -> p c o", p=128)),
                      "dwt", 16)

        v_w1m = c.cnt["dwt"]

        for ts4 in range(4):
            for c2 in range(NC):
                u = 2 + c2 % 2
                with nc.Block(f"hd{head}_gm{ts4}_{c2}") as blk:
                    @blk.tensor
                    def _(t, ts4=ts4, c2=c2, u=u):
                        c.pu_guard(t, u)
                        if ts4 == 0 and c2 == 0:
                            c.wait(t, "dwt", v_w1m)
                            c.wait(t, "dv")
                        t.matmul(g.pu[u][:], g.w1s_sb[:, c2 * 128: c2 * 128 + 128],
                                 stats_fm[:, ts4 * 512: ts4 * 512 + 512],
                                 start=True, stop=False)
                        for k in range(NC):
                            mm = t.matmul(g.pu[u][:],
                                          w1m[:, k * D + c2 * 128: k * D + c2 * 128 + 128],
                                          g.x_ln_fm[:, k * L + ts4 * 512: k * L + ts4 * 512 + 512],
                                          start=False, stop=(k == NC - 1))
                        c.inc(mm, "pe")

                    v_mm = c.cnt["pe"]

                    @blk.scalar
                    def _(s, ts4=ts4, c2=c2, u=u, v_mm=v_mm):
                        c.wait(s, "pe", v_mm)
                        c.inc(s.activation(h1h[:, c2 * 512: c2 * 512 + 512], g.pu[u][:],
                                           AF.Gelu, bias=g.b1_sb[:, c2: c2 + 1]), "ac")
                        c.pu_read(u, "ac")

            v_h1 = c.cnt["ac"]
            with nc.Block(f"hd{head}_gw2_{ts4}") as blk:
                @blk.tensor
                def _(t, ts4=ts4, v_h1=v_h1):
                    c.wait(t, "ac", v_h1)
                    c.pu_guard(t, 4)
                    for c2 in range(NC):
                        mm = t.matmul(g.pu[4][0:4, 0:512],
                                      g.w2_sb[:, c2 * 4: c2 * 4 + 4],
                                      h1h[:, c2 * 512: c2 * 512 + 512],
                                      start=(c2 == 0), stop=(c2 == NC - 1))
                    c.inc(mm, "pe")

                v_lg = c.cnt["pe"]

                @blk.vector
                def _(v, ts4=ts4, v_lg=v_lg):
                    c.wait(v, "pe", v_lg)
                    c.inc(v.tensor_scalar(logits_sb[0:4, ts4 * 512: ts4 * 512 + 512],
                                          g.pu[4][0:4, 0:512],
                                          g.gb_sb[:, head: head + 1],
                                          g.invtemp4[:, head: head + 1],
                                          op0=ALU.add, op1=ALU.mult), "dv")
                    c.pu_read(4, "dv")

        v_logits = c.cnt["dv"]
        with nc.Block(f"hd{head}_ptr") as blk:
            @blk.tensor
            def _(t):
                c.wait(t, "dv", v_logits)
                c.pu_guard(t, 5)
                for ti in range(NT):
                    mm = t.transpose(g.pu[5][:, ti * 4: ti * 4 + 4],
                                     logits_sb[0:4, ti * 128: ti * 128 + 128],
                                     g.identf[0:4, 0:4])
                c.inc(mm, "pe")

            v_tr = c.cnt["pe"]

            @blk.vector
            def _(v, v_tr=v_tr):
                c.wait(v, "pe", v_tr)
                c.inc(v.tensor_copy(p_scr[:], g.pu[5][:, 0:64]), "dv")
                c.pu_read(5, "dv")

        for ti in range(NT):
            with nc.Block(f"hd{head}_sm{ti}") as blk:
                @blk.vector
                def _(v, ti=ti):
                    if ti == 0:
                        c.wait(v, "dv")
                    c.inc(v.tensor_reduce(sm_scr[:, 0:1], p_scr[:, ti * 4: ti * 4 + 4],
                                          axis=AX.X, op=ALU.max, negate=True), "dv")

                v_mx = c.cnt["dv"]

                @blk.scalar
                def _(s, ti=ti, v_mx=v_mx):
                    c.wait(s, "dv", v_mx)
                    c.inc(s.activation(p_scr[:, ti * 4: ti * 4 + 4],
                                       p_scr[:, ti * 4: ti * 4 + 4],
                                       AF.Exp, bias=sm_scr[:, 0:1]), "ac")

                v_ex = c.cnt["ac"]

                @blk.vector
                def _(v, ti=ti, v_ex=v_ex):
                    c.wait(v, "ac", v_ex)
                    v.tensor_reduce(sm_scr[:, 1:2], p_scr[:, ti * 4: ti * 4 + 4],
                                    axis=AX.X, op=ALU.add)
                    v.drain()
                    v.reciprocal(sm_scr[:, 2:3], sm_scr[:, 1:2])
                    v.drain()
                    v.tensor_scalar(p_scr[:, ti * 4: ti * 4 + 4],
                                    p_scr[:, ti * 4: ti * 4 + 4],
                                    sm_scr[:, 2:3], None, op0=ALU.mult)
                    v.drain()
                    v.tensor_tensor(p_scr[:, ti * 4: ti * 4 + 4],
                                    p_scr[:, ti * 4: ti * 4 + 4], g.clampt[:], op=ALU.max)
                    v.drain()
                    v.tensor_reduce(sm_scr[:, 3:4], p_scr[:, ti * 4: ti * 4 + 4],
                                    axis=AX.X, op=ALU.add)
                    v.drain()
                    v.reciprocal(sm_scr[:, 4:5], sm_scr[:, 3:4])
                    v.drain()
                    v.tensor_scalar(p_scr[:, ti * 4: ti * 4 + 4],
                                    p_scr[:, ti * 4: ti * 4 + 4],
                                    sm_scr[:, 4:5], None, op0=ALU.mult)
                    v.drain()
                    c.inc(v.tensor_tensor(p_scr[:, ti * 4: ti * 4 + 1],
                                          p_scr[:, ti * 4: ti * 4 + 1],
                                          g.crb[:, head: head + 1], op=ALU.add), "dv")

        c.tap(f"h{head}_p", p_scr[:], "dv")
        c.flush_taps()

        for ti in range(NT):
            with nc.Block(f"hd{head}_mix{ti}") as blk:
                @blk.vector
                def _(v, ti=ti):
                    if ti == 0:
                        c.wait(v, "dv")
                    c.wait(v, "ac")
                    v.tensor_scalar(o_acc[:], fs_tm[:, ti * DK: ti * DK + DK],
                                    p_scr[:, ti * 4: ti * 4 + 1], None, op0=ALU.mult)
                    for x, src in ((1, fl_tm), (2, delta_tm), (3, v_tm)):
                        v.drain()
                        mm = v.scalar_tensor_tensor(o_acc[:], src[:, ti * DK: ti * DK + DK],
                                                    p_scr[:, ti * 4 + x: ti * 4 + x + 1],
                                                    o_acc[:], op0=ALU.mult, op1=ALU.add)
                    c.inc(mm, "dv")

                v_mix = c.cnt["dv"]

                @blk.scalar
                def _(s, ti=ti, v_mix=v_mix):
                    c.wait(s, "dv", v_mix)
                    c.inc(s.activation(st_junk[:], o_acc[:], AF.Square,
                                       accum_out=sm_scr[:, 5:6]), "ac")
                    c.wait(s, "ac")
                    c.inc(s.activation(sm_scr[:, 6:7], sm_scr[:, 5:6], AF.Sqrt,
                                       scale=1.0 / DK, bias=g.cb[:, 2:3]), "ac")

                v_rs = c.cnt["ac"]

                @blk.vector
                def _(v, ti=ti, v_rs=v_rs):
                    c.wait(v, "ac", v_rs)
                    c.wait(v, "pe")
                    c.wait(v, "dv")
                    v.reciprocal(sm_scr[:, 7:8], sm_scr[:, 6:7])
                    v.drain()
                    c.inc(v.scalar_tensor_tensor(o_n[:], o_acc[:], sm_scr[:, 7:8],
                                                 g.rmsw_bc[:], op0=ALU.mult, op1=ALU.mult), "dv")

                v_on = c.cnt["dv"]

                @blk.tensor
                def _(t, ti=ti, v_on=v_on):
                    c.wait(t, "dv", v_on)
                    u = ti % 2
                    c.pu_guard(t, u)
                    for cc in range(2):
                        mm = t.transpose(g.pub[u][:, cc * 128: cc * 128 + 128],
                                         o_n[:, cc * 128: cc * 128 + 128], g.identb[:])
                    c.inc(mm, "pe")

                v_otr = c.cnt["pe"]

                @blk.vector
                def _(v, ti=ti, v_otr=v_otr):
                    c.wait(v, "pe", v_otr)
                    u = ti % 2
                    for cc in range(2):
                        gc = 2 * head + cc
                        mm = v.tensor_copy(g.o_fm[:, gc * L + ti * 128: gc * L + ti * 128 + 128],
                                           g.pub[u][:, cc * 128: cc * 128 + 128])
                    c.inc(mm, "dv")
                    c.pu_read(u, "dv")



def wo_residual(c, g):
    nc = g.nc
    fence(c, nc, "wo_fence")
    with contextlib.ExitStack() as ws:
        def sb(name, shape, dtype):
            return ws.enter_context(nc.sbuf_tensor(f"wo_{name}", shape, dtype))

        wo_sb = sb("w", [128, NC * D], BF16)
        x_slab = [sb(f"xs{i}", [128, 4 * D], F32) for i in range(2)]
        ao_fm = sb("ao", [128, 512], BF16)

        with nc.Block("wo_ld") as blk:
            @blk.gpsimd
            def _(gg):
                c.inc(gg.dma_start(out=wo_sb[:],
                                   in_=g.Wo_h.rearrange("(c p) o -> p c o", p=128)), "dwt", 16)

        v_wo = c.cnt["dwt"]
        xs_used = []
        for ts4 in range(4):
            with nc.Block(f"wo_x{ts4}") as blk:
                @blk.sync
                def _(sy, ts4=ts4):
                    if ts4 >= 2:
                        c.wait(sy, "dv", xs_used[ts4 - 2])
                    c.inc(sy.dma_start(
                        out=x_slab[ts4 % 2][:],
                        in_=g.x_hbm[ts4 * 512:(ts4 + 1) * 512, :].rearrange(
                            "(t p) d -> p t d", p=128)), "dx0" if ts4 % 2 == 0 else "dx1", 16)

            wxs = "dx0" if ts4 % 2 == 0 else "dx1"
            v_xs = c.cnt[wxs]
            for co in range(NC):
                u = co % 2
                ub = 2 + co % 2
                with nc.Block(f"wo_{ts4}_{co}") as blk:
                    @blk.tensor
                    def _(t, ts4=ts4, co=co, u=u):
                        if ts4 == 0 and co == 0:
                            c.wait(t, "dwt", v_wo)
                            c.wait(t, "dv")
                        c.pu_guard(t, u)
                        for k in range(NC):
                            mm = t.matmul(g.pu[u][:],
                                          wo_sb[:, k * D + co * 128: k * D + co * 128 + 128],
                                          g.o_fm[:, k * L + ts4 * 512: k * L + ts4 * 512 + 512],
                                          start=(k == 0), stop=(k == NC - 1))
                        c.inc(mm, "pe")

                    v_mm = c.cnt["pe"]

                    @blk.scalar
                    def _(s, u=u, v_mm=v_mm):
                        c.wait(s, "pe", v_mm)
                        c.wait(s, "dv")
                        c.inc(s.activation(ao_fm[:], g.pu[u][:], AF.Copy), "ac")
                        c.pu_read(u, "ac")

                    v_ev = c.cnt["ac"]

                    @blk.tensor
                    def _(t, ub=ub, v_ev=v_ev):
                        c.wait(t, "ac", v_ev)
                        c.pu_guard(t, ub)
                        for q in range(4):
                            mm = t.transpose(g.pub[ub][:, q * 128: q * 128 + 128],
                                             ao_fm[:, q * 128: q * 128 + 128], g.identb[:])
                        c.inc(mm, "pe")

                    v_tr = c.cnt["pe"]

                    @blk.vector
                    def _(v, ts4=ts4, co=co, ub=ub, v_tr=v_tr, v_xs=v_xs, wxs=wxs):
                        c.wait(v, "pe", v_tr)
                        c.wait(v, wxs, v_xs)
                        for q in range(4):
                            tt = ts4 * 4 + q
                            mm = v.tensor_tensor(
                                g.x2_tm[:, tt * D + co * 128: tt * D + co * 128 + 128],
                                g.pub[ub][:, q * 128: q * 128 + 128],
                                x_slab[ts4 % 2][:, q * D + co * 128: q * D + co * 128 + 128],
                                op=ALU.add)
                        c.inc(mm, "dv")
                        c.pu_read(ub, "dv")

            xs_used.append(c.cnt["dv"])


def moe(c, g):
    nc = g.nc
    fence(c, nc, "moe_fence")
    with contextlib.ExitStack() as ms:
        def sb(name, shape, dtype):
            return ms.enter_context(nc.sbuf_tensor(f"moe_{name}", shape, dtype, side="right"))

        xl2_rot = [sb(f"xl{i}", [128, D], BF16) for i in range(2)]
        ln_scr = sb("lnscr", [128, 8], F32)
        sq_junk = sb("sqj", [128, D], BF16)
        xl2_fm = sb("xfm", [128, NC * SLAB], BF16)
        lg_sb = sb("lg", [8, SLAB], F32)
        lg_tm = sb("lgtm", [128, 8 * 8], F32)
        g_tm = sb("gtm", [128, 8 * 8], F32)
        m8 = sb("m8", [128, 8], F32)
        e1 = sb("e1", [128, 8], F32)
        cum = sb("cum", [128, 8], F32)
        pr2 = sb("pr2", [128, 8], F32)
        ones8 = sb("ones8", [128, 8], F32)
        gsc = sb("gsc", [128, 8], F32)
        acc = sb("acc", [128, 8 * D], F32)
        h_sb = sb("h", [128, NF * SLAB], BF16)
        NWB = 4
        wvb = [sb(f"wv{i}", [128, NC * 128], BF16) for i in range(NWB)]
        wb_ = [sb(f"wb{i}", [128, NC * 128], BF16) for i in range(NWB)]
        w1b = [sb(f"w1b{i}", [128, D], BF16) for i in range(NWB)]
        ysc = sb("ysc", [128, 512], BF16)
        s1b = sb("s1b", [128, D], F32)
        s1e = sb("s1e", [1, D], F32)

        with nc.Block("moe_init") as blk:
            @blk.vector
            def _(v):
                c.inc(v.memset(ones8[:], 1.0), "dv")

        v_store = None
        for slab in range(NSLAB):
            ln_done = []
            tr_done = []
            ev_done = []
            for q in range(8):
                tt = slab * 8 + q
                with nc.Block(f"m_ln{slab}_{q}") as blk:
                    @blk.vector
                    def _(v, q=q, tt=tt):
                        if q == 0 and slab > 0:
                            c.wait(v, "dout", v_store)
                        c.inc(v.tensor_reduce(ln_scr[:, 0:1],
                                              g.x2_tm[:, tt * D: tt * D + D],
                                              axis=AX.X, op=ALU.add), "dv")

                    @blk.scalar
                    def _(s, q=q, tt=tt):
                        c.inc(s.activation(sq_junk[:], g.x2_tm[:, tt * D: tt * D + D],
                                           AF.Square, accum_out=ln_scr[:, 1:2]), "ac")

                    v_sq = c.cnt["ac"]

                    @blk.vector
                    def _(v, v_sq=v_sq):
                        c.wait(v, "ac", v_sq)
                        c.wait(v, "dv")
                        v.tensor_tensor(ln_scr[:, 2:3], ln_scr[:, 0:1], ln_scr[:, 0:1], op=ALU.mult)
                        v.drain()
                        c.inc(v.scalar_tensor_tensor(ln_scr[:, 3:4], ln_scr[:, 2:3], -1.0 / D,
                                                     ln_scr[:, 1:2], op0=ALU.mult, op1=ALU.add), "dv")

                    v_var = c.cnt["dv"]

                    @blk.scalar
                    def _(s, v_var=v_var):
                        c.wait(s, "dv", v_var)
                        c.inc(s.activation(ln_scr[:, 4:5], ln_scr[:, 3:4], AF.Sqrt,
                                           scale=1.0 / D, bias=g.cb[:, 2:3]), "ac")

                    v_std = c.cnt["ac"]

                    @blk.vector
                    def _(v, q=q, tt=tt, v_std=v_std):
                        c.wait(v, "ac", v_std)
                        v.reciprocal(ln_scr[:, 5:6], ln_scr[:, 4:5])
                        v.tensor_scalar(ln_scr[:, 6:7], ln_scr[:, 0:1], -1.0 / D, None, op0=ALU.mult)
                        v.drain()
                        if q >= 2:
                            c.wait(v, "pe", tr_done[q - 2])
                        ln_done.append(c.inc(v.tensor_scalar(xl2_rot[q % 2][:],
                                                             g.x2_tm[:, tt * D: tt * D + D],
                                                             ln_scr[:, 6:7], ln_scr[:, 5:6],
                                                             op0=ALU.add, op1=ALU.mult), "dv"))

                    v_xl = c.cnt["dv"]

                    @blk.tensor
                    def _(t, q=q, v_xl=v_xl):
                        c.wait(t, "dv", v_xl)
                        u = q % 2
                        c.pu_guard(t, u)
                        c.pu_guard(t, 2 + u)
                        for k in range(NC):
                            pb = g.pub[u] if k < 4 else g.pub[2 + u]
                            mm = t.transpose(pb[:, (k % 4) * 128:(k % 4) * 128 + 128],
                                             xl2_rot[q % 2][:, k * 128:(k + 1) * 128], g.identb[:])
                        tr_done.append(c.inc(mm, "pe"))

                    v_tr = c.cnt["pe"]

                    @blk.vector
                    def _(v, q=q, v_tr=v_tr):
                        c.wait(v, "pe", v_tr)
                        u = q % 2
                        for k in range(NC):
                            pb = g.pub[u] if k < 4 else g.pub[2 + u]
                            mm = v.tensor_copy(
                                xl2_fm[:, k * SLAB + q * 128: k * SLAB + q * 128 + 128],
                                pb[:, (k % 4) * 128:(k % 4) * 128 + 128])
                        ev_done.append(c.inc(mm, "dv"))
                        c.pu_read(u, "dv")
                        c.pu_read(2 + u, "dv")

            v_xl2 = c.cnt["dv"]

            for th in range(2):
                u = 4 + th
                with nc.Block(f"m_rt{slab}_{th}") as blk:
                    @blk.tensor
                    def _(t, th=th, u=u):
                        if th == 0:
                            c.wait(t, "dv", v_xl2)
                        c.pu_guard(t, u)
                        for k in range(NC):
                            mm = t.matmul(g.pu[u][0:8, 0:512],
                                          g.rw_sb[:, k * 8: k * 8 + 8],
                                          xl2_fm[:, k * SLAB + th * 512: k * SLAB + th * 512 + 512],
                                          start=(k == 0), stop=(k == NC - 1))
                        c.inc(mm, "pe")

                    v_mm = c.cnt["pe"]

                    @blk.vector
                    def _(v, th=th, u=u, v_mm=v_mm):
                        c.wait(v, "pe", v_mm)
                        c.inc(v.tensor_copy(lg_sb[0:8, th * 512: th * 512 + 512],
                                            g.pu[u][0:8, 0:512]), "dv")
                        c.pu_read(u, "dv")

            v_lg = c.cnt["dv"]
            with nc.Block(f"m_rtt{slab}") as blk:
                @blk.tensor
                def _(t):
                    c.wait(t, "dv", v_lg)
                    c.pu_guard(t, 6)
                    for q in range(8):
                        mm = t.transpose(g.pu[6][:, q * 8: q * 8 + 8],
                                         lg_sb[0:8, q * 128: q * 128 + 128],
                                         g.identf[0:8, 0:8])
                    c.inc(mm, "pe")

                v_tr = c.cnt["pe"]

                @blk.vector
                def _(v, v_tr=v_tr):
                    c.wait(v, "pe", v_tr)
                    c.inc(v.tensor_copy(lg_tm[:], g.pu[6][:, 0:64]), "dv")
                    c.pu_read(6, "dv")

            for q in range(8):
                with nc.Block(f"m_g{slab}_{q}") as blk:
                    @blk.vector
                    def _(v, q=q):
                        if q == 0:
                            c.wait(v, "dv")
                        c.inc(v.tensor_reduce(gsc[:, 0:1], lg_tm[:, q * 8: q * 8 + 8],
                                              axis=AX.X, op=ALU.max, negate=True), "dv")

                    v_mx = c.cnt["dv"]

                    @blk.scalar
                    def _(s, q=q, v_mx=v_mx):
                        c.wait(s, "dv", v_mx)
                        c.inc(s.activation(lg_tm[:, q * 8: q * 8 + 8],
                                           lg_tm[:, q * 8: q * 8 + 8],
                                           AF.Exp, bias=gsc[:, 0:1]), "ac")

                    v_ex = c.cnt["ac"]

                    @blk.vector
                    def _(v, q=q, v_ex=v_ex):
                        c.wait(v, "ac", v_ex)
                        pr = lg_tm[:, q * 8: q * 8 + 8]
                        v.tensor_reduce(gsc[:, 1:2], pr, axis=AX.X, op=ALU.add)
                        v.drain()
                        v.reciprocal(gsc[:, 2:3], gsc[:, 1:2])
                        v.drain()
                        v.tensor_scalar(pr, pr, gsc[:, 2:3], None, op0=ALU.mult)
                        v.drain()
                        v.max(m8[:], pr)
                        v.drain()
                        v.tensor_scalar(e1[:], pr, m8[:, 0:1], None, op0=ALU.is_equal)
                        v.drain()
                        v.tensor_tensor_scan(cum[:], e1[:], ones8[:], 0.0,
                                             op0=ALU.add, op1=ALU.mult)
                        v.drain()
                        v.tensor_scalar(cum[:], cum[:], 1.0, None, op0=ALU.is_equal)
                        v.drain()
                        v.tensor_tensor(e1[:], e1[:], cum[:], op=ALU.mult)
                        v.drain()
                        v.tensor_scalar(g_tm[:, q * 8: q * 8 + 8], e1[:], m8[:, 0:1],
                                        None, op0=ALU.mult)
                        v.drain()
                        v.scalar_tensor_tensor(pr2[:], e1[:], -1e30, pr,
                                               op0=ALU.mult, op1=ALU.add)
                        v.drain()
                        v.tensor_scalar(e1[:], pr2[:], m8[:, 1:2], None, op0=ALU.is_equal)
                        v.drain()
                        v.tensor_tensor_scan(cum[:], e1[:], ones8[:], 0.0,
                                             op0=ALU.add, op1=ALU.mult)
                        v.drain()
                        v.tensor_scalar(cum[:], cum[:], 1.0, None, op0=ALU.is_equal)
                        v.drain()
                        v.tensor_tensor(e1[:], e1[:], cum[:], op=ALU.mult)
                        v.drain()
                        v.tensor_scalar(e1[:], e1[:], m8[:, 1:2], None, op0=ALU.mult)
                        v.drain()
                        c.inc(v.tensor_tensor(g_tm[:, q * 8: q * 8 + 8],
                                              g_tm[:, q * 8: q * 8 + 8], e1[:],
                                              op=ALU.add), "dv")

            v_gates = c.cnt["dv"]
            if slab == 0:
                c.tap("g_tm", g_tm[:], "dv")
                c.tap("xl2_fm", xl2_fm[:], "dv")
                c.flush_taps()

            for e in range(E):
                with nc.Block(f"m_s1{slab}_{e}") as blk:
                    @blk.sync
                    def _(sy, e=e):
                        c.wait(sy, "pe")
                        c.inc(sy.dma_start(out=s1e[:], in_=g.ms1_h[e:e + 1, :]), "ds1", 16)

                    v_s1d = c.cnt["ds1"]

                    @blk.tensor
                    def _(t, e=e, v_s1d=v_s1d):
                        if e == 0:
                            c.wait(t, "dv", v_gates)
                        c.wait(t, "ds1", v_s1d)
                        c.pu_guard(t, 6)
                        c.pu_guard(t, 7)
                        for hh in range(2):
                            mm = t.matmul(g.pu[6 + hh][:], g.onesrow[:],
                                          s1e[0:1, hh * 512: hh * 512 + 512],
                                          start=True, stop=True)
                        c.inc(mm, "pe")

                    v_s1 = c.cnt["pe"]

                    @blk.vector
                    def _(v, v_s1=v_s1):
                        c.wait(v, "pe", v_s1)
                        v.tensor_copy(s1b[:, 0:512], g.pu[6][:])
                        c.inc(v.tensor_copy(s1b[:, 512:1024], g.pu[7][:]), "dv")
                        c.pu_read(6, "dv")
                        c.pu_read(7, "dv")

                pe_b = c.cnt["pe"]; din_b = c.cnt["din"]
                ac_b = c.cnt["ac"]; dv_b = c.cnt["dv"]
                with nc.Block(f"m_vy{slab}_{e}") as blk:
                    wtarget = []

                    @blk.sync
                    def _(sy, e=e):
                        c.wait(sy, "dpp", g.pp_total)
                        for f in range(NF):
                            sl_ = f"dw{f % NWB}"
                            if f >= NWB - 1:
                                c.wait(sy, "pe", pe_b + 2 * (f - (NWB - 1) + 1))
                            row = (e * NF + f) * 128
                            c.inc(sy.dma_start(out=wvb[f % NWB][:],
                                               in_=g.wvt_d[row:row + 128, :]), sl_, 16)
                            c.inc(sy.dma_start(out=wb_[f % NWB][:],
                                               in_=g.wt_d[row:row + 128, :]), sl_, 16)
                            wtarget.append(c.cnt[sl_])

                    @blk.tensor
                    def _(t, e=e):
                        for f in range(NF):
                            c.wait(t, f"dw{f % NWB}", wtarget[f])
                            for th in range(2):
                                idx = 2 * f + th
                                par = idx % 2
                                ua, ub = par * 2, par * 2 + 1
                                if idx < 2:
                                    c.pu_guard(t, ua)
                                    c.pu_guard(t, ub)
                                else:
                                    c.wait(t, "ac", ac_b + idx - 1)
                                    c.wait(t, "dv", dv_b + idx - 1)
                                for k in range(NC):
                                    t.matmul(g.pu[ua][:],
                                             wvb[f % NWB][:, k * 128: k * 128 + 128],
                                             xl2_fm[:, k * SLAB + th * 512: k * SLAB + th * 512 + 512],
                                             start=(k == 0), stop=(k == NC - 1))
                                for k in range(NC):
                                    mm = t.matmul(g.pu[ub][:],
                                                  wb_[f % NWB][:, k * 128: k * 128 + 128],
                                                  xl2_fm[:, k * SLAB + th * 512: k * SLAB + th * 512 + 512],
                                                  start=(k == 0), stop=(k == NC - 1))
                                c.inc(mm, "pe")

                    @blk.scalar
                    def _(s, e=e):
                        for f in range(NF):
                            for th in range(2):
                                idx = 2 * f + th
                                par = idx % 2
                                ua, ub = par * 2, par * 2 + 1
                                c.wait(s, "pe", pe_b + idx + 1)
                                c.wait(s, "dv", dv_b + idx)
                                c.inc(s.activation(ysc[:], g.pu[ub][:], AF.Gelu_apprx_tanh,
                                                   scale=g.s_sb[:, e * NF + f: e * NF + f + 1]),
                                      "ac")

                    @blk.vector
                    def _(v, e=e):
                        for f in range(NF):
                            for th in range(2):
                                idx = 2 * f + th
                                par = idx % 2
                                ua = par * 2
                                c.wait(v, "ac", ac_b + idx + 1)
                                cp = v.scalar_tensor_tensor(
                                    h_sb[:, f * SLAB + th * 512: f * SLAB + th * 512 + 512],
                                    g.pu[ua][:],
                                    g.sv_sb[:, e * NF + f: e * NF + f + 1],
                                    ysc[:], op0=ALU.mult, op1=ALU.mult)
                                c.inc(cp, "dv")

                for u in range(4):
                    c.pu_read(u, "dv")

                for th in range(2):
                    pe_b = c.cnt["pe"]; din_b = c.cnt["din"]; dv_b = c.cnt["dv"]
                    v_s1b = c.cnt["dv"]
                    with nc.Block(f"m_o{slab}_{e}_{th}") as blk:
                        wtarget = []

                        @blk.sync
                        def _(sy, e=e, th=th):
                            c.wait(sy, "dpp", g.pp_total)
                            for f in range(NF):
                                sl_ = f"dw{f % NWB}"
                                if f >= NWB - 1:
                                    c.wait(sy, "pe", pe_b + (f - (NWB - 1) + 1))
                                row = (e * NF + f) * 128
                                c.inc(sy.dma_start(out=w1b[f % NWB][:],
                                                   in_=g.w1t_d[row:row + 128, :]), sl_, 16)
                                wtarget.append(c.cnt[sl_])

                        @blk.vector
                        def _(v, e=e, th=th):
                            for f in range(NF):
                                c.wait(v, f"dw{f % NWB}", wtarget[f])
                                if f >= NWB:
                                    c.wait(v, "pe", pe_b + (f - NWB + 1))
                                c.inc(v.tensor_tensor(w1b[f % NWB][:], w1b[f % NWB][:],
                                                      s1b[:], op=ALU.mult), "dv")

                        @blk.tensor
                        def _(t, e=e, th=th):
                            for u in range(8):
                                c.pu_guard(t, u)
                            for f in range(NF):
                                c.wait(t, "dv", dv_b + f + 1)
                                for dh in range(2):
                                    for tq in range(4):
                                        u = dh * 4 + tq
                                        mm = t.matmul(
                                            g.pu[u][:],
                                            h_sb[:, f * SLAB + th * 512 + tq * 128:
                                                 f * SLAB + th * 512 + tq * 128 + 128],
                                            w1b[f % NWB][:, dh * 512: dh * 512 + 512],
                                            start=(f == 0), stop=(f == NF - 1))
                                c.inc(mm, "pe")

                    v_o = c.cnt["pe"]
                    with nc.Block(f"m_oe{slab}_{e}_{th}") as blk:
                        @blk.vector
                        def _(v, e=e, th=th, slab=slab, v_o=v_o):
                            c.wait(v, "pe", v_o)
                            for dh in range(2):
                                for tq in range(4):
                                    u = dh * 4 + tq
                                    tl = th * 4 + tq
                                    tt = slab * 8 + tl
                                    gcol = g_tm[:, tl * 8 + e: tl * 8 + e + 1]
                                    if e == 0:
                                        base = g.x2_tm[:, tt * D + dh * 512: tt * D + dh * 512 + 512]
                                    else:
                                        base = acc[:, tl * D + dh * 512: tl * D + dh * 512 + 512]
                                    mm = v.scalar_tensor_tensor(
                                        acc[:, tl * D + dh * 512: tl * D + dh * 512 + 512],
                                        g.pu[u][:], gcol, base, op0=ALU.mult, op1=ALU.add)
                                    c.pu_read(u, "dv", c.cnt["dv"] + 1)
                            c.inc(mm, "dv")

            v_acc = c.cnt["dv"]
            with nc.Block(f"m_st{slab}") as blk:
                @blk.sync
                def _(sy, slab=slab, v_acc=v_acc):
                    c.wait(sy, "dv", v_acc)
                    c.inc(sy.dma_start(
                        out=g.out_hbm[slab * SLAB:(slab + 1) * SLAB, :].rearrange(
                            "(t p) d -> p t d", p=128),
                        in_=acc[:]), "dout", 16)

            v_store = c.cnt["dout"]


def _host_consts():
    p32 = np.arange(128)[:, None] % 32
    j32 = np.arange(32)[None, :]
    masks = np.zeros((128, 128), np.float32)
    masks[:, 0:32] = np.where(p32 < j32, -1.0, 0.0)
    masks[:, 32:64] = np.where(p32 > j32, -1.0, 0.0)
    masks[:, 64:96] = np.where(p32 <= j32, 1.0, 0.0)
    masks[:, 96:128] = np.where(p32 == j32, 1.0, 0.0)
    clamp = np.zeros((128, 4), np.float32)
    clamp[:, 0] = 0.05
    clamp[:, 1] = 0.05
    return {
        "ident_bf": np.eye(128, dtype=BF),
        "ident_f32": np.eye(128, dtype=np.float32),
        "masks_c": masks,
        "clamp": clamp,
        "ones_col": np.ones((128, 1), dtype=BF),
        "ones_row": np.ones((1, 128), dtype=np.float32),
        "ones_row_bf": np.ones((1, 128), dtype=BF),
        "cbias": np.tile(np.array([1.0, 1e-6, 1e-5, 0.0], np.float32), (128, 1)),
    }


def _chunked(w):
    cnum, j = w.shape[0] // 128, w.shape[1]
    return np.ascontiguousarray(w.reshape(cnum, 128, j).transpose(1, 0, 2).reshape(128, cnum * j))


def _expertcols(s):
    e, ff = s.shape
    return np.ascontiguousarray(s.reshape(e, ff // 128, 128).transpose(2, 0, 1).reshape(128, e * (ff // 128)))


def make_in_maps(inputs):
    f = np.float32
    a = lambda k: np.ascontiguousarray(np.asarray(inputs[k], dtype=f))
    shared = {
        "Wq": a("L0_attn_Wq"), "Wk": a("L0_attn_Wk"), "Wv": a("L0_attn_Wv"),
        "Wb": _chunked(a("L0_attn_Wb")),
        "conv_q": _chunked(a("L0_attn_conv_q")),
        "conv_k": _chunked(a("L0_attn_conv_k")),
        "conv_v": _chunked(a("L0_attn_conv_v")),
        "fir_long": _chunked(a("L0_attn_fir_long").reshape(D, 64)),
        "fir_short": _chunked(a("L0_attn_fir_short").reshape(D, 5)),
        "W1": a("L0_attn_W1"),
        "b1": np.ascontiguousarray(a("L0_attn_b1").reshape(8, 128).T),
        "W2": _chunked(a("L0_attn_W2")),
        "bbT": np.ascontiguousarray(a("L0_attn_base_bias").T),
        "b2c": a("L0_attn_b2").reshape(4, 1),
        "log_temp": a("L0_attn_log_temp").reshape(1, 4),
        "crl": a("L0_attn_conv_residual_logit").reshape(1, 4),
        "rms_w": a("L0_attn_rms_w").reshape(1, DK),
        "Wo": a("L0_attn_Wo"),
        "router_w": _chunked(a("L0_ffn_router_w")),
        "m_wv": a("L0_ffn_wv").reshape(E * D, FF),
        "m_sv": _expertcols(a("L0_ffn_sv").reshape(E, FF)),
        "m_w": a("L0_ffn_w").reshape(E * D, FF),
        "m_s": _expertcols(a("L0_ffn_s").reshape(E, FF)),
        "m_w1": a("L0_ffn_w1").reshape(E * FF, D),
        "m_s1": a("L0_ffn_s1").reshape(E, D),
    }
    shared.update(_host_consts())
    x_full = np.ascontiguousarray(np.asarray(inputs["L0_attn_x"], dtype=f))
    in_maps = []
    for core in range(8):
        m = dict(shared)
        m["x"] = np.ascontiguousarray(x_full[core])
        in_maps.append(m)
    return in_maps


def kernel(**inputs):
    in_maps = make_in_maps(inputs)
    nc = build_program()
    res = run_bass_kernel_spmd(nc, in_maps, core_ids=list(range(8)))
    out = np.stack([np.asarray(r["out"], dtype=np.float32) for r in res.results], axis=0)
    return out


# version 09
# speedup vs baseline: 1.1862x; 1.1862x over previous
import contextlib
import numpy as np
import ml_dtypes
import concourse.bass as bass
import concourse.mybir as mybir
from concourse.bass_utils import run_bass_kernel_spmd

F32 = mybir.dt.float32
BF16 = mybir.dt.bfloat16
AF = mybir.ActivationFunctionType
ALU = mybir.AluOpType
AX = mybir.AxisListType
BF = ml_dtypes.bfloat16

L = 2048
D = 1024
H = 4
DK = 256
NT = L // 128
NC = D // 128
NCH = L // 32
E = 8
FF = 4096
NF = FF // 128
SLAB = 1024
NSLAB = L // SLAB

DEBUG_TAPS = set()
STOP_AFTER = None


class _Cut(Exception):
    pass


class Ctx:
    def __init__(self, nc):
        self.nc = nc
        self.sem = {}
        self.cnt = {}
        self.dbg = []
        self.pu_last = {}
        self.stack = contextlib.ExitStack()

    def new_sem(self, name):
        self.sem[name] = self.stack.enter_context(self.nc.semaphore(name))
        self.cnt[name] = 0

    def inc(self, instr, name, n=1):
        self.cnt[name] += n
        instr.then_inc(self.sem[name], n)
        return self.cnt[name]

    def wait(self, eng, name, v=None):
        eng.wait_ge(self.sem[name], self.cnt[name] if v is None else v)

    def sb(self, name, shape, dtype):
        return self.stack.enter_context(self.nc.sbuf_tensor(name, shape, dtype))

    def pu_read(self, unit, semname, v=None):
        self.pu_last[unit] = (semname, self.cnt[semname] if v is None else v)

    def pu_guard(self, eng, unit):
        rec = self.pu_last.get(unit)
        if rec is not None:
            self.wait(eng, rec[0], rec[1])

    def tap(self, name, src_ap, semname):
        if name not in DEBUG_TAPS:
            return
        dram = self.nc.dram_tensor("dbg_" + name, list(src_ap.shape), src_ap.dtype,
                                   kind="ExternalOutput").ap()
        self.dbg.append((name, dram, src_ap, semname, self.cnt[semname]))

    def checkpoint(self, name):
        if STOP_AFTER != name:
            return False
        with self.nc.Block(f"cut_{name}") as blk:
            @blk.sync
            def _(sy):
                for s in ("pe", "dv", "ac", "gp", "din", "dwt", "dout"):
                    self.wait(sy, s)
        return True

    def flush_taps(self):
        if not self.dbg:
            return
        taps = self.dbg
        self.dbg = []
        self._tapn = getattr(self, "_tapn", 0) + 1
        with self.nc.Block(f"taps{self._tapn}") as blk:
            @blk.sync
            def _(sy):
                for name, dram, src, semname, v in taps:
                    sy.wait_ge(self.sem[semname], v)
                    self.inc(sy.dma_start(out=dram, in_=src), "dout", 16)


def build_program():
    nc = bass.Bass("TRN2", target_bir_lowering=False)
    c = Ctx(nc)

    def din(name, shape, dtype=F32):
        return nc.dram_tensor(name, list(shape), dtype, kind="ExternalInput").ap()

    x_hbm = din("x", (L, D))
    Wq_h = din("Wq", (D, D)); Wk_h = din("Wk", (D, D)); Wv_h = din("Wv", (D, D))
    Wb_h = din("Wb", (128, 32))
    cq_h = din("conv_q", (128, 32)); ck_h = din("conv_k", (128, 32)); cv_h = din("conv_v", (128, 32))
    firl_h = din("fir_long", (128, 512)); firs_h = din("fir_short", (128, 40))
    W1_h = din("W1", (1040, D)); b1_h = din("b1", (128, 8))
    W2_h = din("W2", (128, 32))
    bbT_h = din("bbT", (4, 4)); b2c_h = din("b2c", (4, 1))
    ltm_h = din("log_temp", (1, 4)); crl_h = din("crl", (1, 4))
    rmsw_h = din("rms_w", (1, DK))
    Wo_h = din("Wo", (D, D))
    rw_h = din("router_w", (128, 64))
    mwv_h = din("m_wv", (E * D, FF)); msv_h = din("m_sv", (128, E * NF))
    mw_h = din("m_w", (E * D, FF)); ms_h = din("m_s", (128, E * NF))
    mw1_h = din("m_w1", (E * FF, D)); ms1_h = din("m_s1", (E, D))
    identb_h = din("ident_bf", (128, 128), BF16)
    identf_h = din("ident_f32", (128, 128))
    masks_h = din("masks_c", (128, 128))
    clamp_h = din("clamp", (128, 4))
    onesc_h = din("ones_col", (128, 1), BF16)
    onesr_h = din("ones_row", (1, 128))
    onesrbf_h = din("ones_row_bf", (1, 128), BF16)
    cb_h = din("cbias", (128, 4))

    out_hbm = nc.dram_tensor("out", [L, D], F32, kind="ExternalOutput").ap()
    wvt_d = nc.dram_tensor("wvt_d", [E * NF * 128, NC * 128], BF16).ap()
    wt_d = nc.dram_tensor("wt_d", [E * NF * 128, NC * 128], BF16).ap()
    w1t_d = nc.dram_tensor("w1t_d", [E * NF * 128, D], BF16).ap()

    for s in ("din", "dwt", "dpp", "dout", "pe", "dv", "ac", "gp",
               "dw0", "dw1", "dw2", "dw3", "ds1", "dx0", "dx1"):
        c.new_sem(s)

    pu = [c.stack.enter_context(nc.psum_tensor(f"pu{i}", [128, 512], F32)) for i in range(8)]
    pub = [p[:].bitcast(BF16) for p in pu]

    identb = c.sb("identb", [128, 128], BF16)
    identf = c.sb("identf", [128, 128], F32)
    masks = c.sb("masks", [128, 128], F32)
    clampt = c.sb("clampt", [128, 4], F32)
    onescol = c.sb("onescol", [128, 1], BF16)
    onesrow = c.sb("onesrow", [1, 128], F32)
    cb = c.sb("cb", [128, 4], F32)
    cq_sb = c.sb("cq_sb", [128, 32], F32)
    ck_sb = c.sb("ck_sb", [128, 32], F32)
    cv_sb = c.sb("cv_sb", [128, 32], F32)
    firs_sb = c.sb("firs_sb", [128, 40], F32)
    firl_sb = c.sb("firl_sb", [128, 512], F32)
    b1_sb = c.sb("b1_sb", [128, 8], F32)
    bbT_sb = c.sb("bbT_sb", [4, 4], F32)
    b2c_sb = c.sb("b2c_sb", [4, 1], F32)
    gb_sb = c.sb("gb_sb", [4, 4], F32)
    ltm_sb = c.sb("ltm_sb", [1, 4], F32)
    crl_sb = c.sb("crl_sb", [1, 4], F32)
    rmsw_row = c.sb("rmsw_row", [1, DK], F32)
    w1s_sb = c.sb("w1s_sb", [16, D], BF16)
    w2_sb = c.sb("w2_sb", [128, 32], BF16)
    rw_sb = c.sb("rw_sb", [128, 64], BF16)
    wb_sb = c.sb("wb_sb", [128, 32], BF16)
    sv_sb = c.sb("sv_sb", [128, E * NF], F32)
    s_sb = c.sb("s_sb", [128, E * NF], F32)
    invtemp4 = c.sb("invtemp4", [4, 4], F32)
    crb = c.sb("crb", [128, 4], F32)
    rmsw_bc = c.sb("rmsw_bc", [128, DK], F32)
    onesrow_bf = c.sb("onesrow_bf", [1, 128], BF16)

    es_o = contextlib.ExitStack()
    es_x = contextlib.ExitStack()
    o_fm = es_o.enter_context(nc.sbuf_tensor("o_fm", [128, NC * L], BF16))
    x_ln_fm = es_x.enter_context(nc.sbuf_tensor("x_ln_fm", [128, NC * L], BF16))

    with nc.Block("ph0") as blk:
        @blk.sync
        def _(sy):
            for dst, src in [
                (identb[:], identb_h), (identf[:], identf_h), (masks[:], masks_h),
                (clampt[:], clamp_h), (onescol[:], onesc_h), (onesrow[:], onesr_h),
                (cb[:], cb_h),
                (cq_sb[:], cq_h), (ck_sb[:], ck_h), (cv_sb[:], cv_h),
                (firs_sb[:], firs_h), (firl_sb[:], firl_h), (b1_sb[:], b1_h),
                (bbT_sb[:], bbT_h), (b2c_sb[:], b2c_h),
                (ltm_sb[:], ltm_h), (crl_sb[:], crl_h), (rmsw_row[:], rmsw_h),
                (sv_sb[:], msv_h), (s_sb[:], ms_h),
                (onesrow_bf[:], onesrbf_h),
            ]:
                c.inc(sy.dma_start(out=dst, in_=src, allow_slow_non_contiguous=True),
                      "din", 16)

        @blk.gpsimd
        def _(g):
            c.inc(g.dma_start(out=w1s_sb[:], in_=W1_h[1024:1040, :]), "dwt", 16)
            c.inc(g.dma_start(out=w2_sb[:], in_=W2_h), "dwt", 16)
            c.inc(g.dma_start(out=rw_sb[:], in_=rw_h), "dwt", 16)
            c.inc(g.dma_start(out=wb_sb[:], in_=Wb_h), "dwt", 16)

    v_consts = c.cnt["din"]

    with nc.Block("ph0b") as blk:
        @blk.scalar
        def _(s):
            c.wait(s, "din", v_consts)
            c.inc(s.activation(ltm_sb[:], ltm_sb[:], AF.Exp), "ac")
            c.wait(s, "ac")
            c.inc(s.activation(ltm_sb[:], ltm_sb[:], AF.Ln, bias=cb[0:1, 0:1]), "ac")
            c.inc(s.activation(crl_sb[:], crl_sb[:], AF.Sigmoid), "ac")

        @blk.vector
        def _(v):
            c.wait(v, "ac")
            c.inc(v.tensor_scalar(ltm_sb[:], ltm_sb[:], 1e-4, None, op0=ALU.add), "dv")
            c.wait(v, "dv")
            c.inc(v.reciprocal(ltm_sb[:], ltm_sb[:]), "dv")
            c.wait(v, "dv")
            v.tensor_scalar(crl_sb[:], crl_sb[:], 0.5, None, op0=ALU.mult)
            c.inc(v.tensor_scalar(gb_sb[:], bbT_sb[:], b2c_sb[:, 0:1], None, op0=ALU.add), "dv")

        @blk.tensor
        def _(t):
            c.wait(t, "dv")
            t.matmul(pu[0][0:4, 0:4], onesrow[0:1, 0:4], ltm_sb[0:1, :], start=True, stop=True)
            t.matmul(pu[1][:, 0:4], onesrow[0:1, :], crl_sb[0:1, :], start=True, stop=True)
            c.inc(t.matmul(pu[2][:, 0:DK], onesrow[0:1, :], rmsw_row[0:1, :], start=True, stop=True), "pe")

        @blk.vector
        def _(v):
            c.wait(v, "pe")
            v.tensor_copy(invtemp4[:], pu[0][0:4, 0:4])
            v.tensor_copy(crb[:], pu[1][:, 0:4])
            c.inc(v.tensor_copy(rmsw_bc[:], pu[2][:, 0:DK]), "dv")
            for u in (0, 1, 2):
                c.pu_read(u, "dv")

    with contextlib.ExitStack() as ph1:
        x_rot = [ph1.enter_context(nc.sbuf_tensor(f"x_rot{i}", [128, D], F32)) for i in range(2)]
        xln_rot = [ph1.enter_context(nc.sbuf_tensor(f"xln_rot{i}", [128, D], BF16)) for i in range(2)]
        ln_scr = ph1.enter_context(nc.sbuf_tensor("ln_scr", [128, 8], F32))
        sq_junk = ph1.enter_context(nc.sbuf_tensor("sq_junk", [128, D], BF16))

        ln_done = []
        sq_done = []
        tr_done = []
        ev_done = []
        for ti in range(NT):
            with nc.Block(f"ln1_{ti}") as blk:
                @blk.sync
                def _(sy, ti=ti):
                    if ti >= 2:
                        c.wait(sy, "dv", ln_done[ti - 2])
                        c.wait(sy, "ac", sq_done[ti - 2])
                    c.inc(sy.dma_start(out=x_rot[ti % 2][:],
                                       in_=x_hbm[ti * 128:(ti + 1) * 128, :]),
                          "dx0" if ti % 2 == 0 else "dx1", 16)

                xs = "dx0" if ti % 2 == 0 else "dx1"
                v_x = c.cnt[xs]

                @blk.vector
                def _(v, ti=ti, v_x=v_x, xs=xs):
                    c.wait(v, xs, v_x)
                    c.inc(v.tensor_reduce(ln_scr[:, 0:1], x_rot[ti % 2][:], axis=AX.X, op=ALU.add), "dv")

                v_sum = c.cnt["dv"]

                @blk.scalar
                def _(s, ti=ti, v_x=v_x, xs=xs):
                    c.wait(s, xs, v_x)
                    sq_done.append(c.inc(s.activation(sq_junk[:], x_rot[ti % 2][:], AF.Square,
                                                      accum_out=ln_scr[:, 1:2]), "ac"))

                v_sq = c.cnt["ac"]

                @blk.vector
                def _(v, v_sq=v_sq, v_sum=v_sum):
                    c.wait(v, "ac", v_sq)
                    c.wait(v, "dv", v_sum)
                    v.tensor_tensor(ln_scr[:, 2:3], ln_scr[:, 0:1], ln_scr[:, 0:1], op=ALU.mult)
                    v.drain()
                    c.inc(v.scalar_tensor_tensor(ln_scr[:, 3:4], ln_scr[:, 2:3], -1.0 / D,
                                                 ln_scr[:, 1:2], op0=ALU.mult, op1=ALU.add), "dv")

                v_var = c.cnt["dv"]

                @blk.scalar
                def _(s, v_var=v_var):
                    c.wait(s, "dv", v_var)
                    c.inc(s.activation(ln_scr[:, 4:5], ln_scr[:, 3:4], AF.Sqrt,
                                       scale=1.0 / D, bias=cb[:, 2:3]), "ac")

                v_std = c.cnt["ac"]

                @blk.vector
                def _(v, ti=ti, v_std=v_std):
                    c.wait(v, "ac", v_std)
                    v.reciprocal(ln_scr[:, 5:6], ln_scr[:, 4:5])
                    v.tensor_scalar(ln_scr[:, 6:7], ln_scr[:, 0:1], -1.0 / D, None, op0=ALU.mult)
                    v.drain()
                    if ti >= 2:
                        c.wait(v, "pe", tr_done[ti - 2])
                    ln_done.append(c.inc(v.tensor_scalar(xln_rot[ti % 2][:], x_rot[ti % 2][:],
                                                         ln_scr[:, 6:7], ln_scr[:, 5:6],
                                                         op0=ALU.add, op1=ALU.mult), "dv"))

                v_xln = c.cnt["dv"]

                @blk.tensor
                def _(t, ti=ti, v_xln=v_xln):
                    c.wait(t, "dv", v_xln)
                    if ti >= 2:
                        c.wait(t, "dv", ev_done[ti - 2])
                    for k in range(NC):
                        pb = pub[ti % 2] if k < 4 else pub[2 + ti % 2]
                        mm = t.transpose(pb[:, (k % 4) * 128:(k % 4) * 128 + 128],
                                         xln_rot[ti % 2][:, k * 128:(k + 1) * 128], identb[:])
                    tr_done.append(c.inc(mm, "pe"))

                v_tr = c.cnt["pe"]

                @blk.vector
                def _(v, ti=ti, v_tr=v_tr):
                    c.wait(v, "pe", v_tr)
                    for k in range(NC):
                        pb = pub[ti % 2] if k < 4 else pub[2 + ti % 2]
                        cp = v.tensor_copy(x_ln_fm[:, k * L + ti * 128: k * L + ti * 128 + 128],
                                           pb[:, (k % 4) * 128:(k % 4) * 128 + 128])
                    ev_done.append(c.inc(cp, "dv"))

    v_xln_all = c.cnt["dv"]
    for u in range(4):
        c.pu_read(u, "dv", v_xln_all)
    c.tap("x_ln_fm", x_ln_fm[:], "dv")
    c.flush_taps()
    if c.checkpoint("ln1"):
        return nc

    class G:
        pass
    g = G()
    g.nc = nc
    _locals = dict(
        pu=pu, pub=pub, identb=identb, identf=identf, masks=masks, clampt=clampt,
        onescol=onescol, onesrow=onesrow, onesrow_bf=onesrow_bf, cq_sb=cq_sb,
        ck_sb=ck_sb, cv_sb=cv_sb, firs_sb=firs_sb, firl_sb=firl_sb, b1_sb=b1_sb,
        gb_sb=gb_sb, ltm_sb=ltm_sb, crl_sb=crl_sb, w1s_sb=w1s_sb, w2_sb=w2_sb,
        rw_sb=rw_sb, wb_sb=wb_sb, sv_sb=sv_sb, s_sb=s_sb, invtemp4=invtemp4,
        crb=crb, rmsw_bc=rmsw_bc, cb=cb, x_ln_fm=x_ln_fm, o_fm=o_fm, x_hbm=x_hbm,
        Wq_h=Wq_h, Wk_h=Wk_h, Wv_h=Wv_h, Wo_h=Wo_h, W1_h=W1_h, mwv_h=mwv_h,
        mw_h=mw_h, mw1_h=mw1_h, ms1_h=ms1_h, out_hbm=out_hbm)
    for nm, val in _locals.items():
        setattr(g, nm, val)

    g.wvt_d, g.wt_d, g.w1t_d = wvt_d, wt_d, w1t_d
    g.pp_done = {}
    g.pp_todo = [(e, f) for e in range(E) for f in range(NF)]

    g.x2_tm = c.stack.enter_context(nc.sbuf_tensor("x2_tm", [128, NT * D], BF16, side="right"))

    try:
        for head in range(H):
            if attn_head(c, g, head):
                return nc
    except _Cut:
        return nc

    es_x.close()

    wo_residual(c, g)
    c.tap("x2_tm", g.x2_tm[:], "dv")
    c.flush_taps()
    if c.checkpoint("wo"):
        return nc
    es_o.close()

    emit_prepass(c, g, len(g.pp_todo), "rest")
    g.pp_total = c.cnt["dpp"]

    moe(c, g)

    with nc.Block("fin") as blk:
        @blk.sync
        def _(sy):
            c.wait(sy, "dout")

    return nc


def emit_prepass(c, g, n, tag):
    todo = g.pp_todo[:n]
    del g.pp_todo[:n]
    if not todo:
        return
    with g.nc.Block(f"pp_{tag}") as blk:
        @blk.gpsimd
        def _(gg):
            for e, f in todo:
                row = (e * NF + f) * 128
                c.inc(gg.dma_start(
                    out=g.wvt_d[row:row + 128, :],
                    in_=g.mwv_h[e * D:(e + 1) * D, f * 128:(f + 1) * 128].rearrange(
                        "(cc p) w -> p cc w", p=128)), "dpp", 16)
                c.inc(gg.dma_start(
                    out=g.wt_d[row:row + 128, :],
                    in_=g.mw_h[e * D:(e + 1) * D, f * 128:(f + 1) * 128].rearrange(
                        "(cc p) w -> p cc w", p=128)), "dpp", 16)
                c.inc(gg.dma_start(
                    out=g.w1t_d[row:row + 128, :],
                    in_=g.mw1_h[e * FF + f * 128: e * FF + (f + 1) * 128, :]), "dpp", 16)
                g.pp_done[(e, f)] = c.cnt["dpp"]


def fence(c, nc, name):
    with nc.Block(name) as blk:
        @blk.tensor
        def _(t):
            for s in ("dv", "ac", "gp", "din", "dwt", "dout"):
                c.wait(t, s)

        @blk.vector
        def _(v):
            for s in ("pe", "ac", "gp", "din", "dwt", "dout"):
                c.wait(v, s)

        @blk.scalar
        def _(s_):
            for s in ("pe", "dv", "gp", "din", "dwt", "dout"):
                c.wait(s_, s)

        @blk.gpsimd
        def _(gg):
            for s in ("pe", "dv", "ac", "dout"):
                c.wait(gg, s)

        @blk.sync
        def _(sy):
            for s in ("pe", "dv", "ac", "gp", "dout", "dpp"):
                c.wait(sy, s)


def tr_fm_to_tm(c, g, name, src_fn, dst_fn, n=16, units=(6, 7)):
    ngrp = (n + 7) // 8
    pe_b = c.cnt["pe"]
    with g.nc.Block(f"{name}_tr") as blk:
        @blk.tensor
        def _(t):
            c.wait(t, "dv")
            c.wait(t, "ac")
            for grp in range(ngrp):
                lo, hi = grp * 8, min(grp * 8 + 8, n)
                u = units[grp % len(units)]
                c.pu_guard(t, u)
                pb = g.pub[u]
                for tt in range(lo, hi):
                    mm = t.transpose(pb[:, (tt - lo) * 128:(tt - lo + 1) * 128],
                                     src_fn(tt), g.identb[:])
                c.inc(mm, "pe")

        @blk.vector
        def _(v):
            for grp in range(ngrp):
                lo, hi = grp * 8, min(grp * 8 + 8, n)
                u = units[grp % len(units)]
                c.wait(v, "pe", pe_b + grp + 1)
                pb = g.pub[u]
                for tt in range(lo, hi):
                    cp = v.tensor_copy(dst_fn(tt), pb[:, (tt - lo) * 128:(tt - lo + 1) * 128])
                c.inc(cp, "dv")
                c.pu_read(u, "dv")
    return c.cnt["dv"]


def attn_head(c, g, head):
    nc = g.nc
    fence(c, nc, f"hd{head}_fence")

    with contextlib.ExitStack() as hs:
        def sb(name, shape, dtype):
            return hs.enter_context(nc.sbuf_tensor(f"h{head}_{name}", shape, dtype))

        q_n_fm = sb("q_n_fm", [128, 2 * L], BF16)
        k_tm = sb("k_tm", [128, NT * DK], BF16)
        v_tm = sb("v_tm", [128, NT * DK], BF16)
        fs_tm = sb("fs_tm", [128, NT * DK], BF16)
        fl_tm = sb("fl_tm", [128, NT * DK], BF16)
        delta_tm = sb("delta_tm", [128, NT * DK], BF16)
        Wt_sb = sb("Wt_sb", [128, 2 * NCH * 32], BF16)
        ALt_sb = sb("ALt_sb", [128, 512], BF16)
        iTb = sb("iTb", [128, 512], BF16)

        with contextlib.ExitStack() as pa:
            def sba(name, shape, dtype):
                return pa.enter_context(nc.sbuf_tensor(f"h{head}a_{name}", shape, dtype))

            wslab = sba("wslab", [128, NC * DK], BF16)
            P_buf = sba("P_buf", [128, 3 + L], BF16)
            cacc = sba("cacc", [128, L], BF16)
            beta_bf = sba("beta_bf", [1, L], BF16)
            rsq = sba("rsq", [1, 512], F32)
            k_n_fm = sba("k_n_fm", [128, 2 * L], BF16)
            iT = sba("iT", [128, 512], BF16)
            iT2 = sba("iT2", [128, 512], BF16)
            Pt = sba("Pt", [128, 512], BF16)
            Qt = sba("Qt", [128, 512], BF16)
            Pt2 = sba("Pt2", [128, 512], BF16)
            Qt2 = sba("Qt2", [128, 512], BF16)
            q2s = P_buf[:, 0:1024]
            b_cb = sba("b_cb", [128, 512], BF16)
            b_rb = sba("b_rb", [128, 512], BF16)
            qfac = sba("qfac", [128, 512], BF16)

            pe_b = c.cnt["pe"]
            ac_b = c.cnt["ac"]
            with nc.Block(f"hd{head}_bt") as blk:
                @blk.tensor
                def _(t):
                    for ts4 in range(4):
                        u = ts4 % 2
                        if ts4 < 2:
                            c.pu_guard(t, u)
                        else:
                            c.wait(t, "ac", ac_b + ts4 - 1)
                        for k in range(NC):
                            mm = t.matmul(g.pu[u][0:1, 0:512],
                                          g.wb_sb[:, k * 4 + head: k * 4 + head + 1],
                                          g.x_ln_fm[:, k * L + ts4 * 512: k * L + ts4 * 512 + 512],
                                          start=(k == 0), stop=(k == NC - 1))
                        c.inc(mm, "pe")

                @blk.scalar
                def _(s):
                    for ts4 in range(4):
                        u = ts4 % 2
                        c.wait(s, "pe", pe_b + ts4 + 1)
                        cp = s.activation(beta_bf[0:1, ts4 * 512: ts4 * 512 + 512],
                                          g.pu[u][0:1, 0:512], AF.Sigmoid)
                        c.inc(cp, "ac")
            for u in (0, 1):
                c.pu_read(u, "ac")

            def proj_part(part, csb, cc, dst, dst_off):
                gc = 2 * head + cc
                pe_b = c.cnt["pe"]
                ac_b = c.cnt["ac"]
                dv_b = c.cnt["dv"]
                with nc.Block(f"hd{head}_{part}{cc}") as blk:
                    @blk.tensor
                    def _(t):
                        for ts4 in range(4):
                            u = ts4 % 2
                            if ts4 < 2:
                                c.pu_guard(t, u)
                            else:
                                c.wait(t, "ac", ac_b + ts4 - 1)
                            for k in range(NC):
                                mm = t.matmul(g.pu[u][:],
                                              wslab[:, k * DK + cc * 128: k * DK + cc * 128 + 128],
                                              g.x_ln_fm[:, k * L + ts4 * 512: k * L + ts4 * 512 + 512],
                                              start=(k == 0), stop=(k == NC - 1))
                            c.inc(mm, "pe")

                    @blk.scalar
                    def _(s):
                        for ts4 in range(4):
                            u = ts4 % 2
                            c.wait(s, "pe", pe_b + ts4 + 1)
                            if ts4 == 0:
                                c.wait(s, "dv", dv_b)
                            cp = s.activation(P_buf[:, 3 + ts4 * 512: 3 + ts4 * 512 + 512],
                                              g.pu[u][:], AF.Copy)
                            c.inc(cp, "ac")
                            c.pu_read(u, "ac")
                        c.wait(s, "dv", dv_b + 1)
                        c.wait(s, "pe")
                        if part == "v":
                            cp = s.activation(v_fm[:, 63:63 + L], cacc[:], AF.Silu)
                        else:
                            cp = s.activation(dst[:, dst_off: dst_off + L], cacc[:], AF.Silu)
                        c.inc(cp, "ac")

                    @blk.vector
                    def _(v):
                        c.wait(v, "ac", ac_b + 4)
                        v.memset(P_buf[:, 0:3], 0.0)
                        v.drain()
                        v.tensor_scalar(cacc[:], P_buf[:, 0:L], csb[:, gc * 4: gc * 4 + 1],
                                        None, op0=ALU.mult)
                        for j in range(1, 4):
                            v.drain()
                            mm = v.scalar_tensor_tensor(cacc[:], P_buf[:, j: j + L],
                                                        csb[:, gc * 4 + j: gc * 4 + j + 1],
                                                        cacc[:], op0=ALU.mult, op1=ALU.add)
                        c.inc(mm, "dv")

            def l2norm(part, dst_fm):
                for ts4 in range(4):
                    ua, ub = 2 + ts4 % 2, 4 + ts4 % 2
                    with nc.Block(f"hd{head}_l2{part}_{ts4}") as blk:
                        @blk.scalar
                        def _(s, ts4=ts4):
                            c.wait(s, "ac")
                            c.wait(s, "dv")
                            c.wait(s, "pe")
                            for cc in range(2):
                                cp = s.activation(q2s[:, cc * 512: cc * 512 + 512],
                                                  dst_fm[:, cc * L + ts4 * 512: cc * L + ts4 * 512 + 512],
                                                  AF.Square)
                            c.inc(cp, "ac")

                        v_sq = c.cnt["ac"]

                        @blk.tensor
                        def _(t, ts4=ts4, ua=ua, v_sq=v_sq):
                            c.wait(t, "ac", v_sq)
                            c.pu_guard(t, ua)
                            for cc in range(2):
                                mm = t.matmul(g.pu[ua][0:1, 0:512], g.onescol[:],
                                              q2s[:, cc * 512: cc * 512 + 512],
                                              start=(cc == 0), stop=(cc == 1))
                            c.inc(mm, "pe")

                        v_ss = c.cnt["pe"]

                        @blk.scalar
                        def _(s, ts4=ts4, ua=ua, v_ss=v_ss):
                            c.wait(s, "pe", v_ss)
                            c.inc(s.activation(rsq[0:1, :], g.pu[ua][0:1, 0:512], AF.Ln,
                                               bias=g.cb[0:1, 1:2]), "ac")
                            c.wait(s, "ac")
                            cp = s.activation(rsq[0:1, :], rsq[0:1, :], AF.Exp, scale=-0.5)
                            c.inc(cp, "ac")
                            c.pu_read(ua, "ac")

                        v_rs = c.cnt["ac"]

                        @blk.tensor
                        def _(t, ts4=ts4, ub=ub, v_rs=v_rs):
                            c.wait(t, "ac", v_rs)
                            c.pu_guard(t, ub)
                            c.inc(t.matmul(g.pu[ub][:], g.onesrow[:], rsq[0:1, :],
                                           start=True, stop=True), "pe")

                        v_bc = c.cnt["pe"]

                        @blk.vector
                        def _(v, ts4=ts4, ub=ub, v_bc=v_bc):
                            c.wait(v, "pe", v_bc)
                            for cc in range(2):
                                sl = dst_fm[:, cc * L + ts4 * 512: cc * L + ts4 * 512 + 512]
                                cp = v.tensor_tensor(sl, sl, g.pu[ub][:], op=ALU.mult)
                            c.inc(cp, "dv")
                            c.pu_read(ub, "dv")

            def load_wslab(W_h, gate_pe):
                with nc.Block(f"hd{head}_wld{W_h.name}") as blk:
                    @blk.gpsimd
                    def _(gg):
                        c.wait(gg, "pe", gate_pe)
                        c.inc(gg.dma_start(
                            out=wslab[:],
                            in_=W_h[:, head * DK:(head + 1) * DK].rearrange(
                                "(c p) o -> p c o", p=128)), "dwt", 16)
                return c.cnt["dwt"]

            if c.checkpoint(f"h{head}beta"):
                return True
            emit_prepass(c, g, 48, f"h{head}q")
            v_w = load_wslab(g.Wq_h, 0)
            with nc.Block(f"hd{head}_qw") as blk:
                @blk.tensor
                def _(t, v_w=v_w):
                    c.wait(t, "dwt", v_w)
            proj_part("q", g.cq_sb, 0, q_n_fm, 0)
            proj_part("q", g.cq_sb, 1, q_n_fm, L)
            if c.checkpoint(f"h{head}qproj"):
                return True
            l2norm("q", q_n_fm)
            if c.checkpoint(f"h{head}ql2"):
                return True
            emit_prepass(c, g, 48, f"h{head}k")
            v_w = load_wslab(g.Wk_h, c.cnt["pe"])
            with nc.Block(f"hd{head}_kw") as blk:
                @blk.tensor
                def _(t, v_w=v_w):
                    c.wait(t, "dwt", v_w)
            proj_part("k", g.ck_sb, 0, k_n_fm, 0)
            proj_part("k", g.ck_sb, 1, k_n_fm, L)
            l2norm("k", k_n_fm)
            if c.checkpoint(f"h{head}kl2"):
                return True
            for cc in range(2):
                tr_fm_to_tm(c, g, f"hd{head}_ktm{cc}",
                            lambda tt, cc=cc: k_n_fm[:, cc * L + tt * 128: cc * L + tt * 128 + 128],
                            lambda tt, cc=cc: k_tm[:, tt * DK + cc * 128: tt * DK + cc * 128 + 128])

            c.tap(f"h{head}_q_n_fm", q_n_fm[:], "dv")
            c.tap(f"h{head}_k_n_fm", k_n_fm[:], "dv")
            c.flush_taps()
            if c.checkpoint(f"h{head}ktm"):
                return True

            try:
                _dp = delta_prep
            except NameError:
                pass
            delta_prep(c, g, head, dict(
                q_n_fm=q_n_fm, k_n_fm=k_n_fm, k_tm=k_tm, beta_bf=beta_bf,
                Wt_sb=Wt_sb, ALt_sb=ALt_sb, iTb=iTb,
                iT=iT, iT2=iT2, Pt=Pt, Qt=Qt, Pt2=Pt2, Qt2=Qt2,
                b_cb=b_cb, b_rb=b_rb, qfac=qfac))

            c.tap(f"h{head}_iTb", iTb[:], "dv")
            c.tap(f"h{head}_Wt_sb", Wt_sb[:], "dv")
            c.tap(f"h{head}_ALt_sb", ALt_sb[:], "dv")
            c.flush_taps()
            if c.checkpoint(f"h{head}a"):
                return True
            fence(c, nc, f"hd{head}_fa")

        with contextlib.ExitStack() as pb:
            def sbb(name, shape, dtype):
                return pb.enter_context(nc.sbuf_tensor(f"h{head}b_{name}", shape, dtype))

            wslab = sbb("wslab", [128, NC * DK], BF16)
            P_buf = sbb("P_buf", [128, 3 + L], BF16)
            cacc = sbb("cacc", [128, L], BF16)
            v_fm = sbb("v_fm", [128, 63 + L], BF16)
            flA = sbb("flA", [128, L], BF16)

            emit_prepass(c, g, 48, f"h{head}v")
            with nc.Block(f"hd{head}_vw") as blk:
                @blk.gpsimd
                def _(gg):
                    c.wait(gg, "pe")
                    c.inc(gg.dma_start(
                        out=wslab[:],
                        in_=g.Wv_h[:, head * DK:(head + 1) * DK].rearrange(
                            "(c p) o -> p c o", p=128)), "dwt", 16)

                v_w = c.cnt["dwt"]

                @blk.tensor
                def _(t, v_w=v_w):
                    c.wait(t, "dwt", v_w)

            for cc in range(2):
                gc = 2 * head + cc
                with nc.Block(f"hd{head}_v{cc}_pad") as blk:
                    @blk.vector
                    def _(v):
                        c.inc(v.memset(v_fm[:, 0:63], 0.0), "dv")

                for ts4 in range(4):
                    u = ts4 % 2
                    with nc.Block(f"hd{head}_v{cc}_mm{ts4}") as blk:
                        @blk.tensor
                        def _(t, ts4=ts4, u=u, cc=cc):
                            c.pu_guard(t, u)
                            for k in range(NC):
                                mm = t.matmul(g.pu[u][:],
                                              wslab[:, k * DK + cc * 128: k * DK + cc * 128 + 128],
                                              g.x_ln_fm[:, k * L + ts4 * 512: k * L + ts4 * 512 + 512],
                                              start=(k == 0), stop=(k == NC - 1))
                            c.inc(mm, "pe")

                        v_mm = c.cnt["pe"]

                        @blk.scalar
                        def _(s, ts4=ts4, u=u, v_mm=v_mm):
                            c.wait(s, "pe", v_mm)
                            if ts4 == 0:
                                c.wait(s, "dv")
                            cp = s.activation(P_buf[:, 3 + ts4 * 512: 3 + ts4 * 512 + 512],
                                              g.pu[u][:], AF.Copy)
                            c.inc(cp, "ac")
                            c.pu_read(u, "ac")

                v_evac = c.cnt["ac"]
                with nc.Block(f"hd{head}_v{cc}_conv") as blk:
                    @blk.vector
                    def _(v, gc=gc, v_evac=v_evac):
                        c.wait(v, "ac", v_evac)
                        v.memset(P_buf[:, 0:3], 0.0)
                        v.drain()
                        v.tensor_scalar(cacc[:], P_buf[:, 0:L], g.cv_sb[:, gc * 4: gc * 4 + 1],
                                        None, op0=ALU.mult)
                        for j in range(1, 4):
                            v.drain()
                            mm = v.scalar_tensor_tensor(cacc[:], P_buf[:, j: j + L],
                                                        g.cv_sb[:, gc * 4 + j: gc * 4 + j + 1],
                                                        cacc[:], op0=ALU.mult, op1=ALU.add)
                        c.inc(mm, "dv")

                    v_conv = c.cnt["dv"]

                    @blk.scalar
                    def _(s, v_conv=v_conv):
                        c.wait(s, "dv", v_conv)
                        c.wait(s, "pe")
                        c.inc(s.activation(v_fm[:, 63:63 + L], cacc[:], AF.Silu), "ac")

                tr_fm_to_tm(c, g, f"hd{head}_vtm{cc}",
                            lambda tt: v_fm[:, 63 + tt * 128: 63 + tt * 128 + 128],
                            lambda tt, cc=cc: v_tm[:, tt * DK + cc * 128: tt * DK + cc * 128 + 128])

                with nc.Block(f"hd{head}_fir{cc}") as blk:
                    @blk.vector
                    def _(v, gc=gc):
                        c.wait(v, "ac")
                        c.wait(v, "pe")
                        v.tensor_scalar(P_buf[:, 0:L], v_fm[:, 59:59 + L],
                                        g.firs_sb[:, gc * 5: gc * 5 + 1], None, op0=ALU.mult)
                        for j in range(1, 5):
                            v.drain()
                            mm = v.scalar_tensor_tensor(P_buf[:, 0:L], v_fm[:, 59 + j: 59 + j + L],
                                                        g.firs_sb[:, gc * 5 + j: gc * 5 + j + 1],
                                                        P_buf[:, 0:L], op0=ALU.mult, op1=ALU.add)
                        c.inc(mm, "dv")
                        for blk4 in range(4):
                            dst = flA if blk4 == 0 else cacc
                            j0 = blk4 * 16
                            v.drain()
                            v.tensor_scalar(dst[:], v_fm[:, j0: j0 + L],
                                            g.firl_sb[:, gc * 64 + j0: gc * 64 + j0 + 1],
                                            None, op0=ALU.mult)
                            for j in range(j0 + 1, j0 + 16):
                                v.drain()
                                v.scalar_tensor_tensor(dst[:], v_fm[:, j: j + L],
                                                       g.firl_sb[:, gc * 64 + j: gc * 64 + j + 1],
                                                       dst[:], op0=ALU.mult, op1=ALU.add)
                            if blk4 > 0:
                                v.drain()
                                mm = v.tensor_tensor(flA[:], flA[:], cacc[:], op=ALU.add)
                        c.inc(mm, "dv")

                tr_fm_to_tm(c, g, f"hd{head}_fstm{cc}",
                            lambda tt: P_buf[:, tt * 128: tt * 128 + 128],
                            lambda tt, cc=cc: fs_tm[:, tt * DK + cc * 128: tt * DK + cc * 128 + 128])
                tr_fm_to_tm(c, g, f"hd{head}_fltm{cc}",
                            lambda tt: flA[:, tt * 128: tt * 128 + 128],
                            lambda tt, cc=cc: fl_tm[:, tt * DK + cc * 128: tt * DK + cc * 128 + 128])

            c.tap(f"h{head}_v_tm", v_tm[:], "dv")
            c.tap(f"h{head}_fs_tm", fs_tm[:], "dv")
            c.tap(f"h{head}_fl_tm", fl_tm[:], "dv")
            c.flush_taps()
            if c.checkpoint(f"h{head}b"):
                return True
            fence(c, nc, f"hd{head}_fb")

        with contextlib.ExitStack() as sc_s:
            S_sb = sc_s.enter_context(nc.sbuf_tensor(f"h{head}_S", [128, 512], F32))
            S_bf = sc_s.enter_context(nc.sbuf_tensor(f"h{head}_Sbf", [128, 512], BF16))
            u_sc = sc_s.enter_context(nc.sbuf_tensor(f"h{head}_usc", [128, DK], BF16))
            delta_scan(c, g, head, dict(
                q_n_fm=q_n_fm, k_tm=k_tm, v_tm=v_tm, Wt_sb=Wt_sb, ALt_sb=ALt_sb,
                iTb=iTb, S_sb=S_sb, S_bf=S_bf, u_sc=u_sc, delta_tm=delta_tm))
            fence(c, nc, f"hd{head}_fs2")

        c.tap(f"h{head}_delta_tm", delta_tm[:], "dv")
        c.flush_taps()

        if c.checkpoint(f"h{head}scan"):
            return True

        attn_post(c, g, head, dict(
            v_tm=v_tm, fs_tm=fs_tm, fl_tm=fl_tm, delta_tm=delta_tm))

        if c.checkpoint(f"h{head}post"):
            return True

    fence(c, nc, f"hd{head}_fend")
    return False


def delta_prep(c, g, head, T):
    nc = g.nc
    q_n_fm, k_n_fm, k_tm = T["q_n_fm"], T["k_n_fm"], T["k_tm"]
    beta_bf = T["beta_bf"]
    Wt_sb, ALt_sb, iTb = T["Wt_sb"], T["ALt_sb"], T["iTb"]
    iT, iT2, Pt, Qt, Pt2, Qt2 = T["iT"], T["iT2"], T["Pt"], T["Qt"], T["Pt2"], T["Qt2"]
    b_cb, b_rb, qfac = T["b_cb"], T["b_rb"], T["qfac"]

    def sl(buf, i, w=32):
        j = i % 4
        return buf[32 * j:32 * j + 32, (i // 4) * w:(i // 4) * w + w]

    def mask_bc(col):
        return g.masks[:, col:col + 32].unsqueeze(1).broadcast_to([128, 16, 32])

    def r3(ap):
        return ap.rearrange("p (gr n) -> p gr n", n=32)

    with nc.Block(f"hd{head}_bfac") as blk:
        @blk.tensor
        def _(t):
            c.wait(t, "ac")
            c.pu_guard(t, 2)
            c.pu_guard(t, 3)
            for j in range(4):
                rhs = beta_bf[0:1, :].rearrange("o (gr n) -> o gr n", n=32)[:, j::4, :]
                mm = t.matmul(g.pu[2][32 * j:32 * j + 32, :], g.onesrow_bf[0:1, 0:32],
                              rhs, start=True, stop=True, tile_position=(0, 32 * j))
            c.inc(mm, "pe")
            for i in range(NCH):
                j = i % 4
                mm = t.matmul(g.pu[3][32 * j:32 * j + 32, (i // 4) * 32:(i // 4) * 32 + 32],
                              beta_bf[0:1, i * 32: i * 32 + 32], g.onesrow_bf[0:1, 0:32],
                              start=True, stop=True, tile_position=(0, 32 * j))
            c.inc(mm, "pe")

        v_b = c.cnt["pe"]

        @blk.vector
        def _(v, v_b=v_b):
            c.wait(v, "pe", v_b)
            v.tensor_copy(b_rb[:], g.pu[3][:])
            v.tensor_tensor(r3(b_cb[:]), r3(g.pu[2][:]), mask_bc(0), op=ALU.mult)
            c.inc(v.tensor_tensor(r3(qfac[:]), r3(g.pu[3][:]), mask_bc(32), op=ALU.mult), "dv")
            c.pu_read(2, "dv")
            c.pu_read(3, "dv")

    if c.checkpoint(f"h{head}bfac"):
        raise _Cut()
    with nc.Block(f"hd{head}_kkt") as blk:
        @blk.tensor
        def _(t):
            c.wait(t, "dv")
            c.pu_guard(t, 0)
            for i in range(NCH):
                j = i % 4
                for cc in range(2):
                    mm = t.matmul(sl(g.pu[0], i),
                                  k_n_fm[:, cc * L + i * 32: cc * L + i * 32 + 32],
                                  k_n_fm[:, cc * L + i * 32: cc * L + i * 32 + 32],
                                  start=(cc == 0), stop=(cc == 1), tile_position=(0, 32 * j))
            c.inc(mm, "pe")

        v_kkt = c.cnt["pe"]

        @blk.vector
        def _(v, v_kkt=v_kkt):
            c.wait(v, "pe", v_kkt)
            c.wait(v, "dv")
            v.tensor_tensor(Pt[:], g.pu[0][:], b_cb[:], op=ALU.mult)
            v.tensor_tensor(Qt[:], g.pu[0][:], qfac[:], op=ALU.mult)
            v.drain()
            c.inc(v.tensor_tensor(r3(iT[:]), r3(Pt[:]), mask_bc(96), op=ALU.add), "dv")
            c.pu_read(0, "dv")

    if c.checkpoint(f"h{head}kkt"):
        raise _Cut()
    P_cur, Q_cur, i_cur = Pt, Qt, iT
    P_nxt, Q_nxt, i_nxt = Pt2, Qt2, iT2
    for it in range(1, 5):
        with nc.Block(f"hd{head}_inv{it}") as blk:
            @blk.tensor
            def _(t, it=it, P_cur=P_cur, Q_cur=Q_cur):
                c.wait(t, "dv")
                c.pu_guard(t, 0)
                c.pu_guard(t, 1)
                if it < 4:
                    for i in range(NCH):
                        j = i % 4
                        mm = t.matmul(sl(g.pu[0], i), sl(Q_cur, i), sl(P_cur, i),
                                      start=True, stop=True, tile_position=(32 * j, 32 * j))
                    c.inc(mm, "pe")
                for i in range(NCH):
                    j = i % 4
                    mm = t.matmul(sl(g.pu[1], i), sl(P_cur, i), sl(Q_cur, i),
                                  start=True, stop=True, tile_position=(32 * j, 32 * j))
                c.inc(mm, "pe")

            v_sq = c.cnt["pe"]

            @blk.vector
            def _(v, it=it, P_nxt=P_nxt, Q_nxt=Q_nxt, v_sq=v_sq):
                c.wait(v, "pe", v_sq)
                if it < 4:
                    v.tensor_copy(P_nxt[:], g.pu[0][:])
                c.inc(v.tensor_copy(Q_nxt[:], g.pu[1][:]), "dv")
                c.pu_read(0, "dv")
                c.pu_read(1, "dv")

            v_pq = c.cnt["dv"]

            @blk.tensor
            def _(t, Q_nxt=Q_nxt, i_cur=i_cur, v_pq=v_pq):
                c.wait(t, "dv", v_pq)
                c.pu_guard(t, 2)
                for i in range(NCH):
                    j = i % 4
                    mm = t.matmul(sl(g.pu[2], i), sl(Q_nxt, i), sl(i_cur, i),
                                  start=True, stop=True, tile_position=(32 * j, 32 * j))
                c.inc(mm, "pe")

            v_up = c.cnt["pe"]

            @blk.vector
            def _(v, i_cur=i_cur, i_nxt=i_nxt, v_up=v_up):
                c.wait(v, "pe", v_up)
                c.wait(v, "dv")
                c.inc(v.tensor_tensor(i_nxt[:], g.pu[2][:], i_cur[:], op=ALU.add), "dv")
                c.pu_read(2, "dv")

        P_cur, P_nxt = P_nxt, P_cur
        Q_cur, Q_nxt = Q_nxt, Q_cur
        i_cur, i_nxt = i_nxt, i_cur

    if c.checkpoint(f"h{head}inv"):
        raise _Cut()
    with nc.Block(f"hd{head}_itb") as blk:
        @blk.vector
        def _(v, i_cur=i_cur):
            c.wait(v, "dv")
            c.inc(v.tensor_tensor(iTb[:], i_cur[:], b_rb[:], op=ALU.mult), "dv")

    v_itb = c.cnt["dv"]

    if c.checkpoint(f"h{head}itb"):
        raise _Cut()
    with nc.Block(f"hd{head}_alt") as blk:
        @blk.tensor
        def _(t):
            c.pu_guard(t, 0)
            for i in range(NCH):
                j = i % 4
                for cc in range(2):
                    mm = t.matmul(sl(g.pu[0], i),
                                  k_n_fm[:, cc * L + i * 32: cc * L + i * 32 + 32],
                                  q_n_fm[:, cc * L + i * 32: cc * L + i * 32 + 32],
                                  start=(cc == 0), stop=(cc == 1), tile_position=(0, 32 * j))
            c.inc(mm, "pe")

        v_al = c.cnt["pe"]

        @blk.vector
        def _(v, v_al=v_al):
            c.wait(v, "pe", v_al)
            c.inc(v.tensor_tensor(r3(ALt_sb[:]), r3(g.pu[0][:]), mask_bc(64), op=ALU.mult), "dv")
            c.pu_read(0, "dv")

    if c.checkpoint(f"h{head}alt"):
        raise _Cut()
    for hf in range(2):
        for sbi in range(4):
            u = 1 + (hf * 4 + sbi) % 2
            with nc.Block(f"hd{head}_wt{hf}_{sbi}") as blk:
                @blk.tensor
                def _(t, hf=hf, sbi=sbi, u=u, v_itb=v_itb):
                    if hf == 0 and sbi == 0:
                        c.wait(t, "dv", v_itb)
                    c.pu_guard(t, u)
                    prev_j = None
                    for ii in range(16):
                        i = sbi * 16 + ii
                        j = i % 4
                        if prev_j is not None and j != prev_j:
                            t.drain()
                        prev_j = j
                        mm = t.matmul(g.pu[u][:, ii * 32: ii * 32 + 32],
                                      k_tm[32 * j:32 * j + 32,
                                           (i // 4) * DK + hf * 128: (i // 4) * DK + hf * 128 + 128],
                                      sl(iTb, i),
                                      start=True, stop=True, tile_position=(32 * j, 0))
                    c.inc(mm, "pe")

                v_w = c.cnt["pe"]

                @blk.vector
                def _(v, hf=hf, sbi=sbi, u=u, v_w=v_w):
                    c.wait(v, "pe", v_w)
                    c.inc(v.tensor_scalar(
                        Wt_sb[:, hf * (NCH * 32) + sbi * 512: hf * (NCH * 32) + sbi * 512 + 512],
                        g.pu[u][:], -1.0, None, op0=ALU.mult), "dv")
                    c.pu_read(u, "dv")


def delta_scan(c, g, head, T):
    nc = g.nc
    q_n_fm, k_tm, v_tm = T["q_n_fm"], T["k_tm"], T["v_tm"]
    Wt_sb, ALt_sb, iTb = T["Wt_sb"], T["ALt_sb"], T["iTb"]
    S_sb, S_bf, u_sc, delta_tm = T["S_sb"], T["S_bf"], T["u_sc"], T["delta_tm"]

    with nc.Block(f"hd{head}_scan_init") as blk:
        @blk.vector
        def _(v):
            v.memset(S_sb[:], 0.0)
            c.inc(v.memset(S_bf[:], 0.0), "dv")

    pe_b = c.cnt["pe"]
    dv_b = c.cnt["dv"]
    with nc.Block(f"hd{head}_scan") as blk:
        @blk.tensor
        def _(t):
            c.pu_guard(t, 5)
            c.pu_guard(t, 3)
            c.pu_guard(t, 6)
            c.pu_guard(t, 7)
            for i in range(NCH):
                j = i % 4
                tt = i // 4
                uo = 6 + tt % 2
                c.wait(t, "dv", dv_b + 2 * i)
                t.matmul(g.pu[5][32 * j:32 * j + 32, 0:DK],
                         iTb[32 * j:32 * j + 32, (i // 4) * 32: (i // 4) * 32 + 32],
                         v_tm[32 * j:32 * j + 32, tt * DK: tt * DK + DK],
                         start=True, stop=False, tile_position=(32 * j, 32 * j))
                for hf in range(2):
                    mm = t.matmul(g.pu[5][32 * j:32 * j + 32, 0:DK],
                                  Wt_sb[:, hf * (NCH * 32) + i * 32: hf * (NCH * 32) + i * 32 + 32],
                                  S_bf[:, hf * DK: hf * DK + DK],
                                  start=False, stop=(hf == 1), tile_position=(0, 32 * j))
                c.inc(mm, "pe")
                c.wait(t, "dv", dv_b + 2 * i + 1)
                if j == 0 and tt >= 2:
                    c.wait(t, "dv", dv_b + 2 * (4 * (tt - 2) + 3) + 2)
                for hf in range(2):
                    t.matmul(g.pu[uo][32 * j:32 * j + 32, 0:DK],
                             q_n_fm[:, hf * L + i * 32: hf * L + i * 32 + 32],
                             S_bf[:, hf * DK: hf * DK + DK],
                             start=(hf == 0), stop=False, tile_position=(0, 32 * j))
                mm = t.matmul(g.pu[uo][32 * j:32 * j + 32, 0:DK],
                              ALt_sb[32 * j:32 * j + 32, tt * 32: tt * 32 + 32],
                              u_sc[32 * j:32 * j + 32, :],
                              start=False, stop=True, tile_position=(32 * j, 32 * j))
                c.inc(mm, "pe")
                t.drain()
                for hf in range(2):
                    mm = t.matmul(g.pu[3][:, hf * DK: hf * DK + DK],
                                  k_tm[32 * j:32 * j + 32, tt * DK + hf * 128: tt * DK + hf * 128 + 128],
                                  u_sc[32 * j:32 * j + 32, :],
                                  start=True, stop=True, tile_position=(32 * j, 0))
                c.inc(mm, "pe")

        @blk.vector
        def _(v):
            for i in range(NCH):
                j = i % 4
                tt = i // 4
                uo = 6 + tt % 2
                c.wait(v, "pe", pe_b + 3 * i + 1)
                c.inc(v.tensor_copy(u_sc[32 * j:32 * j + 32, :],
                                    g.pu[5][32 * j:32 * j + 32, 0:DK]), "dv")
                c.wait(v, "pe", pe_b + 3 * i + 3)
                v.tensor_tensor(S_sb[:], g.pu[3][:], S_sb[:], op=ALU.add)
                v.drain()
                cp = v.tensor_copy(S_bf[:], S_sb[:])
                if j == 3:
                    cp = v.tensor_copy(delta_tm[:, tt * DK: tt * DK + DK], g.pu[uo][:, 0:DK])
                c.inc(cp, "dv")

    v_sadd = c.cnt["dv"]
    c.pu_read(5, "dv", v_sadd)
    c.pu_read(3, "dv", v_sadd)


def attn_post(c, g, head, T):
    nc = g.nc
    v_tm, fs_tm, fl_tm, delta_tm = T["v_tm"], T["fs_tm"], T["fl_tm"], T["delta_tm"]
    tensors = (fs_tm, fl_tm, delta_tm, v_tm)

    with contextlib.ExitStack() as ps:
        def sb(name, shape, dtype):
            return ps.enter_context(nc.sbuf_tensor(f"h{head}p_{name}", shape, dtype))

        stats_fm = sb("stats_fm", [16, L], BF16)
        raw = sb("raw", [128, NT * 12], F32)
        stats_tm = sb("stats_tm", [128, NT * 16], F32)
        st_junk = sb("st_junk", [128, DK], BF16)
        st_scr = sb("st_scr", [128, 8], F32)
        h1h = sb("h1h", [128, NC * 512], BF16)
        w1m = sb("w1m", [128, NC * D], BF16)
        logits_sb = sb("logits_sb", [4, L], F32)
        p_scr = sb("p_scr", [128, NT * 4], F32)
        sm_scr = sb("sm_scr", [128, 8], F32)
        o_acc = sb("o_acc", [128, DK], F32)
        o_n = sb("o_n", [128, DK], BF16)

        v_scan = c.cnt["dv"]
        with nc.Block(f"hd{head}_stats_raw") as blk:
            @blk.vector
            def _(v):
                c.wait(v, "dv", v_scan)
                for ti in range(NT):
                    for x in range(4):
                        src = tensors[x][:, ti * DK: ti * DK + DK]
                        v.tensor_reduce(raw[:, ti * 12 + x: ti * 12 + x + 1], src,
                                        axis=AX.X, op=ALU.add)
                        mm = v.tensor_reduce(raw[:, ti * 12 + 8 + x: ti * 12 + 8 + x + 1], src,
                                             axis=AX.X, op=ALU.add, apply_absolute_value=True)
                c.inc(mm, "dv")

            @blk.scalar
            def _(s):
                c.wait(s, "dv", v_scan)
                for ti in range(NT):
                    for x in range(4):
                        src = tensors[x][:, ti * DK: ti * DK + DK]
                        if ti or x:
                            s.drain()
                        mm = s.activation(st_junk[:], src, AF.Square,
                                          accum_out=raw[:, ti * 12 + 4 + x: ti * 12 + 4 + x + 1])
                c.inc(mm, "ac")

        v_raw_v = c.cnt["dv"]
        v_raw_s = c.cnt["ac"]

        with nc.Block(f"hd{head}_stats_drv") as blk:
            @blk.vector
            def _(v):
                c.wait(v, "dv", v_raw_v)
                c.wait(v, "ac", v_raw_s)
                for ti in range(NT):
                    sums = raw[:, ti * 12: ti * 12 + 4]
                    sqs = raw[:, ti * 12 + 4: ti * 12 + 8]
                    abss = raw[:, ti * 12 + 8: ti * 12 + 12]
                    st = stats_tm[:, ti * 16: ti * 16 + 16].rearrange("p (x f) -> p f x", f=4)
                    v.tensor_scalar(st[:, 0:1, :], sums, 1.0 / DK, None, op0=ALU.mult)
                    mm = v.tensor_scalar(st[:, 2:3, :], abss, 1.0 / DK, None, op0=ALU.mult)
                    v.tensor_tensor(st_scr[:, 0:4], sums, sums, op=ALU.mult)
                    v.drain()
                    v.scalar_tensor_tensor(st_scr[:, 4:8], st_scr[:, 0:4], -1.0 / DK, sqs,
                                           op0=ALU.mult, op1=ALU.add)
                    v.drain()
                    mm = v.tensor_scalar(st[:, 1:2, :], st_scr[:, 4:8], 1.0 / DK, None,
                                         op0=ALU.mult)
                c.inc(mm, "dv")

            v_drv = c.cnt["dv"]

            @blk.scalar
            def _(s, v_drv=v_drv):
                c.wait(s, "dv", v_drv)
                for ti in range(NT):
                    sqs = raw[:, ti * 12 + 4: ti * 12 + 8]
                    st = stats_tm[:, ti * 16: ti * 16 + 16].rearrange("p (x f) -> p f x", f=4)
                    mm = s.activation(st[:, 3:4, :], sqs, AF.Sqrt)
                c.inc(mm, "ac")

        v_stats = c.cnt["ac"]

        for grp in range(4):
            u = grp % 2
            with nc.Block(f"hd{head}_sttr{grp}") as blk:
                @blk.tensor
                def _(t, grp=grp, u=u):
                    if grp == 0:
                        c.wait(t, "ac", v_stats)
                        c.wait(t, "dv")
                    c.pu_guard(t, u)
                    for q in range(4):
                        ti = grp * 4 + q
                        mm = t.transpose(g.pu[u][0:16, q * 128: q * 128 + 128],
                                         stats_tm[:, ti * 16: ti * 16 + 16], g.identf[:])
                    c.inc(mm, "pe")

                v_tr = c.cnt["pe"]

                @blk.vector
                def _(v, grp=grp, u=u, v_tr=v_tr):
                    c.wait(v, "pe", v_tr)
                    c.inc(v.tensor_copy(stats_fm[:, grp * 512: grp * 512 + 512],
                                        g.pu[u][0:16, :]), "dv")
                    c.pu_read(u, "dv")

        emit_prepass(c, g, 48, f"h{head}p")
        with nc.Block(f"hd{head}_w1ld") as blk:
            @blk.gpsimd
            def _(gg):
                c.wait(gg, "pe")
                c.wait(gg, "dv")
                c.inc(gg.dma_start(out=w1m[:],
                                   in_=g.W1_h[0:1024, :].rearrange("(c p) o -> p c o", p=128)),
                      "dwt", 16)

        v_w1m = c.cnt["dwt"]

        for ts4 in range(4):
            for c2 in range(NC):
                u = 2 + c2 % 2
                with nc.Block(f"hd{head}_gm{ts4}_{c2}") as blk:
                    @blk.tensor
                    def _(t, ts4=ts4, c2=c2, u=u):
                        c.pu_guard(t, u)
                        if ts4 == 0 and c2 == 0:
                            c.wait(t, "dwt", v_w1m)
                            c.wait(t, "dv")
                        t.matmul(g.pu[u][:], g.w1s_sb[:, c2 * 128: c2 * 128 + 128],
                                 stats_fm[:, ts4 * 512: ts4 * 512 + 512],
                                 start=True, stop=False)
                        for k in range(NC):
                            mm = t.matmul(g.pu[u][:],
                                          w1m[:, k * D + c2 * 128: k * D + c2 * 128 + 128],
                                          g.x_ln_fm[:, k * L + ts4 * 512: k * L + ts4 * 512 + 512],
                                          start=False, stop=(k == NC - 1))
                        c.inc(mm, "pe")

                    v_mm = c.cnt["pe"]

                    @blk.scalar
                    def _(s, ts4=ts4, c2=c2, u=u, v_mm=v_mm):
                        c.wait(s, "pe", v_mm)
                        c.inc(s.activation(h1h[:, c2 * 512: c2 * 512 + 512], g.pu[u][:],
                                           AF.Gelu, bias=g.b1_sb[:, c2: c2 + 1]), "ac")
                        c.pu_read(u, "ac")

            v_h1 = c.cnt["ac"]
            with nc.Block(f"hd{head}_gw2_{ts4}") as blk:
                @blk.tensor
                def _(t, ts4=ts4, v_h1=v_h1):
                    c.wait(t, "ac", v_h1)
                    c.pu_guard(t, 4)
                    for c2 in range(NC):
                        mm = t.matmul(g.pu[4][0:4, 0:512],
                                      g.w2_sb[:, c2 * 4: c2 * 4 + 4],
                                      h1h[:, c2 * 512: c2 * 512 + 512],
                                      start=(c2 == 0), stop=(c2 == NC - 1))
                    c.inc(mm, "pe")

                v_lg = c.cnt["pe"]

                @blk.vector
                def _(v, ts4=ts4, v_lg=v_lg):
                    c.wait(v, "pe", v_lg)
                    c.inc(v.tensor_scalar(logits_sb[0:4, ts4 * 512: ts4 * 512 + 512],
                                          g.pu[4][0:4, 0:512],
                                          g.gb_sb[:, head: head + 1],
                                          g.invtemp4[:, head: head + 1],
                                          op0=ALU.add, op1=ALU.mult), "dv")
                    c.pu_read(4, "dv")

        v_logits = c.cnt["dv"]
        with nc.Block(f"hd{head}_ptr") as blk:
            @blk.tensor
            def _(t):
                c.wait(t, "dv", v_logits)
                c.pu_guard(t, 5)
                for ti in range(NT):
                    mm = t.transpose(g.pu[5][:, ti * 4: ti * 4 + 4],
                                     logits_sb[0:4, ti * 128: ti * 128 + 128],
                                     g.identf[0:4, 0:4])
                c.inc(mm, "pe")

            v_tr = c.cnt["pe"]

            @blk.vector
            def _(v, v_tr=v_tr):
                c.wait(v, "pe", v_tr)
                c.inc(v.tensor_copy(p_scr[:], g.pu[5][:, 0:64]), "dv")
                c.pu_read(5, "dv")

        for ti in range(NT):
            with nc.Block(f"hd{head}_sm{ti}") as blk:
                @blk.vector
                def _(v, ti=ti):
                    if ti == 0:
                        c.wait(v, "dv")
                    c.inc(v.tensor_reduce(sm_scr[:, 0:1], p_scr[:, ti * 4: ti * 4 + 4],
                                          axis=AX.X, op=ALU.max, negate=True), "dv")

                v_mx = c.cnt["dv"]

                @blk.scalar
                def _(s, ti=ti, v_mx=v_mx):
                    c.wait(s, "dv", v_mx)
                    c.inc(s.activation(p_scr[:, ti * 4: ti * 4 + 4],
                                       p_scr[:, ti * 4: ti * 4 + 4],
                                       AF.Exp, bias=sm_scr[:, 0:1]), "ac")

                v_ex = c.cnt["ac"]

                @blk.vector
                def _(v, ti=ti, v_ex=v_ex):
                    c.wait(v, "ac", v_ex)
                    v.tensor_reduce(sm_scr[:, 1:2], p_scr[:, ti * 4: ti * 4 + 4],
                                    axis=AX.X, op=ALU.add)
                    v.drain()
                    v.reciprocal(sm_scr[:, 2:3], sm_scr[:, 1:2])
                    v.drain()
                    v.tensor_scalar(p_scr[:, ti * 4: ti * 4 + 4],
                                    p_scr[:, ti * 4: ti * 4 + 4],
                                    sm_scr[:, 2:3], None, op0=ALU.mult)
                    v.drain()
                    v.tensor_tensor(p_scr[:, ti * 4: ti * 4 + 4],
                                    p_scr[:, ti * 4: ti * 4 + 4], g.clampt[:], op=ALU.max)
                    v.drain()
                    v.tensor_reduce(sm_scr[:, 3:4], p_scr[:, ti * 4: ti * 4 + 4],
                                    axis=AX.X, op=ALU.add)
                    v.drain()
                    v.reciprocal(sm_scr[:, 4:5], sm_scr[:, 3:4])
                    v.drain()
                    v.tensor_scalar(p_scr[:, ti * 4: ti * 4 + 4],
                                    p_scr[:, ti * 4: ti * 4 + 4],
                                    sm_scr[:, 4:5], None, op0=ALU.mult)
                    v.drain()
                    c.inc(v.tensor_tensor(p_scr[:, ti * 4: ti * 4 + 1],
                                          p_scr[:, ti * 4: ti * 4 + 1],
                                          g.crb[:, head: head + 1], op=ALU.add), "dv")

        c.tap(f"h{head}_p", p_scr[:], "dv")
        c.flush_taps()

        for ti in range(NT):
            with nc.Block(f"hd{head}_mix{ti}") as blk:
                @blk.vector
                def _(v, ti=ti):
                    if ti == 0:
                        c.wait(v, "dv")
                    c.wait(v, "ac")
                    v.tensor_scalar(o_acc[:], fs_tm[:, ti * DK: ti * DK + DK],
                                    p_scr[:, ti * 4: ti * 4 + 1], None, op0=ALU.mult)
                    for x, src in ((1, fl_tm), (2, delta_tm), (3, v_tm)):
                        v.drain()
                        mm = v.scalar_tensor_tensor(o_acc[:], src[:, ti * DK: ti * DK + DK],
                                                    p_scr[:, ti * 4 + x: ti * 4 + x + 1],
                                                    o_acc[:], op0=ALU.mult, op1=ALU.add)
                    c.inc(mm, "dv")

                v_mix = c.cnt["dv"]

                @blk.scalar
                def _(s, ti=ti, v_mix=v_mix):
                    c.wait(s, "dv", v_mix)
                    c.inc(s.activation(st_junk[:], o_acc[:], AF.Square,
                                       accum_out=sm_scr[:, 5:6]), "ac")
                    c.wait(s, "ac")
                    c.inc(s.activation(sm_scr[:, 6:7], sm_scr[:, 5:6], AF.Sqrt,
                                       scale=1.0 / DK, bias=g.cb[:, 2:3]), "ac")

                v_rs = c.cnt["ac"]

                @blk.vector
                def _(v, ti=ti, v_rs=v_rs):
                    c.wait(v, "ac", v_rs)
                    c.wait(v, "pe")
                    c.wait(v, "dv")
                    v.reciprocal(sm_scr[:, 7:8], sm_scr[:, 6:7])
                    v.drain()
                    c.inc(v.scalar_tensor_tensor(o_n[:], o_acc[:], sm_scr[:, 7:8],
                                                 g.rmsw_bc[:], op0=ALU.mult, op1=ALU.mult), "dv")

                v_on = c.cnt["dv"]

                @blk.tensor
                def _(t, ti=ti, v_on=v_on):
                    c.wait(t, "dv", v_on)
                    u = ti % 2
                    c.pu_guard(t, u)
                    for cc in range(2):
                        mm = t.transpose(g.pub[u][:, cc * 128: cc * 128 + 128],
                                         o_n[:, cc * 128: cc * 128 + 128], g.identb[:])
                    c.inc(mm, "pe")

                v_otr = c.cnt["pe"]

                @blk.vector
                def _(v, ti=ti, v_otr=v_otr):
                    c.wait(v, "pe", v_otr)
                    u = ti % 2
                    for cc in range(2):
                        gc = 2 * head + cc
                        mm = v.tensor_copy(g.o_fm[:, gc * L + ti * 128: gc * L + ti * 128 + 128],
                                           g.pub[u][:, cc * 128: cc * 128 + 128])
                    c.inc(mm, "dv")
                    c.pu_read(u, "dv")



def wo_residual(c, g):
    nc = g.nc
    fence(c, nc, "wo_fence")
    with contextlib.ExitStack() as ws:
        def sb(name, shape, dtype):
            return ws.enter_context(nc.sbuf_tensor(f"wo_{name}", shape, dtype))

        wo_sb = sb("w", [128, NC * D], BF16)
        x_slab = [sb(f"xs{i}", [128, 4 * D], F32) for i in range(2)]
        ao_fm = sb("ao", [128, 512], BF16)

        with nc.Block("wo_ld") as blk:
            @blk.gpsimd
            def _(gg):
                c.inc(gg.dma_start(out=wo_sb[:],
                                   in_=g.Wo_h.rearrange("(c p) o -> p c o", p=128)), "dwt", 16)

        v_wo = c.cnt["dwt"]
        xs_used = []
        for ts4 in range(4):
            with nc.Block(f"wo_x{ts4}") as blk:
                @blk.sync
                def _(sy, ts4=ts4):
                    if ts4 >= 2:
                        c.wait(sy, "dv", xs_used[ts4 - 2])
                    c.inc(sy.dma_start(
                        out=x_slab[ts4 % 2][:],
                        in_=g.x_hbm[ts4 * 512:(ts4 + 1) * 512, :].rearrange(
                            "(t p) d -> p t d", p=128)), "dx0" if ts4 % 2 == 0 else "dx1", 16)

            wxs = "dx0" if ts4 % 2 == 0 else "dx1"
            v_xs = c.cnt[wxs]
            for co in range(NC):
                u = co % 2
                ub = 2 + co % 2
                with nc.Block(f"wo_{ts4}_{co}") as blk:
                    @blk.tensor
                    def _(t, ts4=ts4, co=co, u=u):
                        if ts4 == 0 and co == 0:
                            c.wait(t, "dwt", v_wo)
                            c.wait(t, "dv")
                        c.pu_guard(t, u)
                        for k in range(NC):
                            mm = t.matmul(g.pu[u][:],
                                          wo_sb[:, k * D + co * 128: k * D + co * 128 + 128],
                                          g.o_fm[:, k * L + ts4 * 512: k * L + ts4 * 512 + 512],
                                          start=(k == 0), stop=(k == NC - 1))
                        c.inc(mm, "pe")

                    v_mm = c.cnt["pe"]

                    @blk.scalar
                    def _(s, u=u, v_mm=v_mm):
                        c.wait(s, "pe", v_mm)
                        c.wait(s, "dv")
                        c.inc(s.activation(ao_fm[:], g.pu[u][:], AF.Copy), "ac")
                        c.pu_read(u, "ac")

                    v_ev = c.cnt["ac"]

                    @blk.tensor
                    def _(t, ub=ub, v_ev=v_ev):
                        c.wait(t, "ac", v_ev)
                        c.pu_guard(t, ub)
                        for q in range(4):
                            mm = t.transpose(g.pub[ub][:, q * 128: q * 128 + 128],
                                             ao_fm[:, q * 128: q * 128 + 128], g.identb[:])
                        c.inc(mm, "pe")

                    v_tr = c.cnt["pe"]

                    @blk.vector
                    def _(v, ts4=ts4, co=co, ub=ub, v_tr=v_tr, v_xs=v_xs, wxs=wxs):
                        c.wait(v, "pe", v_tr)
                        c.wait(v, wxs, v_xs)
                        for q in range(4):
                            tt = ts4 * 4 + q
                            mm = v.tensor_tensor(
                                g.x2_tm[:, tt * D + co * 128: tt * D + co * 128 + 128],
                                g.pub[ub][:, q * 128: q * 128 + 128],
                                x_slab[ts4 % 2][:, q * D + co * 128: q * D + co * 128 + 128],
                                op=ALU.add)
                        c.inc(mm, "dv")
                        c.pu_read(ub, "dv")

            xs_used.append(c.cnt["dv"])


def moe(c, g):
    nc = g.nc
    fence(c, nc, "moe_fence")
    with contextlib.ExitStack() as ms:
        def sb(name, shape, dtype):
            return ms.enter_context(nc.sbuf_tensor(f"moe_{name}", shape, dtype, side="right"))

        xl2_rot = [sb(f"xl{i}", [128, D], BF16) for i in range(2)]
        ln_scr = sb("lnscr", [128, 8], F32)
        sq_junk = sb("sqj", [128, D], BF16)
        xl2_fm = sb("xfm", [128, NC * SLAB], BF16)
        lg_sb = sb("lg", [8, SLAB], F32)
        lg_tm = sb("lgtm", [128, 8 * 8], F32)
        g_tm = sb("gtm", [128, 8 * 8], F32)
        m8 = sb("m8", [128, 8], F32)
        e1 = sb("e1", [128, 8], F32)
        cum = sb("cum", [128, 8], F32)
        pr2 = sb("pr2", [128, 8], F32)
        ones8 = sb("ones8", [128, 8], F32)
        gsc = sb("gsc", [128, 8], F32)
        acc = sb("acc", [128, 8 * D], F32)
        h_sb = sb("h", [128, NF * SLAB], BF16)
        NWB = 4
        wvb = [sb(f"wv{i}", [128, NC * 128], BF16) for i in range(NWB)]
        wb_ = [sb(f"wb{i}", [128, NC * 128], BF16) for i in range(NWB)]
        w1b = [sb(f"w1b{i}", [128, D], BF16) for i in range(NWB)]
        ysc = sb("ysc", [128, 512], BF16)
        s1b = sb("s1b", [128, D], F32)
        s1e = sb("s1e", [1, D], F32)

        with nc.Block("moe_init") as blk:
            @blk.vector
            def _(v):
                c.inc(v.memset(ones8[:], 1.0), "dv")

        v_store = None
        for slab in range(NSLAB):
            ln_done = []
            tr_done = []
            ev_done = []
            for q in range(8):
                tt = slab * 8 + q
                with nc.Block(f"m_ln{slab}_{q}") as blk:
                    @blk.vector
                    def _(v, q=q, tt=tt):
                        if q == 0 and slab > 0:
                            c.wait(v, "dout", v_store)
                        c.inc(v.tensor_reduce(ln_scr[:, 0:1],
                                              g.x2_tm[:, tt * D: tt * D + D],
                                              axis=AX.X, op=ALU.add), "dv")

                    @blk.scalar
                    def _(s, q=q, tt=tt):
                        c.inc(s.activation(sq_junk[:], g.x2_tm[:, tt * D: tt * D + D],
                                           AF.Square, accum_out=ln_scr[:, 1:2]), "ac")

                    v_sq = c.cnt["ac"]

                    @blk.vector
                    def _(v, v_sq=v_sq):
                        c.wait(v, "ac", v_sq)
                        c.wait(v, "dv")
                        v.tensor_tensor(ln_scr[:, 2:3], ln_scr[:, 0:1], ln_scr[:, 0:1], op=ALU.mult)
                        v.drain()
                        c.inc(v.scalar_tensor_tensor(ln_scr[:, 3:4], ln_scr[:, 2:3], -1.0 / D,
                                                     ln_scr[:, 1:2], op0=ALU.mult, op1=ALU.add), "dv")

                    v_var = c.cnt["dv"]

                    @blk.scalar
                    def _(s, v_var=v_var):
                        c.wait(s, "dv", v_var)
                        c.inc(s.activation(ln_scr[:, 4:5], ln_scr[:, 3:4], AF.Sqrt,
                                           scale=1.0 / D, bias=g.cb[:, 2:3]), "ac")

                    v_std = c.cnt["ac"]

                    @blk.vector
                    def _(v, q=q, tt=tt, v_std=v_std):
                        c.wait(v, "ac", v_std)
                        v.reciprocal(ln_scr[:, 5:6], ln_scr[:, 4:5])
                        v.tensor_scalar(ln_scr[:, 6:7], ln_scr[:, 0:1], -1.0 / D, None, op0=ALU.mult)
                        v.drain()
                        if q >= 2:
                            c.wait(v, "pe", tr_done[q - 2])
                        ln_done.append(c.inc(v.tensor_scalar(xl2_rot[q % 2][:],
                                                             g.x2_tm[:, tt * D: tt * D + D],
                                                             ln_scr[:, 6:7], ln_scr[:, 5:6],
                                                             op0=ALU.add, op1=ALU.mult), "dv"))

                    v_xl = c.cnt["dv"]

                    @blk.tensor
                    def _(t, q=q, v_xl=v_xl):
                        c.wait(t, "dv", v_xl)
                        u = q % 2
                        c.pu_guard(t, u)
                        c.pu_guard(t, 2 + u)
                        for k in range(NC):
                            pb = g.pub[u] if k < 4 else g.pub[2 + u]
                            mm = t.transpose(pb[:, (k % 4) * 128:(k % 4) * 128 + 128],
                                             xl2_rot[q % 2][:, k * 128:(k + 1) * 128], g.identb[:])
                        tr_done.append(c.inc(mm, "pe"))

                    v_tr = c.cnt["pe"]

                    @blk.vector
                    def _(v, q=q, v_tr=v_tr):
                        c.wait(v, "pe", v_tr)
                        u = q % 2
                        for k in range(NC):
                            pb = g.pub[u] if k < 4 else g.pub[2 + u]
                            mm = v.tensor_copy(
                                xl2_fm[:, k * SLAB + q * 128: k * SLAB + q * 128 + 128],
                                pb[:, (k % 4) * 128:(k % 4) * 128 + 128])
                        ev_done.append(c.inc(mm, "dv"))
                        c.pu_read(u, "dv")
                        c.pu_read(2 + u, "dv")

            v_xl2 = c.cnt["dv"]

            for th in range(2):
                u = 4 + th
                with nc.Block(f"m_rt{slab}_{th}") as blk:
                    @blk.tensor
                    def _(t, th=th, u=u):
                        if th == 0:
                            c.wait(t, "dv", v_xl2)
                        c.pu_guard(t, u)
                        for k in range(NC):
                            mm = t.matmul(g.pu[u][0:8, 0:512],
                                          g.rw_sb[:, k * 8: k * 8 + 8],
                                          xl2_fm[:, k * SLAB + th * 512: k * SLAB + th * 512 + 512],
                                          start=(k == 0), stop=(k == NC - 1))
                        c.inc(mm, "pe")

                    v_mm = c.cnt["pe"]

                    @blk.vector
                    def _(v, th=th, u=u, v_mm=v_mm):
                        c.wait(v, "pe", v_mm)
                        c.inc(v.tensor_copy(lg_sb[0:8, th * 512: th * 512 + 512],
                                            g.pu[u][0:8, 0:512]), "dv")
                        c.pu_read(u, "dv")

            v_lg = c.cnt["dv"]
            with nc.Block(f"m_rtt{slab}") as blk:
                @blk.tensor
                def _(t):
                    c.wait(t, "dv", v_lg)
                    c.pu_guard(t, 6)
                    for q in range(8):
                        mm = t.transpose(g.pu[6][:, q * 8: q * 8 + 8],
                                         lg_sb[0:8, q * 128: q * 128 + 128],
                                         g.identf[0:8, 0:8])
                    c.inc(mm, "pe")

                v_tr = c.cnt["pe"]

                @blk.vector
                def _(v, v_tr=v_tr):
                    c.wait(v, "pe", v_tr)
                    c.inc(v.tensor_copy(lg_tm[:], g.pu[6][:, 0:64]), "dv")
                    c.pu_read(6, "dv")

            for q in range(8):
                with nc.Block(f"m_g{slab}_{q}") as blk:
                    @blk.vector
                    def _(v, q=q):
                        if q == 0:
                            c.wait(v, "dv")
                        c.inc(v.tensor_reduce(gsc[:, 0:1], lg_tm[:, q * 8: q * 8 + 8],
                                              axis=AX.X, op=ALU.max, negate=True), "dv")

                    v_mx = c.cnt["dv"]

                    @blk.scalar
                    def _(s, q=q, v_mx=v_mx):
                        c.wait(s, "dv", v_mx)
                        c.inc(s.activation(lg_tm[:, q * 8: q * 8 + 8],
                                           lg_tm[:, q * 8: q * 8 + 8],
                                           AF.Exp, bias=gsc[:, 0:1]), "ac")

                    v_ex = c.cnt["ac"]

                    @blk.vector
                    def _(v, q=q, v_ex=v_ex):
                        c.wait(v, "ac", v_ex)
                        pr = lg_tm[:, q * 8: q * 8 + 8]
                        v.tensor_reduce(gsc[:, 1:2], pr, axis=AX.X, op=ALU.add)
                        v.drain()
                        v.reciprocal(gsc[:, 2:3], gsc[:, 1:2])
                        v.drain()
                        v.tensor_scalar(pr, pr, gsc[:, 2:3], None, op0=ALU.mult)
                        v.drain()
                        v.max(m8[:], pr)
                        v.drain()
                        v.tensor_scalar(e1[:], pr, m8[:, 0:1], None, op0=ALU.is_equal)
                        v.drain()
                        v.tensor_tensor_scan(cum[:], e1[:], ones8[:], 0.0,
                                             op0=ALU.add, op1=ALU.mult)
                        v.drain()
                        v.tensor_scalar(cum[:], cum[:], 1.0, None, op0=ALU.is_equal)
                        v.drain()
                        v.tensor_tensor(e1[:], e1[:], cum[:], op=ALU.mult)
                        v.drain()
                        v.tensor_scalar(g_tm[:, q * 8: q * 8 + 8], e1[:], m8[:, 0:1],
                                        None, op0=ALU.mult)
                        v.drain()
                        v.scalar_tensor_tensor(pr2[:], e1[:], -1e30, pr,
                                               op0=ALU.mult, op1=ALU.add)
                        v.drain()
                        v.tensor_scalar(e1[:], pr2[:], m8[:, 1:2], None, op0=ALU.is_equal)
                        v.drain()
                        v.tensor_tensor_scan(cum[:], e1[:], ones8[:], 0.0,
                                             op0=ALU.add, op1=ALU.mult)
                        v.drain()
                        v.tensor_scalar(cum[:], cum[:], 1.0, None, op0=ALU.is_equal)
                        v.drain()
                        v.tensor_tensor(e1[:], e1[:], cum[:], op=ALU.mult)
                        v.drain()
                        v.tensor_scalar(e1[:], e1[:], m8[:, 1:2], None, op0=ALU.mult)
                        v.drain()
                        c.inc(v.tensor_tensor(g_tm[:, q * 8: q * 8 + 8],
                                              g_tm[:, q * 8: q * 8 + 8], e1[:],
                                              op=ALU.add), "dv")

            v_gates = c.cnt["dv"]
            if slab == 0:
                c.tap("g_tm", g_tm[:], "dv")
                c.tap("xl2_fm", xl2_fm[:], "dv")
                c.flush_taps()

            for e in range(E):
                with nc.Block(f"m_s1{slab}_{e}") as blk:
                    @blk.sync
                    def _(sy, e=e):
                        c.wait(sy, "pe")
                        c.inc(sy.dma_start(out=s1e[:], in_=g.ms1_h[e:e + 1, :]), "ds1", 16)

                    v_s1d = c.cnt["ds1"]

                    @blk.tensor
                    def _(t, e=e, v_s1d=v_s1d):
                        if e == 0:
                            c.wait(t, "dv", v_gates)
                        c.wait(t, "ds1", v_s1d)
                        c.pu_guard(t, 6)
                        c.pu_guard(t, 7)
                        for hh in range(2):
                            mm = t.matmul(g.pu[6 + hh][:], g.onesrow[:],
                                          s1e[0:1, hh * 512: hh * 512 + 512],
                                          start=True, stop=True)
                        c.inc(mm, "pe")

                    v_s1 = c.cnt["pe"]

                    @blk.vector
                    def _(v, v_s1=v_s1):
                        c.wait(v, "pe", v_s1)
                        v.tensor_copy(s1b[:, 0:512], g.pu[6][:])
                        c.inc(v.tensor_copy(s1b[:, 512:1024], g.pu[7][:]), "dv")
                        c.pu_read(6, "dv")
                        c.pu_read(7, "dv")

                pe_b = c.cnt["pe"]; din_b = c.cnt["din"]
                ac_b = c.cnt["ac"]; dv_b = c.cnt["dv"]
                with nc.Block(f"m_vy{slab}_{e}") as blk:
                    wtarget = []

                    @blk.sync
                    def _(sy, e=e):
                        c.wait(sy, "dpp", g.pp_total)
                        for f in range(NF):
                            sl_ = f"dw{f % NWB}"
                            if f >= NWB - 1:
                                c.wait(sy, "pe", pe_b + 2 * (f - (NWB - 1) + 1))
                            row = (e * NF + f) * 128
                            c.inc(sy.dma_start(out=wvb[f % NWB][:],
                                               in_=g.wvt_d[row:row + 128, :]), sl_, 16)
                            c.inc(sy.dma_start(out=wb_[f % NWB][:],
                                               in_=g.wt_d[row:row + 128, :]), sl_, 16)
                            wtarget.append(c.cnt[sl_])

                    @blk.tensor
                    def _(t, e=e):
                        for f in range(NF):
                            c.wait(t, f"dw{f % NWB}", wtarget[f])
                            for th in range(2):
                                idx = 2 * f + th
                                par = idx % 2
                                ua, ub = par * 2, par * 2 + 1
                                if idx < 2:
                                    c.pu_guard(t, ua)
                                    c.pu_guard(t, ub)
                                else:
                                    c.wait(t, "ac", ac_b + idx - 1)
                                    c.wait(t, "dv", dv_b + idx - 1)
                                for k in range(NC):
                                    t.matmul(g.pu[ua][:],
                                             wvb[f % NWB][:, k * 128: k * 128 + 128],
                                             xl2_fm[:, k * SLAB + th * 512: k * SLAB + th * 512 + 512],
                                             start=(k == 0), stop=(k == NC - 1))
                                for k in range(NC):
                                    mm = t.matmul(g.pu[ub][:],
                                                  wb_[f % NWB][:, k * 128: k * 128 + 128],
                                                  xl2_fm[:, k * SLAB + th * 512: k * SLAB + th * 512 + 512],
                                                  start=(k == 0), stop=(k == NC - 1))
                                c.inc(mm, "pe")

                    @blk.scalar
                    def _(s, e=e):
                        for f in range(NF):
                            for th in range(2):
                                idx = 2 * f + th
                                par = idx % 2
                                ua, ub = par * 2, par * 2 + 1
                                c.wait(s, "pe", pe_b + idx + 1)
                                c.wait(s, "dv", dv_b + idx)
                                c.inc(s.activation(ysc[:], g.pu[ub][:], AF.Gelu_apprx_tanh,
                                                   scale=g.s_sb[:, e * NF + f: e * NF + f + 1]),
                                      "ac")

                    @blk.vector
                    def _(v, e=e):
                        for f in range(NF):
                            for th in range(2):
                                idx = 2 * f + th
                                par = idx % 2
                                ua = par * 2
                                c.wait(v, "ac", ac_b + idx + 1)
                                cp = v.scalar_tensor_tensor(
                                    h_sb[:, f * SLAB + th * 512: f * SLAB + th * 512 + 512],
                                    g.pu[ua][:],
                                    g.sv_sb[:, e * NF + f: e * NF + f + 1],
                                    ysc[:], op0=ALU.mult, op1=ALU.mult)
                                c.inc(cp, "dv")

                for u in range(4):
                    c.pu_read(u, "dv")

                for th in range(2):
                    pe_b = c.cnt["pe"]; din_b = c.cnt["din"]; dv_b = c.cnt["dv"]
                    v_s1b = c.cnt["dv"]
                    with nc.Block(f"m_o{slab}_{e}_{th}") as blk:
                        wtarget = []

                        @blk.sync
                        def _(sy, e=e, th=th):
                            c.wait(sy, "dpp", g.pp_total)
                            for f in range(NF):
                                sl_ = f"dw{f % NWB}"
                                if f >= NWB - 1:
                                    c.wait(sy, "pe", pe_b + (f - (NWB - 1) + 1))
                                row = (e * NF + f) * 128
                                c.inc(sy.dma_start(out=w1b[f % NWB][:],
                                                   in_=g.w1t_d[row:row + 128, :]), sl_, 16)
                                wtarget.append(c.cnt[sl_])

                        @blk.vector
                        def _(v, e=e, th=th):
                            for f in range(NF):
                                c.wait(v, f"dw{f % NWB}", wtarget[f])
                                if f >= NWB:
                                    c.wait(v, "pe", pe_b + (f - NWB + 1))
                                c.inc(v.tensor_tensor(w1b[f % NWB][:], w1b[f % NWB][:],
                                                      s1b[:], op=ALU.mult), "dv")

                        @blk.tensor
                        def _(t, e=e, th=th):
                            for u in range(8):
                                c.pu_guard(t, u)
                            for f in range(NF):
                                c.wait(t, "dv", dv_b + f + 1)
                                for dh in range(2):
                                    for tq in range(4):
                                        u = dh * 4 + tq
                                        mm = t.matmul(
                                            g.pu[u][:],
                                            h_sb[:, f * SLAB + th * 512 + tq * 128:
                                                 f * SLAB + th * 512 + tq * 128 + 128],
                                            w1b[f % NWB][:, dh * 512: dh * 512 + 512],
                                            start=(f == 0), stop=(f == NF - 1))
                                c.inc(mm, "pe")

                    v_o = c.cnt["pe"]
                    with nc.Block(f"m_oe{slab}_{e}_{th}") as blk:
                        @blk.vector
                        def _(v, e=e, th=th, slab=slab, v_o=v_o):
                            c.wait(v, "pe", v_o)
                            for dh in range(2):
                                for tq in range(4):
                                    u = dh * 4 + tq
                                    tl = th * 4 + tq
                                    tt = slab * 8 + tl
                                    gcol = g_tm[:, tl * 8 + e: tl * 8 + e + 1]
                                    if e == 0:
                                        base = g.x2_tm[:, tt * D + dh * 512: tt * D + dh * 512 + 512]
                                    else:
                                        base = acc[:, tl * D + dh * 512: tl * D + dh * 512 + 512]
                                    mm = v.scalar_tensor_tensor(
                                        acc[:, tl * D + dh * 512: tl * D + dh * 512 + 512],
                                        g.pu[u][:], gcol, base, op0=ALU.mult, op1=ALU.add)
                                    c.pu_read(u, "dv", c.cnt["dv"] + 1)
                            c.inc(mm, "dv")

            v_acc = c.cnt["dv"]
            with nc.Block(f"m_st{slab}") as blk:
                @blk.sync
                def _(sy, slab=slab, v_acc=v_acc):
                    c.wait(sy, "dv", v_acc)
                    c.inc(sy.dma_start(
                        out=g.out_hbm[slab * SLAB:(slab + 1) * SLAB, :].rearrange(
                            "(t p) d -> p t d", p=128),
                        in_=acc[:]), "dout", 16)

            v_store = c.cnt["dout"]


def _host_consts():
    p32 = np.arange(128)[:, None] % 32
    j32 = np.arange(32)[None, :]
    masks = np.zeros((128, 128), np.float32)
    masks[:, 0:32] = np.where(p32 < j32, -1.0, 0.0)
    masks[:, 32:64] = np.where(p32 > j32, -1.0, 0.0)
    masks[:, 64:96] = np.where(p32 <= j32, 1.0, 0.0)
    masks[:, 96:128] = np.where(p32 == j32, 1.0, 0.0)
    clamp = np.zeros((128, 4), np.float32)
    clamp[:, 0] = 0.05
    clamp[:, 1] = 0.05
    return {
        "ident_bf": np.eye(128, dtype=BF),
        "ident_f32": np.eye(128, dtype=np.float32),
        "masks_c": masks,
        "clamp": clamp,
        "ones_col": np.ones((128, 1), dtype=BF),
        "ones_row": np.ones((1, 128), dtype=np.float32),
        "ones_row_bf": np.ones((1, 128), dtype=BF),
        "cbias": np.tile(np.array([1.0, 1e-6, 1e-5, 0.0], np.float32), (128, 1)),
    }


def _chunked(w):
    cnum, j = w.shape[0] // 128, w.shape[1]
    return np.ascontiguousarray(w.reshape(cnum, 128, j).transpose(1, 0, 2).reshape(128, cnum * j))


def _expertcols(s):
    e, ff = s.shape
    return np.ascontiguousarray(s.reshape(e, ff // 128, 128).transpose(2, 0, 1).reshape(128, e * (ff // 128)))


def make_in_maps(inputs):
    f = np.float32
    a = lambda k: np.ascontiguousarray(np.asarray(inputs[k], dtype=f))
    shared = {
        "Wq": a("L0_attn_Wq"), "Wk": a("L0_attn_Wk"), "Wv": a("L0_attn_Wv"),
        "Wb": _chunked(a("L0_attn_Wb")),
        "conv_q": _chunked(a("L0_attn_conv_q")),
        "conv_k": _chunked(a("L0_attn_conv_k")),
        "conv_v": _chunked(a("L0_attn_conv_v")),
        "fir_long": _chunked(a("L0_attn_fir_long").reshape(D, 64)),
        "fir_short": _chunked(a("L0_attn_fir_short").reshape(D, 5)),
        "W1": a("L0_attn_W1"),
        "b1": np.ascontiguousarray(a("L0_attn_b1").reshape(8, 128).T),
        "W2": _chunked(a("L0_attn_W2")),
        "bbT": np.ascontiguousarray(a("L0_attn_base_bias").T),
        "b2c": a("L0_attn_b2").reshape(4, 1),
        "log_temp": a("L0_attn_log_temp").reshape(1, 4),
        "crl": a("L0_attn_conv_residual_logit").reshape(1, 4),
        "rms_w": a("L0_attn_rms_w").reshape(1, DK),
        "Wo": a("L0_attn_Wo"),
        "router_w": _chunked(a("L0_ffn_router_w")),
        "m_wv": a("L0_ffn_wv").reshape(E * D, FF),
        "m_sv": _expertcols(a("L0_ffn_sv").reshape(E, FF)),
        "m_w": a("L0_ffn_w").reshape(E * D, FF),
        "m_s": _expertcols(a("L0_ffn_s").reshape(E, FF)),
        "m_w1": a("L0_ffn_w1").reshape(E * FF, D),
        "m_s1": a("L0_ffn_s1").reshape(E, D),
    }
    shared.update(_host_consts())
    x_full = np.ascontiguousarray(np.asarray(inputs["L0_attn_x"], dtype=f))
    in_maps = []
    for core in range(8):
        m = dict(shared)
        m["x"] = np.ascontiguousarray(x_full[core])
        in_maps.append(m)
    return in_maps


def kernel(**inputs):
    in_maps = make_in_maps(inputs)
    nc = build_program()
    res = run_bass_kernel_spmd(nc, in_maps, core_ids=list(range(8)))
    out = np.stack([np.asarray(r["out"], dtype=np.float32) for r in res.results], axis=0)
    return out


# version 10
# speedup vs baseline: 1.1933x; 1.0060x over previous
import contextlib
import numpy as np
import ml_dtypes
import concourse.bass as bass
import concourse.mybir as mybir
from concourse.bass_utils import run_bass_kernel_spmd

F32 = mybir.dt.float32
BF16 = mybir.dt.bfloat16
AF = mybir.ActivationFunctionType
ALU = mybir.AluOpType
AX = mybir.AxisListType
BF = ml_dtypes.bfloat16

L = 2048
D = 1024
H = 4
DK = 256
NT = L // 128
NC = D // 128
NCH = L // 32
E = 8
FF = 4096
NF = FF // 128
SLAB = 1024
NSLAB = L // SLAB

DEBUG_TAPS = set()
STOP_AFTER = None


class _Cut(Exception):
    pass


class Ctx:
    def __init__(self, nc):
        self.nc = nc
        self.sem = {}
        self.cnt = {}
        self.dbg = []
        self.pu_last = {}
        self.stack = contextlib.ExitStack()

    def new_sem(self, name):
        self.sem[name] = self.stack.enter_context(self.nc.semaphore(name))
        self.cnt[name] = 0

    def inc(self, instr, name, n=1):
        self.cnt[name] += n
        instr.then_inc(self.sem[name], n)
        return self.cnt[name]

    def wait(self, eng, name, v=None):
        eng.wait_ge(self.sem[name], self.cnt[name] if v is None else v)

    def sb(self, name, shape, dtype):
        return self.stack.enter_context(self.nc.sbuf_tensor(name, shape, dtype))

    def pu_read(self, unit, semname, v=None):
        self.pu_last[unit] = (semname, self.cnt[semname] if v is None else v)

    def pu_guard(self, eng, unit):
        rec = self.pu_last.get(unit)
        if rec is not None:
            self.wait(eng, rec[0], rec[1])

    def tap(self, name, src_ap, semname):
        if name not in DEBUG_TAPS:
            return
        dram = self.nc.dram_tensor("dbg_" + name, list(src_ap.shape), src_ap.dtype,
                                   kind="ExternalOutput").ap()
        self.dbg.append((name, dram, src_ap, semname, self.cnt[semname]))

    def checkpoint(self, name):
        if STOP_AFTER != name:
            return False
        with self.nc.Block(f"cut_{name}") as blk:
            @blk.sync
            def _(sy):
                for s in ("pe", "dv", "ac", "gp", "din", "dwt", "dout"):
                    self.wait(sy, s)
        return True

    def flush_taps(self):
        if not self.dbg:
            return
        taps = self.dbg
        self.dbg = []
        self._tapn = getattr(self, "_tapn", 0) + 1
        with self.nc.Block(f"taps{self._tapn}") as blk:
            @blk.sync
            def _(sy):
                for name, dram, src, semname, v in taps:
                    sy.wait_ge(self.sem[semname], v)
                    self.inc(sy.dma_start(out=dram, in_=src), "dout", 16)


def build_program():
    nc = bass.Bass("TRN2", target_bir_lowering=False)
    c = Ctx(nc)

    def din(name, shape, dtype=F32):
        return nc.dram_tensor(name, list(shape), dtype, kind="ExternalInput").ap()

    x_hbm = din("x", (L, D))
    Wq_h = din("Wq", (D, D)); Wk_h = din("Wk", (D, D)); Wv_h = din("Wv", (D, D))
    Wb_h = din("Wb", (128, 32))
    cq_h = din("conv_q", (128, 32)); ck_h = din("conv_k", (128, 32)); cv_h = din("conv_v", (128, 32))
    firl_h = din("fir_long", (128, 512)); firs_h = din("fir_short", (128, 40))
    W1_h = din("W1", (1040, D)); b1_h = din("b1", (128, 8))
    W2_h = din("W2", (128, 32))
    bbT_h = din("bbT", (4, 4)); b2c_h = din("b2c", (4, 1))
    ltm_h = din("log_temp", (1, 4)); crl_h = din("crl", (1, 4))
    rmsw_h = din("rms_w", (1, DK))
    Wo_h = din("Wo", (D, D))
    rw_h = din("router_w", (128, 64))
    mwv_h = din("m_wv", (E * D, FF)); msv_h = din("m_sv", (128, E * NF))
    mw_h = din("m_w", (E * D, FF)); ms_h = din("m_s", (128, E * NF))
    mw1_h = din("m_w1", (E * FF, D)); ms1_h = din("m_s1", (E, D))
    identb_h = din("ident_bf", (128, 128), BF16)
    identf_h = din("ident_f32", (128, 128))
    masks_h = din("masks_c", (128, 128))
    clamp_h = din("clamp", (128, 4))
    onesc_h = din("ones_col", (128, 1), BF16)
    onesr_h = din("ones_row", (1, 128))
    onesrbf_h = din("ones_row_bf", (1, 128), BF16)
    cb_h = din("cbias", (128, 4))

    out_hbm = nc.dram_tensor("out", [L, D], F32, kind="ExternalOutput").ap()
    wvt_d = nc.dram_tensor("wvt_d", [E * NF * 128, NC * 128], BF16).ap()
    wt_d = nc.dram_tensor("wt_d", [E * NF * 128, NC * 128], BF16).ap()
    w1t_d = nc.dram_tensor("w1t_d", [E * NF * 128, D], BF16).ap()

    for s in ("din", "dwt", "dpp", "dout", "pe", "dv", "ac", "gp",
               "dw0", "dw1", "dw2", "dw3", "ds1", "dx0", "dx1"):
        c.new_sem(s)

    pu = [c.stack.enter_context(nc.psum_tensor(f"pu{i}", [128, 512], F32)) for i in range(8)]
    pub = [p[:].bitcast(BF16) for p in pu]

    identb = c.sb("identb", [128, 128], BF16)
    identf = c.sb("identf", [128, 128], F32)
    masks = c.sb("masks", [128, 128], F32)
    clampt = c.sb("clampt", [128, 4], F32)
    onescol = c.sb("onescol", [128, 1], BF16)
    onesrow = c.sb("onesrow", [1, 128], F32)
    cb = c.sb("cb", [128, 4], F32)
    cq_sb = c.sb("cq_sb", [128, 32], F32)
    ck_sb = c.sb("ck_sb", [128, 32], F32)
    cv_sb = c.sb("cv_sb", [128, 32], F32)
    firs_sb = c.sb("firs_sb", [128, 40], F32)
    firl_sb = c.sb("firl_sb", [128, 512], F32)
    b1_sb = c.sb("b1_sb", [128, 8], F32)
    bbT_sb = c.sb("bbT_sb", [4, 4], F32)
    b2c_sb = c.sb("b2c_sb", [4, 1], F32)
    gb_sb = c.sb("gb_sb", [4, 4], F32)
    ltm_sb = c.sb("ltm_sb", [1, 4], F32)
    crl_sb = c.sb("crl_sb", [1, 4], F32)
    rmsw_row = c.sb("rmsw_row", [1, DK], F32)
    w1s_sb = c.sb("w1s_sb", [16, D], BF16)
    w2_sb = c.sb("w2_sb", [128, 32], BF16)
    rw_sb = c.sb("rw_sb", [128, 64], BF16)
    wb_sb = c.sb("wb_sb", [128, 32], BF16)
    sv_sb = c.sb("sv_sb", [128, E * NF], F32)
    s_sb = c.sb("s_sb", [128, E * NF], F32)
    invtemp4 = c.sb("invtemp4", [4, 4], F32)
    crb = c.sb("crb", [128, 4], F32)
    rmsw_bc = c.sb("rmsw_bc", [128, DK], F32)
    onesrow_bf = c.sb("onesrow_bf", [1, 128], BF16)

    es_o = contextlib.ExitStack()
    es_x = contextlib.ExitStack()
    o_fm = es_o.enter_context(nc.sbuf_tensor("o_fm", [128, NC * L], BF16))
    x_ln_fm = es_x.enter_context(nc.sbuf_tensor("x_ln_fm", [128, NC * L], BF16))

    with nc.Block("ph0") as blk:
        @blk.sync
        def _(sy):
            for dst, src in [
                (identb[:], identb_h), (identf[:], identf_h), (masks[:], masks_h),
                (clampt[:], clamp_h), (onescol[:], onesc_h), (onesrow[:], onesr_h),
                (cb[:], cb_h),
                (cq_sb[:], cq_h), (ck_sb[:], ck_h), (cv_sb[:], cv_h),
                (firs_sb[:], firs_h), (firl_sb[:], firl_h), (b1_sb[:], b1_h),
                (bbT_sb[:], bbT_h), (b2c_sb[:], b2c_h),
                (ltm_sb[:], ltm_h), (crl_sb[:], crl_h), (rmsw_row[:], rmsw_h),
                (sv_sb[:], msv_h), (s_sb[:], ms_h),
                (onesrow_bf[:], onesrbf_h),
            ]:
                c.inc(sy.dma_start(out=dst, in_=src, allow_slow_non_contiguous=True),
                      "din", 16)

        @blk.gpsimd
        def _(g):
            c.inc(g.dma_start(out=w1s_sb[:], in_=W1_h[1024:1040, :]), "dwt", 16)
            c.inc(g.dma_start(out=w2_sb[:], in_=W2_h), "dwt", 16)
            c.inc(g.dma_start(out=rw_sb[:], in_=rw_h), "dwt", 16)
            c.inc(g.dma_start(out=wb_sb[:], in_=Wb_h), "dwt", 16)

    v_consts = c.cnt["din"]

    with nc.Block("ph0b") as blk:
        @blk.scalar
        def _(s):
            c.wait(s, "din", v_consts)
            c.inc(s.activation(ltm_sb[:], ltm_sb[:], AF.Exp), "ac")
            c.wait(s, "ac")
            c.inc(s.activation(ltm_sb[:], ltm_sb[:], AF.Ln, bias=cb[0:1, 0:1]), "ac")
            c.inc(s.activation(crl_sb[:], crl_sb[:], AF.Sigmoid), "ac")

        @blk.vector
        def _(v):
            c.wait(v, "ac")
            c.inc(v.tensor_scalar(ltm_sb[:], ltm_sb[:], 1e-4, None, op0=ALU.add), "dv")
            c.wait(v, "dv")
            c.inc(v.reciprocal(ltm_sb[:], ltm_sb[:]), "dv")
            c.wait(v, "dv")
            v.tensor_scalar(crl_sb[:], crl_sb[:], 0.5, None, op0=ALU.mult)
            c.inc(v.tensor_scalar(gb_sb[:], bbT_sb[:], b2c_sb[:, 0:1], None, op0=ALU.add), "dv")

        @blk.tensor
        def _(t):
            c.wait(t, "dv")
            t.matmul(pu[0][0:4, 0:4], onesrow[0:1, 0:4], ltm_sb[0:1, :], start=True, stop=True)
            t.matmul(pu[1][:, 0:4], onesrow[0:1, :], crl_sb[0:1, :], start=True, stop=True)
            c.inc(t.matmul(pu[2][:, 0:DK], onesrow[0:1, :], rmsw_row[0:1, :], start=True, stop=True), "pe")

        @blk.vector
        def _(v):
            c.wait(v, "pe")
            v.tensor_copy(invtemp4[:], pu[0][0:4, 0:4])
            v.tensor_copy(crb[:], pu[1][:, 0:4])
            c.inc(v.tensor_copy(rmsw_bc[:], pu[2][:, 0:DK]), "dv")
            for u in (0, 1, 2):
                c.pu_read(u, "dv")

    with contextlib.ExitStack() as ph1:
        x_rot = [ph1.enter_context(nc.sbuf_tensor(f"x_rot{i}", [128, D], F32)) for i in range(2)]
        xln_rot = [ph1.enter_context(nc.sbuf_tensor(f"xln_rot{i}", [128, D], BF16)) for i in range(2)]
        ln_scr = ph1.enter_context(nc.sbuf_tensor("ln_scr", [128, 8], F32))
        sq_junk = ph1.enter_context(nc.sbuf_tensor("sq_junk", [128, D], BF16))

        ln_done = []
        sq_done = []
        tr_done = []
        ev_done = []
        for ti in range(NT):
            with nc.Block(f"ln1_{ti}") as blk:
                @blk.sync
                def _(sy, ti=ti):
                    if ti >= 2:
                        c.wait(sy, "dv", ln_done[ti - 2])
                        c.wait(sy, "ac", sq_done[ti - 2])
                    c.inc(sy.dma_start(out=x_rot[ti % 2][:],
                                       in_=x_hbm[ti * 128:(ti + 1) * 128, :]),
                          "dx0" if ti % 2 == 0 else "dx1", 16)

                xs = "dx0" if ti % 2 == 0 else "dx1"
                v_x = c.cnt[xs]

                @blk.vector
                def _(v, ti=ti, v_x=v_x, xs=xs):
                    c.wait(v, xs, v_x)
                    c.inc(v.tensor_reduce(ln_scr[:, 0:1], x_rot[ti % 2][:], axis=AX.X, op=ALU.add), "dv")

                v_sum = c.cnt["dv"]

                @blk.scalar
                def _(s, ti=ti, v_x=v_x, xs=xs):
                    c.wait(s, xs, v_x)
                    sq_done.append(c.inc(s.activation(sq_junk[:], x_rot[ti % 2][:], AF.Square,
                                                      accum_out=ln_scr[:, 1:2]), "ac"))

                v_sq = c.cnt["ac"]

                @blk.vector
                def _(v, v_sq=v_sq, v_sum=v_sum):
                    c.wait(v, "ac", v_sq)
                    c.wait(v, "dv", v_sum)
                    v.tensor_tensor(ln_scr[:, 2:3], ln_scr[:, 0:1], ln_scr[:, 0:1], op=ALU.mult)
                    v.drain()
                    c.inc(v.scalar_tensor_tensor(ln_scr[:, 3:4], ln_scr[:, 2:3], -1.0 / D,
                                                 ln_scr[:, 1:2], op0=ALU.mult, op1=ALU.add), "dv")

                v_var = c.cnt["dv"]

                @blk.scalar
                def _(s, v_var=v_var):
                    c.wait(s, "dv", v_var)
                    c.inc(s.activation(ln_scr[:, 4:5], ln_scr[:, 3:4], AF.Sqrt,
                                       scale=1.0 / D, bias=cb[:, 2:3]), "ac")

                v_std = c.cnt["ac"]

                @blk.vector
                def _(v, ti=ti, v_std=v_std):
                    c.wait(v, "ac", v_std)
                    v.reciprocal(ln_scr[:, 5:6], ln_scr[:, 4:5])
                    v.tensor_scalar(ln_scr[:, 6:7], ln_scr[:, 0:1], -1.0 / D, None, op0=ALU.mult)
                    v.drain()
                    if ti >= 2:
                        c.wait(v, "pe", tr_done[ti - 2])
                    ln_done.append(c.inc(v.tensor_scalar(xln_rot[ti % 2][:], x_rot[ti % 2][:],
                                                         ln_scr[:, 6:7], ln_scr[:, 5:6],
                                                         op0=ALU.add, op1=ALU.mult), "dv"))

                v_xln = c.cnt["dv"]

                @blk.tensor
                def _(t, ti=ti, v_xln=v_xln):
                    c.wait(t, "dv", v_xln)
                    if ti >= 2:
                        c.wait(t, "dv", ev_done[ti - 2])
                    for k in range(NC):
                        pb = pub[ti % 2] if k < 4 else pub[2 + ti % 2]
                        mm = t.transpose(pb[:, (k % 4) * 128:(k % 4) * 128 + 128],
                                         xln_rot[ti % 2][:, k * 128:(k + 1) * 128], identb[:])
                    tr_done.append(c.inc(mm, "pe"))

                v_tr = c.cnt["pe"]

                @blk.vector
                def _(v, ti=ti, v_tr=v_tr):
                    c.wait(v, "pe", v_tr)
                    for k in range(NC):
                        pb = pub[ti % 2] if k < 4 else pub[2 + ti % 2]
                        cp = v.tensor_copy(x_ln_fm[:, k * L + ti * 128: k * L + ti * 128 + 128],
                                           pb[:, (k % 4) * 128:(k % 4) * 128 + 128])
                    ev_done.append(c.inc(cp, "dv"))

    v_xln_all = c.cnt["dv"]
    for u in range(4):
        c.pu_read(u, "dv", v_xln_all)
    c.tap("x_ln_fm", x_ln_fm[:], "dv")
    c.flush_taps()
    if c.checkpoint("ln1"):
        return nc

    class G:
        pass
    g = G()
    g.nc = nc
    _locals = dict(
        pu=pu, pub=pub, identb=identb, identf=identf, masks=masks, clampt=clampt,
        onescol=onescol, onesrow=onesrow, onesrow_bf=onesrow_bf, cq_sb=cq_sb,
        ck_sb=ck_sb, cv_sb=cv_sb, firs_sb=firs_sb, firl_sb=firl_sb, b1_sb=b1_sb,
        gb_sb=gb_sb, ltm_sb=ltm_sb, crl_sb=crl_sb, w1s_sb=w1s_sb, w2_sb=w2_sb,
        rw_sb=rw_sb, wb_sb=wb_sb, sv_sb=sv_sb, s_sb=s_sb, invtemp4=invtemp4,
        crb=crb, rmsw_bc=rmsw_bc, cb=cb, x_ln_fm=x_ln_fm, o_fm=o_fm, x_hbm=x_hbm,
        Wq_h=Wq_h, Wk_h=Wk_h, Wv_h=Wv_h, Wo_h=Wo_h, W1_h=W1_h, mwv_h=mwv_h,
        mw_h=mw_h, mw1_h=mw1_h, ms1_h=ms1_h, out_hbm=out_hbm)
    for nm, val in _locals.items():
        setattr(g, nm, val)

    g.wvt_d, g.wt_d, g.w1t_d = wvt_d, wt_d, w1t_d
    g.pp_done = {}
    g.pp_todo = [(e, f) for e in range(E) for f in range(NF)]

    g.x2_tm = c.stack.enter_context(nc.sbuf_tensor("x2_tm", [128, NT * D], BF16, side="right"))

    try:
        for head in range(H):
            if attn_head(c, g, head):
                return nc
    except _Cut:
        return nc

    es_x.close()

    wo_residual(c, g)
    c.tap("x2_tm", g.x2_tm[:], "dv")
    c.flush_taps()
    if c.checkpoint("wo"):
        return nc
    es_o.close()

    emit_prepass(c, g, len(g.pp_todo), "rest")
    g.pp_total = c.cnt["dpp"]

    moe(c, g)

    with nc.Block("fin") as blk:
        @blk.sync
        def _(sy):
            c.wait(sy, "dout")

    return nc


def emit_prepass(c, g, n, tag):
    todo = g.pp_todo[:n]
    del g.pp_todo[:n]
    if not todo:
        return
    with g.nc.Block(f"pp_{tag}") as blk:
        @blk.gpsimd
        def _(gg):
            for e, f in todo:
                row = (e * NF + f) * 128
                c.inc(gg.dma_start(
                    out=g.wvt_d[row:row + 128, :],
                    in_=g.mwv_h[e * D:(e + 1) * D, f * 128:(f + 1) * 128].rearrange(
                        "(cc p) w -> p cc w", p=128)), "dpp", 16)
                c.inc(gg.dma_start(
                    out=g.wt_d[row:row + 128, :],
                    in_=g.mw_h[e * D:(e + 1) * D, f * 128:(f + 1) * 128].rearrange(
                        "(cc p) w -> p cc w", p=128)), "dpp", 16)
                c.inc(gg.dma_start(
                    out=g.w1t_d[row:row + 128, :],
                    in_=g.mw1_h[e * FF + f * 128: e * FF + (f + 1) * 128, :]), "dpp", 16)
                g.pp_done[(e, f)] = c.cnt["dpp"]


def fence(c, nc, name):
    with nc.Block(name) as blk:
        @blk.tensor
        def _(t):
            for s in ("dv", "ac", "gp", "din", "dwt", "dout"):
                c.wait(t, s)

        @blk.vector
        def _(v):
            for s in ("pe", "ac", "gp", "din", "dwt", "dout"):
                c.wait(v, s)

        @blk.scalar
        def _(s_):
            for s in ("pe", "dv", "gp", "din", "dwt", "dout"):
                c.wait(s_, s)

        @blk.gpsimd
        def _(gg):
            for s in ("pe", "dv", "ac", "dout"):
                c.wait(gg, s)

        @blk.sync
        def _(sy):
            for s in ("pe", "dv", "ac", "gp", "dout", "dpp"):
                c.wait(sy, s)


def tr_fm_to_tm(c, g, name, src_fn, dst_fn, n=16, units=(6, 7)):
    ngrp = (n + 7) // 8
    pe_b = c.cnt["pe"]
    with g.nc.Block(f"{name}_tr") as blk:
        @blk.tensor
        def _(t):
            c.wait(t, "dv")
            c.wait(t, "ac")
            for grp in range(ngrp):
                lo, hi = grp * 8, min(grp * 8 + 8, n)
                u = units[grp % len(units)]
                c.pu_guard(t, u)
                pb = g.pub[u]
                for tt in range(lo, hi):
                    mm = t.transpose(pb[:, (tt - lo) * 128:(tt - lo + 1) * 128],
                                     src_fn(tt), g.identb[:])
                c.inc(mm, "pe")

        @blk.vector
        def _(v):
            for grp in range(ngrp):
                lo, hi = grp * 8, min(grp * 8 + 8, n)
                u = units[grp % len(units)]
                c.wait(v, "pe", pe_b + grp + 1)
                pb = g.pub[u]
                for tt in range(lo, hi):
                    cp = v.tensor_copy(dst_fn(tt), pb[:, (tt - lo) * 128:(tt - lo + 1) * 128])
                c.inc(cp, "dv")
                c.pu_read(u, "dv")
    return c.cnt["dv"]


def attn_head(c, g, head):
    nc = g.nc
    fence(c, nc, f"hd{head}_fence")

    with contextlib.ExitStack() as hs:
        def sb(name, shape, dtype):
            return hs.enter_context(nc.sbuf_tensor(f"h{head}_{name}", shape, dtype))

        q_n_fm = sb("q_n_fm", [128, 2 * L], BF16)
        k_tm = sb("k_tm", [128, NT * DK], BF16)
        v_tm = sb("v_tm", [128, NT * DK], BF16)
        fs_tm = sb("fs_tm", [128, NT * DK], BF16)
        fl_tm = sb("fl_tm", [128, NT * DK], BF16)
        delta_tm = sb("delta_tm", [128, NT * DK], BF16)
        Wt_sb = sb("Wt_sb", [128, 2 * NCH * 32], BF16)
        ALt_sb = sb("ALt_sb", [128, 512], BF16)
        iTb = sb("iTb", [128, 512], BF16)

        with contextlib.ExitStack() as pa:
            def sba(name, shape, dtype):
                return pa.enter_context(nc.sbuf_tensor(f"h{head}a_{name}", shape, dtype))

            wslab = sba("wslab", [128, NC * DK], BF16)
            P_buf = sba("P_buf", [128, 3 + L], BF16)
            cacc = sba("cacc", [128, L], BF16)
            beta_bf = sba("beta_bf", [1, L], BF16)
            rsq = sba("rsq", [1, 512], F32)
            k_n_fm = sba("k_n_fm", [128, 2 * L], BF16)
            iT = sba("iT", [128, 512], BF16)
            iT2 = sba("iT2", [128, 512], BF16)
            Pt = sba("Pt", [128, 512], BF16)
            Qt = sba("Qt", [128, 512], BF16)
            Pt2 = sba("Pt2", [128, 512], BF16)
            Qt2 = sba("Qt2", [128, 512], BF16)
            q2s = P_buf[:, 0:1024]
            b_cb = sba("b_cb", [128, 512], BF16)
            b_rb = sba("b_rb", [128, 512], BF16)
            qfac = sba("qfac", [128, 512], BF16)

            pe_b = c.cnt["pe"]
            ac_b = c.cnt["ac"]
            with nc.Block(f"hd{head}_bt") as blk:
                @blk.tensor
                def _(t):
                    for ts4 in range(4):
                        u = ts4 % 2
                        if ts4 < 2:
                            c.pu_guard(t, u)
                        else:
                            c.wait(t, "ac", ac_b + ts4 - 1)
                        for k in range(NC):
                            mm = t.matmul(g.pu[u][0:1, 0:512],
                                          g.wb_sb[:, k * 4 + head: k * 4 + head + 1],
                                          g.x_ln_fm[:, k * L + ts4 * 512: k * L + ts4 * 512 + 512],
                                          start=(k == 0), stop=(k == NC - 1))
                        c.inc(mm, "pe")

                @blk.scalar
                def _(s):
                    for ts4 in range(4):
                        u = ts4 % 2
                        c.wait(s, "pe", pe_b + ts4 + 1)
                        cp = s.activation(beta_bf[0:1, ts4 * 512: ts4 * 512 + 512],
                                          g.pu[u][0:1, 0:512], AF.Sigmoid)
                        c.inc(cp, "ac")
            for u in (0, 1):
                c.pu_read(u, "ac")

            def proj_part(part, csb, cc, dst, dst_off):
                gc = 2 * head + cc
                pe_b = c.cnt["pe"]
                ac_b = c.cnt["ac"]
                dv_b = c.cnt["dv"]
                with nc.Block(f"hd{head}_{part}{cc}") as blk:
                    @blk.tensor
                    def _(t):
                        for ts4 in range(4):
                            u = ts4 % 2
                            if ts4 < 2:
                                c.pu_guard(t, u)
                            else:
                                c.wait(t, "ac", ac_b + ts4 - 1)
                            for k in range(NC):
                                mm = t.matmul(g.pu[u][:],
                                              wslab[:, k * DK + cc * 128: k * DK + cc * 128 + 128],
                                              g.x_ln_fm[:, k * L + ts4 * 512: k * L + ts4 * 512 + 512],
                                              start=(k == 0), stop=(k == NC - 1))
                            c.inc(mm, "pe")

                    @blk.scalar
                    def _(s):
                        for ts4 in range(4):
                            u = ts4 % 2
                            c.wait(s, "pe", pe_b + ts4 + 1)
                            if ts4 == 0:
                                c.wait(s, "dv", dv_b)
                            cp = s.activation(P_buf[:, 3 + ts4 * 512: 3 + ts4 * 512 + 512],
                                              g.pu[u][:], AF.Copy)
                            c.inc(cp, "ac")
                            c.pu_read(u, "ac")
                        c.wait(s, "dv", dv_b + 1)
                        c.wait(s, "pe")
                        if part == "v":
                            cp = s.activation(v_fm[:, 63:63 + L], cacc[:], AF.Silu)
                        else:
                            cp = s.activation(dst[:, dst_off: dst_off + L], cacc[:], AF.Silu)
                        c.inc(cp, "ac")

                    @blk.vector
                    def _(v):
                        c.wait(v, "ac", ac_b + 4)
                        v.memset(P_buf[:, 0:3], 0.0)
                        v.drain()
                        v.tensor_scalar(cacc[:], P_buf[:, 0:L], csb[:, gc * 4: gc * 4 + 1],
                                        None, op0=ALU.mult)
                        for j in range(1, 4):
                            v.drain()
                            mm = v.scalar_tensor_tensor(cacc[:], P_buf[:, j: j + L],
                                                        csb[:, gc * 4 + j: gc * 4 + j + 1],
                                                        cacc[:], op0=ALU.mult, op1=ALU.add)
                        c.inc(mm, "dv")

            def l2norm(part, dst_fm):
                for ts4 in range(4):
                    ua, ub = 2 + ts4 % 2, 4 + ts4 % 2
                    with nc.Block(f"hd{head}_l2{part}_{ts4}") as blk:
                        @blk.scalar
                        def _(s, ts4=ts4):
                            c.wait(s, "ac")
                            c.wait(s, "dv")
                            c.wait(s, "pe")
                            for cc in range(2):
                                cp = s.activation(q2s[:, cc * 512: cc * 512 + 512],
                                                  dst_fm[:, cc * L + ts4 * 512: cc * L + ts4 * 512 + 512],
                                                  AF.Square)
                            c.inc(cp, "ac")

                        v_sq = c.cnt["ac"]

                        @blk.tensor
                        def _(t, ts4=ts4, ua=ua, v_sq=v_sq):
                            c.wait(t, "ac", v_sq)
                            c.pu_guard(t, ua)
                            for cc in range(2):
                                mm = t.matmul(g.pu[ua][0:1, 0:512], g.onescol[:],
                                              q2s[:, cc * 512: cc * 512 + 512],
                                              start=(cc == 0), stop=(cc == 1))
                            c.inc(mm, "pe")

                        v_ss = c.cnt["pe"]

                        @blk.scalar
                        def _(s, ts4=ts4, ua=ua, v_ss=v_ss):
                            c.wait(s, "pe", v_ss)
                            c.inc(s.activation(rsq[0:1, :], g.pu[ua][0:1, 0:512], AF.Ln,
                                               bias=g.cb[0:1, 1:2]), "ac")
                            c.wait(s, "ac")
                            cp = s.activation(rsq[0:1, :], rsq[0:1, :], AF.Exp, scale=-0.5)
                            c.inc(cp, "ac")
                            c.pu_read(ua, "ac")

                        v_rs = c.cnt["ac"]

                        @blk.tensor
                        def _(t, ts4=ts4, ub=ub, v_rs=v_rs):
                            c.wait(t, "ac", v_rs)
                            c.pu_guard(t, ub)
                            c.inc(t.matmul(g.pu[ub][:], g.onesrow[:], rsq[0:1, :],
                                           start=True, stop=True), "pe")

                        v_bc = c.cnt["pe"]

                        @blk.vector
                        def _(v, ts4=ts4, ub=ub, v_bc=v_bc):
                            c.wait(v, "pe", v_bc)
                            for cc in range(2):
                                sl = dst_fm[:, cc * L + ts4 * 512: cc * L + ts4 * 512 + 512]
                                cp = v.tensor_tensor(sl, sl, g.pu[ub][:], op=ALU.mult)
                            c.inc(cp, "dv")
                            c.pu_read(ub, "dv")

            def load_wslab(W_h, gate_pe):
                with nc.Block(f"hd{head}_wld{W_h.name}") as blk:
                    @blk.gpsimd
                    def _(gg):
                        c.wait(gg, "pe", gate_pe)
                        c.inc(gg.dma_start(
                            out=wslab[:],
                            in_=W_h[:, head * DK:(head + 1) * DK].rearrange(
                                "(c p) o -> p c o", p=128)), "dwt", 16)
                return c.cnt["dwt"]

            if c.checkpoint(f"h{head}beta"):
                return True
            emit_prepass(c, g, 48, f"h{head}q")
            v_w = load_wslab(g.Wq_h, 0)
            with nc.Block(f"hd{head}_qw") as blk:
                @blk.tensor
                def _(t, v_w=v_w):
                    c.wait(t, "dwt", v_w)
            proj_part("q", g.cq_sb, 0, q_n_fm, 0)
            proj_part("q", g.cq_sb, 1, q_n_fm, L)
            if c.checkpoint(f"h{head}qproj"):
                return True
            l2norm("q", q_n_fm)
            if c.checkpoint(f"h{head}ql2"):
                return True
            emit_prepass(c, g, 48, f"h{head}k")
            v_w = load_wslab(g.Wk_h, c.cnt["pe"])
            with nc.Block(f"hd{head}_kw") as blk:
                @blk.tensor
                def _(t, v_w=v_w):
                    c.wait(t, "dwt", v_w)
            proj_part("k", g.ck_sb, 0, k_n_fm, 0)
            proj_part("k", g.ck_sb, 1, k_n_fm, L)
            l2norm("k", k_n_fm)
            if c.checkpoint(f"h{head}kl2"):
                return True
            for cc in range(2):
                tr_fm_to_tm(c, g, f"hd{head}_ktm{cc}",
                            lambda tt, cc=cc: k_n_fm[:, cc * L + tt * 128: cc * L + tt * 128 + 128],
                            lambda tt, cc=cc: k_tm[:, tt * DK + cc * 128: tt * DK + cc * 128 + 128])

            c.tap(f"h{head}_q_n_fm", q_n_fm[:], "dv")
            c.tap(f"h{head}_k_n_fm", k_n_fm[:], "dv")
            c.flush_taps()
            if c.checkpoint(f"h{head}ktm"):
                return True

            try:
                _dp = delta_prep
            except NameError:
                pass
            delta_prep(c, g, head, dict(
                q_n_fm=q_n_fm, k_n_fm=k_n_fm, k_tm=k_tm, beta_bf=beta_bf,
                Wt_sb=Wt_sb, ALt_sb=ALt_sb, iTb=iTb,
                iT=iT, iT2=iT2, Pt=Pt, Qt=Qt, Pt2=Pt2, Qt2=Qt2,
                b_cb=b_cb, b_rb=b_rb, qfac=qfac))

            c.tap(f"h{head}_iTb", iTb[:], "dv")
            c.tap(f"h{head}_Wt_sb", Wt_sb[:], "dv")
            c.tap(f"h{head}_ALt_sb", ALt_sb[:], "dv")
            c.flush_taps()
            if c.checkpoint(f"h{head}a"):
                return True
            fence(c, nc, f"hd{head}_fa")

        with contextlib.ExitStack() as pb:
            def sbb(name, shape, dtype):
                return pb.enter_context(nc.sbuf_tensor(f"h{head}b_{name}", shape, dtype))

            wslab = sbb("wslab", [128, NC * DK], BF16)
            P_buf = sbb("P_buf", [128, 3 + L], BF16)
            cacc = sbb("cacc", [128, L], BF16)
            v_fm = sbb("v_fm", [128, 63 + L], BF16)
            flA = sbb("flA", [128, L], BF16)

            emit_prepass(c, g, 48, f"h{head}v")
            with nc.Block(f"hd{head}_vw") as blk:
                @blk.gpsimd
                def _(gg):
                    c.wait(gg, "pe")
                    c.inc(gg.dma_start(
                        out=wslab[:],
                        in_=g.Wv_h[:, head * DK:(head + 1) * DK].rearrange(
                            "(c p) o -> p c o", p=128)), "dwt", 16)

                v_w = c.cnt["dwt"]

                @blk.tensor
                def _(t, v_w=v_w):
                    c.wait(t, "dwt", v_w)

            for cc in range(2):
                gc = 2 * head + cc
                with nc.Block(f"hd{head}_v{cc}_pad") as blk:
                    @blk.vector
                    def _(v):
                        c.inc(v.memset(v_fm[:, 0:63], 0.0), "dv")

                for ts4 in range(4):
                    u = ts4 % 2
                    with nc.Block(f"hd{head}_v{cc}_mm{ts4}") as blk:
                        @blk.tensor
                        def _(t, ts4=ts4, u=u, cc=cc):
                            c.pu_guard(t, u)
                            for k in range(NC):
                                mm = t.matmul(g.pu[u][:],
                                              wslab[:, k * DK + cc * 128: k * DK + cc * 128 + 128],
                                              g.x_ln_fm[:, k * L + ts4 * 512: k * L + ts4 * 512 + 512],
                                              start=(k == 0), stop=(k == NC - 1))
                            c.inc(mm, "pe")

                        v_mm = c.cnt["pe"]

                        @blk.scalar
                        def _(s, ts4=ts4, u=u, v_mm=v_mm):
                            c.wait(s, "pe", v_mm)
                            if ts4 == 0:
                                c.wait(s, "dv")
                            cp = s.activation(P_buf[:, 3 + ts4 * 512: 3 + ts4 * 512 + 512],
                                              g.pu[u][:], AF.Copy)
                            c.inc(cp, "ac")
                            c.pu_read(u, "ac")

                v_evac = c.cnt["ac"]
                with nc.Block(f"hd{head}_v{cc}_conv") as blk:
                    @blk.vector
                    def _(v, gc=gc, v_evac=v_evac):
                        c.wait(v, "ac", v_evac)
                        v.memset(P_buf[:, 0:3], 0.0)
                        v.drain()
                        v.tensor_scalar(cacc[:], P_buf[:, 0:L], g.cv_sb[:, gc * 4: gc * 4 + 1],
                                        None, op0=ALU.mult)
                        for j in range(1, 4):
                            v.drain()
                            mm = v.scalar_tensor_tensor(cacc[:], P_buf[:, j: j + L],
                                                        g.cv_sb[:, gc * 4 + j: gc * 4 + j + 1],
                                                        cacc[:], op0=ALU.mult, op1=ALU.add)
                        c.inc(mm, "dv")

                    v_conv = c.cnt["dv"]

                    @blk.scalar
                    def _(s, v_conv=v_conv):
                        c.wait(s, "dv", v_conv)
                        c.wait(s, "pe")
                        c.inc(s.activation(v_fm[:, 63:63 + L], cacc[:], AF.Silu), "ac")

                tr_fm_to_tm(c, g, f"hd{head}_vtm{cc}",
                            lambda tt: v_fm[:, 63 + tt * 128: 63 + tt * 128 + 128],
                            lambda tt, cc=cc: v_tm[:, tt * DK + cc * 128: tt * DK + cc * 128 + 128])

                with nc.Block(f"hd{head}_fir{cc}") as blk:
                    @blk.vector
                    def _(v, gc=gc):
                        c.wait(v, "ac")
                        c.wait(v, "pe")
                        v.tensor_scalar(P_buf[:, 0:L], v_fm[:, 59:59 + L],
                                        g.firs_sb[:, gc * 5: gc * 5 + 1], None, op0=ALU.mult)
                        for j in range(1, 5):
                            v.drain()
                            mm = v.scalar_tensor_tensor(P_buf[:, 0:L], v_fm[:, 59 + j: 59 + j + L],
                                                        g.firs_sb[:, gc * 5 + j: gc * 5 + j + 1],
                                                        P_buf[:, 0:L], op0=ALU.mult, op1=ALU.add)
                        c.inc(mm, "dv")
                        for blk4 in range(4):
                            dst = flA if blk4 == 0 else cacc
                            j0 = blk4 * 16
                            v.drain()
                            v.tensor_scalar(dst[:], v_fm[:, j0: j0 + L],
                                            g.firl_sb[:, gc * 64 + j0: gc * 64 + j0 + 1],
                                            None, op0=ALU.mult)
                            for j in range(j0 + 1, j0 + 16):
                                v.drain()
                                v.scalar_tensor_tensor(dst[:], v_fm[:, j: j + L],
                                                       g.firl_sb[:, gc * 64 + j: gc * 64 + j + 1],
                                                       dst[:], op0=ALU.mult, op1=ALU.add)
                            if blk4 > 0:
                                v.drain()
                                mm = v.tensor_tensor(flA[:], flA[:], cacc[:], op=ALU.add)
                        c.inc(mm, "dv")

                tr_fm_to_tm(c, g, f"hd{head}_fstm{cc}",
                            lambda tt: P_buf[:, tt * 128: tt * 128 + 128],
                            lambda tt, cc=cc: fs_tm[:, tt * DK + cc * 128: tt * DK + cc * 128 + 128])
                tr_fm_to_tm(c, g, f"hd{head}_fltm{cc}",
                            lambda tt: flA[:, tt * 128: tt * 128 + 128],
                            lambda tt, cc=cc: fl_tm[:, tt * DK + cc * 128: tt * DK + cc * 128 + 128])

            c.tap(f"h{head}_v_tm", v_tm[:], "dv")
            c.tap(f"h{head}_fs_tm", fs_tm[:], "dv")
            c.tap(f"h{head}_fl_tm", fl_tm[:], "dv")
            c.flush_taps()
            if c.checkpoint(f"h{head}b"):
                return True
            fence(c, nc, f"hd{head}_fb")

        with contextlib.ExitStack() as sc_s:
            S_sb = sc_s.enter_context(nc.sbuf_tensor(f"h{head}_S", [128, 512], F32))
            S_bf = sc_s.enter_context(nc.sbuf_tensor(f"h{head}_Sbf", [128, 512], BF16))
            u_sc = sc_s.enter_context(nc.sbuf_tensor(f"h{head}_usc", [128, DK], BF16))
            delta_scan(c, g, head, dict(
                q_n_fm=q_n_fm, k_tm=k_tm, v_tm=v_tm, Wt_sb=Wt_sb, ALt_sb=ALt_sb,
                iTb=iTb, S_sb=S_sb, S_bf=S_bf, u_sc=u_sc, delta_tm=delta_tm))
            fence(c, nc, f"hd{head}_fs2")

        c.tap(f"h{head}_delta_tm", delta_tm[:], "dv")
        c.flush_taps()

        if c.checkpoint(f"h{head}scan"):
            return True

        attn_post(c, g, head, dict(
            v_tm=v_tm, fs_tm=fs_tm, fl_tm=fl_tm, delta_tm=delta_tm))

        if c.checkpoint(f"h{head}post"):
            return True

    fence(c, nc, f"hd{head}_fend")
    return False


def delta_prep(c, g, head, T):
    nc = g.nc
    q_n_fm, k_n_fm, k_tm = T["q_n_fm"], T["k_n_fm"], T["k_tm"]
    beta_bf = T["beta_bf"]
    Wt_sb, ALt_sb, iTb = T["Wt_sb"], T["ALt_sb"], T["iTb"]
    iT, iT2, Pt, Qt, Pt2, Qt2 = T["iT"], T["iT2"], T["Pt"], T["Qt"], T["Pt2"], T["Qt2"]
    b_cb, b_rb, qfac = T["b_cb"], T["b_rb"], T["qfac"]

    def sl(buf, i, w=32):
        j = i % 4
        return buf[32 * j:32 * j + 32, (i // 4) * w:(i // 4) * w + w]

    def mask_bc(col):
        return g.masks[:, col:col + 32].unsqueeze(1).broadcast_to([128, 16, 32])

    def r3(ap):
        return ap.rearrange("p (gr n) -> p gr n", n=32)

    with nc.Block(f"hd{head}_bfac") as blk:
        @blk.tensor
        def _(t):
            c.wait(t, "ac")
            c.pu_guard(t, 2)
            c.pu_guard(t, 3)
            for j in range(4):
                rhs = beta_bf[0:1, :].rearrange("o (gr n) -> o gr n", n=32)[:, j::4, :]
                mm = t.matmul(g.pu[2][32 * j:32 * j + 32, :], g.onesrow_bf[0:1, 0:32],
                              rhs, start=True, stop=True, tile_position=(0, 32 * j))
            c.inc(mm, "pe")
            for i in range(NCH):
                j = i % 4
                mm = t.matmul(g.pu[3][32 * j:32 * j + 32, (i // 4) * 32:(i // 4) * 32 + 32],
                              beta_bf[0:1, i * 32: i * 32 + 32], g.onesrow_bf[0:1, 0:32],
                              start=True, stop=True, tile_position=(0, 32 * j))
            c.inc(mm, "pe")

        v_b = c.cnt["pe"]

        @blk.vector
        def _(v, v_b=v_b):
            c.wait(v, "pe", v_b)
            v.tensor_copy(b_rb[:], g.pu[3][:])
            v.tensor_tensor(r3(b_cb[:]), r3(g.pu[2][:]), mask_bc(0), op=ALU.mult)
            c.inc(v.tensor_tensor(r3(qfac[:]), r3(g.pu[3][:]), mask_bc(32), op=ALU.mult), "dv")
            c.pu_read(2, "dv")
            c.pu_read(3, "dv")

    if c.checkpoint(f"h{head}bfac"):
        raise _Cut()
    with nc.Block(f"hd{head}_kkt") as blk:
        @blk.tensor
        def _(t):
            c.wait(t, "dv")
            c.pu_guard(t, 0)
            for i in range(NCH):
                j = i % 4
                for cc in range(2):
                    mm = t.matmul(sl(g.pu[0], i),
                                  k_n_fm[:, cc * L + i * 32: cc * L + i * 32 + 32],
                                  k_n_fm[:, cc * L + i * 32: cc * L + i * 32 + 32],
                                  start=(cc == 0), stop=(cc == 1), tile_position=(0, 32 * j))
            c.inc(mm, "pe")

        v_kkt = c.cnt["pe"]

        @blk.vector
        def _(v, v_kkt=v_kkt):
            c.wait(v, "pe", v_kkt)
            c.wait(v, "dv")
            v.tensor_tensor(Pt[:], g.pu[0][:], b_cb[:], op=ALU.mult)
            v.tensor_tensor(Qt[:], g.pu[0][:], qfac[:], op=ALU.mult)
            v.drain()
            c.inc(v.tensor_tensor(r3(iT[:]), r3(Pt[:]), mask_bc(96), op=ALU.add), "dv")
            c.pu_read(0, "dv")

    if c.checkpoint(f"h{head}kkt"):
        raise _Cut()
    P_cur, Q_cur, i_cur = Pt, Qt, iT
    P_nxt, Q_nxt, i_nxt = Pt2, Qt2, iT2
    for it in range(1, 5):
        with nc.Block(f"hd{head}_inv{it}") as blk:
            @blk.tensor
            def _(t, it=it, P_cur=P_cur, Q_cur=Q_cur):
                c.wait(t, "dv")
                c.pu_guard(t, 0)
                c.pu_guard(t, 1)
                if it < 4:
                    for i in range(NCH):
                        j = i % 4
                        mm = t.matmul(sl(g.pu[0], i), sl(Q_cur, i), sl(P_cur, i),
                                      start=True, stop=True, tile_position=(32 * j, 32 * j))
                    c.inc(mm, "pe")
                for i in range(NCH):
                    j = i % 4
                    mm = t.matmul(sl(g.pu[1], i), sl(P_cur, i), sl(Q_cur, i),
                                  start=True, stop=True, tile_position=(32 * j, 32 * j))
                c.inc(mm, "pe")

            v_sq = c.cnt["pe"]

            @blk.vector
            def _(v, it=it, P_nxt=P_nxt, Q_nxt=Q_nxt, v_sq=v_sq):
                c.wait(v, "pe", v_sq)
                if it < 4:
                    v.tensor_copy(P_nxt[:], g.pu[0][:])
                c.inc(v.tensor_copy(Q_nxt[:], g.pu[1][:]), "dv")
                c.pu_read(0, "dv")
                c.pu_read(1, "dv")

            v_pq = c.cnt["dv"]

            @blk.tensor
            def _(t, Q_nxt=Q_nxt, i_cur=i_cur, v_pq=v_pq):
                c.wait(t, "dv", v_pq)
                c.pu_guard(t, 2)
                for i in range(NCH):
                    j = i % 4
                    mm = t.matmul(sl(g.pu[2], i), sl(Q_nxt, i), sl(i_cur, i),
                                  start=True, stop=True, tile_position=(32 * j, 32 * j))
                c.inc(mm, "pe")

            v_up = c.cnt["pe"]

            @blk.vector
            def _(v, i_cur=i_cur, i_nxt=i_nxt, v_up=v_up):
                c.wait(v, "pe", v_up)
                c.wait(v, "dv")
                c.inc(v.tensor_tensor(i_nxt[:], g.pu[2][:], i_cur[:], op=ALU.add), "dv")
                c.pu_read(2, "dv")

        P_cur, P_nxt = P_nxt, P_cur
        Q_cur, Q_nxt = Q_nxt, Q_cur
        i_cur, i_nxt = i_nxt, i_cur

    if c.checkpoint(f"h{head}inv"):
        raise _Cut()
    with nc.Block(f"hd{head}_itb") as blk:
        @blk.vector
        def _(v, i_cur=i_cur):
            c.wait(v, "dv")
            c.inc(v.tensor_tensor(iTb[:], i_cur[:], b_rb[:], op=ALU.mult), "dv")

    v_itb = c.cnt["dv"]

    if c.checkpoint(f"h{head}itb"):
        raise _Cut()
    with nc.Block(f"hd{head}_alt") as blk:
        @blk.tensor
        def _(t):
            c.pu_guard(t, 0)
            for i in range(NCH):
                j = i % 4
                for cc in range(2):
                    mm = t.matmul(sl(g.pu[0], i),
                                  k_n_fm[:, cc * L + i * 32: cc * L + i * 32 + 32],
                                  q_n_fm[:, cc * L + i * 32: cc * L + i * 32 + 32],
                                  start=(cc == 0), stop=(cc == 1), tile_position=(0, 32 * j))
            c.inc(mm, "pe")

        v_al = c.cnt["pe"]

        @blk.vector
        def _(v, v_al=v_al):
            c.wait(v, "pe", v_al)
            c.inc(v.tensor_tensor(r3(ALt_sb[:]), r3(g.pu[0][:]), mask_bc(64), op=ALU.mult), "dv")
            c.pu_read(0, "dv")

    if c.checkpoint(f"h{head}alt"):
        raise _Cut()
    for hf in range(2):
        for sbi in range(4):
            u = 1 + (hf * 4 + sbi) % 2
            with nc.Block(f"hd{head}_wt{hf}_{sbi}") as blk:
                @blk.tensor
                def _(t, hf=hf, sbi=sbi, u=u, v_itb=v_itb):
                    if hf == 0 and sbi == 0:
                        c.wait(t, "dv", v_itb)
                    c.pu_guard(t, u)
                    prev_j = None
                    for ii in range(16):
                        i = sbi * 16 + ii
                        j = i % 4
                        if prev_j is not None and j != prev_j:
                            t.drain()
                        prev_j = j
                        mm = t.matmul(g.pu[u][:, ii * 32: ii * 32 + 32],
                                      k_tm[32 * j:32 * j + 32,
                                           (i // 4) * DK + hf * 128: (i // 4) * DK + hf * 128 + 128],
                                      sl(iTb, i),
                                      start=True, stop=True, tile_position=(32 * j, 0))
                    c.inc(mm, "pe")

                v_w = c.cnt["pe"]

                @blk.vector
                def _(v, hf=hf, sbi=sbi, u=u, v_w=v_w):
                    c.wait(v, "pe", v_w)
                    c.inc(v.tensor_scalar(
                        Wt_sb[:, hf * (NCH * 32) + sbi * 512: hf * (NCH * 32) + sbi * 512 + 512],
                        g.pu[u][:], -1.0, None, op0=ALU.mult), "dv")
                    c.pu_read(u, "dv")


def delta_scan(c, g, head, T):
    nc = g.nc
    q_n_fm, k_tm, v_tm = T["q_n_fm"], T["k_tm"], T["v_tm"]
    Wt_sb, ALt_sb, iTb = T["Wt_sb"], T["ALt_sb"], T["iTb"]
    S_sb, S_bf, u_sc, delta_tm = T["S_sb"], T["S_bf"], T["u_sc"], T["delta_tm"]

    with nc.Block(f"hd{head}_scan_init") as blk:
        @blk.vector
        def _(v):
            v.memset(S_sb[:], 0.0)
            c.inc(v.memset(S_bf[:], 0.0), "dv")

    pe_b = c.cnt["pe"]
    dv_b = c.cnt["dv"]
    with nc.Block(f"hd{head}_scan") as blk:
        @blk.tensor
        def _(t):
            c.pu_guard(t, 5)
            c.pu_guard(t, 3)
            c.pu_guard(t, 6)
            c.pu_guard(t, 7)
            for i in range(NCH):
                j = i % 4
                tt = i // 4
                uo = 6 + tt % 2
                c.wait(t, "dv", dv_b + 2 * i)
                t.matmul(g.pu[5][32 * j:32 * j + 32, 0:DK],
                         iTb[32 * j:32 * j + 32, (i // 4) * 32: (i // 4) * 32 + 32],
                         v_tm[32 * j:32 * j + 32, tt * DK: tt * DK + DK],
                         start=True, stop=False, tile_position=(32 * j, 32 * j))
                for hf in range(2):
                    mm = t.matmul(g.pu[5][32 * j:32 * j + 32, 0:DK],
                                  Wt_sb[:, hf * (NCH * 32) + i * 32: hf * (NCH * 32) + i * 32 + 32],
                                  S_bf[:, hf * DK: hf * DK + DK],
                                  start=False, stop=(hf == 1), tile_position=(0, 32 * j))
                c.inc(mm, "pe")
                c.wait(t, "dv", dv_b + 2 * i + 1)
                if j == 0 and tt >= 2:
                    c.wait(t, "dv", dv_b + 2 * (4 * (tt - 2) + 3) + 2)
                for hf in range(2):
                    t.matmul(g.pu[uo][32 * j:32 * j + 32, 0:DK],
                             q_n_fm[:, hf * L + i * 32: hf * L + i * 32 + 32],
                             S_bf[:, hf * DK: hf * DK + DK],
                             start=(hf == 0), stop=False, tile_position=(0, 32 * j))
                mm = t.matmul(g.pu[uo][32 * j:32 * j + 32, 0:DK],
                              ALt_sb[32 * j:32 * j + 32, tt * 32: tt * 32 + 32],
                              u_sc[32 * j:32 * j + 32, :],
                              start=False, stop=True, tile_position=(32 * j, 32 * j))
                c.inc(mm, "pe")
                t.drain()
                for hf in range(2):
                    mm = t.matmul(g.pu[3][:, hf * DK: hf * DK + DK],
                                  k_tm[32 * j:32 * j + 32, tt * DK + hf * 128: tt * DK + hf * 128 + 128],
                                  u_sc[32 * j:32 * j + 32, :],
                                  start=True, stop=True, tile_position=(32 * j, 0))
                c.inc(mm, "pe")

        @blk.vector
        def _(v):
            for i in range(NCH):
                j = i % 4
                tt = i // 4
                uo = 6 + tt % 2
                c.wait(v, "pe", pe_b + 3 * i + 1)
                c.inc(v.tensor_copy(u_sc[32 * j:32 * j + 32, :],
                                    g.pu[5][32 * j:32 * j + 32, 0:DK]), "dv")
                c.wait(v, "pe", pe_b + 3 * i + 3)
                v.tensor_tensor(S_sb[:], g.pu[3][:], S_sb[:], op=ALU.add)
                v.drain()
                cp = v.tensor_copy(S_bf[:], S_sb[:])
                if j == 3:
                    cp = v.tensor_copy(delta_tm[:, tt * DK: tt * DK + DK], g.pu[uo][:, 0:DK])
                c.inc(cp, "dv")

    v_sadd = c.cnt["dv"]
    c.pu_read(5, "dv", v_sadd)
    c.pu_read(3, "dv", v_sadd)


def attn_post(c, g, head, T):
    nc = g.nc
    v_tm, fs_tm, fl_tm, delta_tm = T["v_tm"], T["fs_tm"], T["fl_tm"], T["delta_tm"]
    tensors = (fs_tm, fl_tm, delta_tm, v_tm)

    with contextlib.ExitStack() as ps:
        def sb(name, shape, dtype):
            return ps.enter_context(nc.sbuf_tensor(f"h{head}p_{name}", shape, dtype))

        stats_fm = sb("stats_fm", [16, L], BF16)
        raw = sb("raw", [128, NT * 12], F32)
        stats_tm = sb("stats_tm", [128, NT * 16], F32)
        st_junk = sb("st_junk", [128, DK], BF16)
        st_scr = sb("st_scr", [128, 8], F32)
        h1h = sb("h1h", [128, NC * 512], BF16)
        w1m = sb("w1m", [128, NC * D], BF16)
        logits_sb = sb("logits_sb", [4, L], F32)
        p_scr = sb("p_scr", [128, NT * 4], F32)
        sm_scr = sb("sm_scr", [128, 8], F32)
        o_acc = sb("o_acc", [128, DK], F32)
        o_n = sb("o_n", [128, DK], BF16)

        v_scan = c.cnt["dv"]
        with nc.Block(f"hd{head}_stats_raw") as blk:
            @blk.vector
            def _(v):
                c.wait(v, "dv", v_scan)
                for ti in range(NT):
                    for x in range(4):
                        src = tensors[x][:, ti * DK: ti * DK + DK]
                        v.tensor_reduce(raw[:, ti * 12 + x: ti * 12 + x + 1], src,
                                        axis=AX.X, op=ALU.add)
                        mm = v.tensor_reduce(raw[:, ti * 12 + 8 + x: ti * 12 + 8 + x + 1], src,
                                             axis=AX.X, op=ALU.add, apply_absolute_value=True)
                c.inc(mm, "dv")

            @blk.scalar
            def _(s):
                c.wait(s, "dv", v_scan)
                for ti in range(NT):
                    for x in range(4):
                        src = tensors[x][:, ti * DK: ti * DK + DK]
                        if ti or x:
                            s.drain()
                        mm = s.activation(st_junk[:], src, AF.Square,
                                          accum_out=raw[:, ti * 12 + 4 + x: ti * 12 + 4 + x + 1])
                c.inc(mm, "ac")

        v_raw_v = c.cnt["dv"]
        v_raw_s = c.cnt["ac"]

        with nc.Block(f"hd{head}_stats_drv") as blk:
            @blk.vector
            def _(v):
                c.wait(v, "dv", v_raw_v)
                c.wait(v, "ac", v_raw_s)
                for ti in range(NT):
                    sums = raw[:, ti * 12: ti * 12 + 4]
                    sqs = raw[:, ti * 12 + 4: ti * 12 + 8]
                    abss = raw[:, ti * 12 + 8: ti * 12 + 12]
                    st = stats_tm[:, ti * 16: ti * 16 + 16].rearrange("p (x f) -> p f x", f=4)
                    v.tensor_scalar(st[:, 0:1, :], sums, 1.0 / DK, None, op0=ALU.mult)
                    mm = v.tensor_scalar(st[:, 2:3, :], abss, 1.0 / DK, None, op0=ALU.mult)
                    v.tensor_tensor(st_scr[:, 0:4], sums, sums, op=ALU.mult)
                    v.drain()
                    v.scalar_tensor_tensor(st_scr[:, 4:8], st_scr[:, 0:4], -1.0 / DK, sqs,
                                           op0=ALU.mult, op1=ALU.add)
                    v.drain()
                    mm = v.tensor_scalar(st[:, 1:2, :], st_scr[:, 4:8], 1.0 / DK, None,
                                         op0=ALU.mult)
                c.inc(mm, "dv")

            v_drv = c.cnt["dv"]

            @blk.scalar
            def _(s, v_drv=v_drv):
                c.wait(s, "dv", v_drv)
                for ti in range(NT):
                    sqs = raw[:, ti * 12 + 4: ti * 12 + 8]
                    st = stats_tm[:, ti * 16: ti * 16 + 16].rearrange("p (x f) -> p f x", f=4)
                    mm = s.activation(st[:, 3:4, :], sqs, AF.Sqrt)
                c.inc(mm, "ac")

        v_stats = c.cnt["ac"]

        for grp in range(4):
            u = grp % 2
            with nc.Block(f"hd{head}_sttr{grp}") as blk:
                @blk.tensor
                def _(t, grp=grp, u=u):
                    if grp == 0:
                        c.wait(t, "ac", v_stats)
                        c.wait(t, "dv")
                    c.pu_guard(t, u)
                    for q in range(4):
                        ti = grp * 4 + q
                        mm = t.transpose(g.pu[u][0:16, q * 128: q * 128 + 128],
                                         stats_tm[:, ti * 16: ti * 16 + 16], g.identf[:])
                    c.inc(mm, "pe")

                v_tr = c.cnt["pe"]

                @blk.vector
                def _(v, grp=grp, u=u, v_tr=v_tr):
                    c.wait(v, "pe", v_tr)
                    c.inc(v.tensor_copy(stats_fm[:, grp * 512: grp * 512 + 512],
                                        g.pu[u][0:16, :]), "dv")
                    c.pu_read(u, "dv")

        emit_prepass(c, g, 48, f"h{head}p")
        with nc.Block(f"hd{head}_w1ld") as blk:
            @blk.gpsimd
            def _(gg):
                c.wait(gg, "pe")
                c.wait(gg, "dv")
                c.inc(gg.dma_start(out=w1m[:],
                                   in_=g.W1_h[0:1024, :].rearrange("(c p) o -> p c o", p=128)),
                      "dwt", 16)

        v_w1m = c.cnt["dwt"]

        for ts4 in range(4):
            for c2 in range(NC):
                u = 2 + c2 % 2
                with nc.Block(f"hd{head}_gm{ts4}_{c2}") as blk:
                    @blk.tensor
                    def _(t, ts4=ts4, c2=c2, u=u):
                        c.pu_guard(t, u)
                        if ts4 == 0 and c2 == 0:
                            c.wait(t, "dwt", v_w1m)
                            c.wait(t, "dv")
                        t.matmul(g.pu[u][:], g.w1s_sb[:, c2 * 128: c2 * 128 + 128],
                                 stats_fm[:, ts4 * 512: ts4 * 512 + 512],
                                 start=True, stop=False)
                        for k in range(NC):
                            mm = t.matmul(g.pu[u][:],
                                          w1m[:, k * D + c2 * 128: k * D + c2 * 128 + 128],
                                          g.x_ln_fm[:, k * L + ts4 * 512: k * L + ts4 * 512 + 512],
                                          start=False, stop=(k == NC - 1))
                        c.inc(mm, "pe")

                    v_mm = c.cnt["pe"]

                    @blk.scalar
                    def _(s, ts4=ts4, c2=c2, u=u, v_mm=v_mm):
                        c.wait(s, "pe", v_mm)
                        c.inc(s.activation(h1h[:, c2 * 512: c2 * 512 + 512], g.pu[u][:],
                                           AF.Gelu, bias=g.b1_sb[:, c2: c2 + 1]), "ac")
                        c.pu_read(u, "ac")

            v_h1 = c.cnt["ac"]
            with nc.Block(f"hd{head}_gw2_{ts4}") as blk:
                @blk.tensor
                def _(t, ts4=ts4, v_h1=v_h1):
                    c.wait(t, "ac", v_h1)
                    c.pu_guard(t, 4)
                    for c2 in range(NC):
                        mm = t.matmul(g.pu[4][0:4, 0:512],
                                      g.w2_sb[:, c2 * 4: c2 * 4 + 4],
                                      h1h[:, c2 * 512: c2 * 512 + 512],
                                      start=(c2 == 0), stop=(c2 == NC - 1))
                    c.inc(mm, "pe")

                v_lg = c.cnt["pe"]

                @blk.vector
                def _(v, ts4=ts4, v_lg=v_lg):
                    c.wait(v, "pe", v_lg)
                    c.inc(v.tensor_scalar(logits_sb[0:4, ts4 * 512: ts4 * 512 + 512],
                                          g.pu[4][0:4, 0:512],
                                          g.gb_sb[:, head: head + 1],
                                          g.invtemp4[:, head: head + 1],
                                          op0=ALU.add, op1=ALU.mult), "dv")
                    c.pu_read(4, "dv")

        v_logits = c.cnt["dv"]
        with nc.Block(f"hd{head}_ptr") as blk:
            @blk.tensor
            def _(t):
                c.wait(t, "dv", v_logits)
                c.pu_guard(t, 5)
                for ti in range(NT):
                    mm = t.transpose(g.pu[5][:, ti * 4: ti * 4 + 4],
                                     logits_sb[0:4, ti * 128: ti * 128 + 128],
                                     g.identf[0:4, 0:4])
                c.inc(mm, "pe")

            v_tr = c.cnt["pe"]

            @blk.vector
            def _(v, v_tr=v_tr):
                c.wait(v, "pe", v_tr)
                c.inc(v.tensor_copy(p_scr[:], g.pu[5][:, 0:64]), "dv")
                c.pu_read(5, "dv")

        with nc.Block(f"hd{head}_sm") as blk:
            dv_b, ac_b = c.cnt["dv"], c.cnt["ac"]

            @blk.vector
            def _(v, dv_b=dv_b, ac_b=ac_b):
                for ti in range(NT):
                    if ti == 0:
                        c.wait(v, "dv", dv_b)
                    c.inc(v.tensor_reduce(sm_scr[:, 0:1], p_scr[:, ti * 4: ti * 4 + 4],
                                          axis=AX.X, op=ALU.max, negate=True), "dv")
                    c.wait(v, "ac", ac_b + ti + 1)
                    v.drain()
                    v.tensor_reduce(sm_scr[:, 1:2], p_scr[:, ti * 4: ti * 4 + 4],
                                    axis=AX.X, op=ALU.add)
                    v.drain()
                    v.reciprocal(sm_scr[:, 2:3], sm_scr[:, 1:2])
                    v.drain()
                    v.tensor_scalar(p_scr[:, ti * 4: ti * 4 + 4],
                                    p_scr[:, ti * 4: ti * 4 + 4],
                                    sm_scr[:, 2:3], None, op0=ALU.mult)
                    v.drain()
                    v.tensor_tensor(p_scr[:, ti * 4: ti * 4 + 4],
                                    p_scr[:, ti * 4: ti * 4 + 4], g.clampt[:], op=ALU.max)
                    v.drain()
                    v.tensor_reduce(sm_scr[:, 3:4], p_scr[:, ti * 4: ti * 4 + 4],
                                    axis=AX.X, op=ALU.add)
                    v.drain()
                    v.reciprocal(sm_scr[:, 4:5], sm_scr[:, 3:4])
                    v.drain()
                    v.tensor_scalar(p_scr[:, ti * 4: ti * 4 + 4],
                                    p_scr[:, ti * 4: ti * 4 + 4],
                                    sm_scr[:, 4:5], None, op0=ALU.mult)
                    v.drain()
                    c.inc(v.tensor_tensor(p_scr[:, ti * 4: ti * 4 + 1],
                                          p_scr[:, ti * 4: ti * 4 + 1],
                                          g.crb[:, head: head + 1], op=ALU.add), "dv")

            @blk.scalar
            def _(s, dv_b=dv_b):
                for ti in range(NT):
                    c.wait(s, "dv", dv_b + 2 * ti + 1)
                    c.inc(s.activation(p_scr[:, ti * 4: ti * 4 + 4],
                                       p_scr[:, ti * 4: ti * 4 + 4],
                                       AF.Exp, bias=sm_scr[:, 0:1]), "ac")

        c.tap(f"h{head}_p", p_scr[:], "dv")
        c.flush_taps()

        pre_rd = [c.pu_last.get(0), c.pu_last.get(1)]
        with nc.Block(f"hd{head}_mix") as blk:
            dv_b, ac_b, pe_b = c.cnt["dv"], c.cnt["ac"], c.cnt["pe"]

            @blk.vector
            def _(v, dv_b=dv_b, ac_b=ac_b, pe_b=pe_b):
                for ti in range(NT):
                    if ti == 0:
                        c.wait(v, "dv", dv_b)
                    c.wait(v, "ac", ac_b + 2 * ti)
                    v.tensor_scalar(o_acc[:], fs_tm[:, ti * DK: ti * DK + DK],
                                    p_scr[:, ti * 4: ti * 4 + 1], None, op0=ALU.mult)
                    for x, src in ((1, fl_tm), (2, delta_tm), (3, v_tm)):
                        v.drain()
                        mm = v.scalar_tensor_tensor(o_acc[:], src[:, ti * DK: ti * DK + DK],
                                                    p_scr[:, ti * 4 + x: ti * 4 + x + 1],
                                                    o_acc[:], op0=ALU.mult, op1=ALU.add)
                    c.inc(mm, "dv")
                    c.wait(v, "ac", ac_b + 2 * ti + 2)
                    c.wait(v, "pe", pe_b + ti)
                    v.drain()
                    v.reciprocal(sm_scr[:, 7:8], sm_scr[:, 6:7])
                    v.drain()
                    c.inc(v.scalar_tensor_tensor(o_n[:], o_acc[:], sm_scr[:, 7:8],
                                                 g.rmsw_bc[:], op0=ALU.mult, op1=ALU.mult),
                          "dv")
                    c.wait(v, "pe", pe_b + ti + 1)
                    u = ti % 2
                    for cc in range(2):
                        gc = 2 * head + cc
                        mm = v.tensor_copy(g.o_fm[:, gc * L + ti * 128: gc * L + ti * 128 + 128],
                                           g.pub[u][:, cc * 128: cc * 128 + 128])
                    c.inc(mm, "dv")
                    c.pu_read(u, "dv")

            @blk.scalar
            def _(s, dv_b=dv_b):
                for ti in range(NT):
                    c.wait(s, "dv", dv_b + 3 * ti + 1)
                    if ti > 0:
                        c.wait(s, "ac")
                    c.inc(s.activation(st_junk[:], o_acc[:], AF.Square,
                                       accum_out=sm_scr[:, 5:6]), "ac")
                    c.wait(s, "ac")
                    c.inc(s.activation(sm_scr[:, 6:7], sm_scr[:, 5:6], AF.Sqrt,
                                       scale=1.0 / DK, bias=g.cb[:, 2:3]), "ac")

            @blk.tensor
            def _(t, dv_b=dv_b, pe_b=pe_b, pre_rd=pre_rd):
                for ti in range(NT):
                    c.wait(t, "dv", dv_b + 3 * ti + 2)
                    if ti < 2 and pre_rd[ti] is not None:
                        c.wait(t, pre_rd[ti][0], pre_rd[ti][1])
                    u = ti % 2
                    for cc in range(2):
                        mm = t.transpose(g.pub[u][:, cc * 128: cc * 128 + 128],
                                         o_n[:, cc * 128: cc * 128 + 128], g.identb[:])
                    c.inc(mm, "pe")



def wo_residual(c, g):
    nc = g.nc
    fence(c, nc, "wo_fence")
    with contextlib.ExitStack() as ws:
        def sb(name, shape, dtype):
            return ws.enter_context(nc.sbuf_tensor(f"wo_{name}", shape, dtype))

        wo_sb = sb("w", [128, NC * D], BF16)
        x_slab = [sb(f"xs{i}", [128, 4 * D], F32) for i in range(2)]
        ao_fm = sb("ao", [128, 512], BF16)

        with nc.Block("wo_ld") as blk:
            @blk.gpsimd
            def _(gg):
                c.inc(gg.dma_start(out=wo_sb[:],
                                   in_=g.Wo_h.rearrange("(c p) o -> p c o", p=128)), "dwt", 16)

        v_wo = c.cnt["dwt"]
        xs_used = []
        for ts4 in range(4):
            with nc.Block(f"wo_x{ts4}") as blk:
                @blk.sync
                def _(sy, ts4=ts4):
                    if ts4 >= 2:
                        c.wait(sy, "dv", xs_used[ts4 - 2])
                    c.inc(sy.dma_start(
                        out=x_slab[ts4 % 2][:],
                        in_=g.x_hbm[ts4 * 512:(ts4 + 1) * 512, :].rearrange(
                            "(t p) d -> p t d", p=128)), "dx0" if ts4 % 2 == 0 else "dx1", 16)

            wxs = "dx0" if ts4 % 2 == 0 else "dx1"
            v_xs = c.cnt[wxs]
            for co in range(NC):
                u = co % 2
                ub = 2 + co % 2
                with nc.Block(f"wo_{ts4}_{co}") as blk:
                    @blk.tensor
                    def _(t, ts4=ts4, co=co, u=u):
                        if ts4 == 0 and co == 0:
                            c.wait(t, "dwt", v_wo)
                            c.wait(t, "dv")
                        c.pu_guard(t, u)
                        for k in range(NC):
                            mm = t.matmul(g.pu[u][:],
                                          wo_sb[:, k * D + co * 128: k * D + co * 128 + 128],
                                          g.o_fm[:, k * L + ts4 * 512: k * L + ts4 * 512 + 512],
                                          start=(k == 0), stop=(k == NC - 1))
                        c.inc(mm, "pe")

                    v_mm = c.cnt["pe"]

                    @blk.scalar
                    def _(s, u=u, v_mm=v_mm):
                        c.wait(s, "pe", v_mm)
                        c.wait(s, "dv")
                        c.inc(s.activation(ao_fm[:], g.pu[u][:], AF.Copy), "ac")
                        c.pu_read(u, "ac")

                    v_ev = c.cnt["ac"]

                    @blk.tensor
                    def _(t, ub=ub, v_ev=v_ev):
                        c.wait(t, "ac", v_ev)
                        c.pu_guard(t, ub)
                        for q in range(4):
                            mm = t.transpose(g.pub[ub][:, q * 128: q * 128 + 128],
                                             ao_fm[:, q * 128: q * 128 + 128], g.identb[:])
                        c.inc(mm, "pe")

                    v_tr = c.cnt["pe"]

                    @blk.vector
                    def _(v, ts4=ts4, co=co, ub=ub, v_tr=v_tr, v_xs=v_xs, wxs=wxs):
                        c.wait(v, "pe", v_tr)
                        c.wait(v, wxs, v_xs)
                        for q in range(4):
                            tt = ts4 * 4 + q
                            mm = v.tensor_tensor(
                                g.x2_tm[:, tt * D + co * 128: tt * D + co * 128 + 128],
                                g.pub[ub][:, q * 128: q * 128 + 128],
                                x_slab[ts4 % 2][:, q * D + co * 128: q * D + co * 128 + 128],
                                op=ALU.add)
                        c.inc(mm, "dv")
                        c.pu_read(ub, "dv")

            xs_used.append(c.cnt["dv"])


def moe(c, g):
    nc = g.nc
    fence(c, nc, "moe_fence")
    with contextlib.ExitStack() as ms:
        def sb(name, shape, dtype):
            return ms.enter_context(nc.sbuf_tensor(f"moe_{name}", shape, dtype, side="right"))

        xl2_rot = [sb(f"xl{i}", [128, D], BF16) for i in range(2)]
        ln_scr = sb("lnscr", [128, 8], F32)
        sq_junk = sb("sqj", [128, D], BF16)
        xl2_fm = sb("xfm", [128, NC * SLAB], BF16)
        lg_sb = sb("lg", [8, SLAB], F32)
        lg_tm = sb("lgtm", [128, 8 * 8], F32)
        g_tm = sb("gtm", [128, 8 * 8], F32)
        m8 = sb("m8", [128, 8], F32)
        e1 = sb("e1", [128, 8], F32)
        cum = sb("cum", [128, 8], F32)
        pr2 = sb("pr2", [128, 8], F32)
        ones8 = sb("ones8", [128, 8], F32)
        gsc = sb("gsc", [128, 8], F32)
        acc = sb("acc", [128, 8 * D], F32)
        h_sb = sb("h", [128, NF * SLAB], BF16)
        NWB = 4
        wvb = [sb(f"wv{i}", [128, NC * 128], BF16) for i in range(NWB)]
        wb_ = [sb(f"wb{i}", [128, NC * 128], BF16) for i in range(NWB)]
        w1b = [sb(f"w1b{i}", [128, D], BF16) for i in range(NWB)]
        ysc = sb("ysc", [128, 512], BF16)
        s1b = sb("s1b", [128, D], F32)
        s1e = sb("s1e", [1, D], F32)

        with nc.Block("moe_init") as blk:
            @blk.vector
            def _(v):
                c.inc(v.memset(ones8[:], 1.0), "dv")

        v_store = None
        for slab in range(NSLAB):
            ln_done = []
            tr_done = []
            ev_done = []
            for q in range(8):
                tt = slab * 8 + q
                with nc.Block(f"m_ln{slab}_{q}") as blk:
                    @blk.vector
                    def _(v, q=q, tt=tt):
                        if q == 0 and slab > 0:
                            c.wait(v, "dout", v_store)
                        c.inc(v.tensor_reduce(ln_scr[:, 0:1],
                                              g.x2_tm[:, tt * D: tt * D + D],
                                              axis=AX.X, op=ALU.add), "dv")

                    @blk.scalar
                    def _(s, q=q, tt=tt):
                        c.inc(s.activation(sq_junk[:], g.x2_tm[:, tt * D: tt * D + D],
                                           AF.Square, accum_out=ln_scr[:, 1:2]), "ac")

                    v_sq = c.cnt["ac"]

                    @blk.vector
                    def _(v, v_sq=v_sq):
                        c.wait(v, "ac", v_sq)
                        c.wait(v, "dv")
                        v.tensor_tensor(ln_scr[:, 2:3], ln_scr[:, 0:1], ln_scr[:, 0:1], op=ALU.mult)
                        v.drain()
                        c.inc(v.scalar_tensor_tensor(ln_scr[:, 3:4], ln_scr[:, 2:3], -1.0 / D,
                                                     ln_scr[:, 1:2], op0=ALU.mult, op1=ALU.add), "dv")

                    v_var = c.cnt["dv"]

                    @blk.scalar
                    def _(s, v_var=v_var):
                        c.wait(s, "dv", v_var)
                        c.inc(s.activation(ln_scr[:, 4:5], ln_scr[:, 3:4], AF.Sqrt,
                                           scale=1.0 / D, bias=g.cb[:, 2:3]), "ac")

                    v_std = c.cnt["ac"]

                    @blk.vector
                    def _(v, q=q, tt=tt, v_std=v_std):
                        c.wait(v, "ac", v_std)
                        v.reciprocal(ln_scr[:, 5:6], ln_scr[:, 4:5])
                        v.tensor_scalar(ln_scr[:, 6:7], ln_scr[:, 0:1], -1.0 / D, None, op0=ALU.mult)
                        v.drain()
                        if q >= 2:
                            c.wait(v, "pe", tr_done[q - 2])
                        ln_done.append(c.inc(v.tensor_scalar(xl2_rot[q % 2][:],
                                                             g.x2_tm[:, tt * D: tt * D + D],
                                                             ln_scr[:, 6:7], ln_scr[:, 5:6],
                                                             op0=ALU.add, op1=ALU.mult), "dv"))

                    v_xl = c.cnt["dv"]

                    @blk.tensor
                    def _(t, q=q, v_xl=v_xl):
                        c.wait(t, "dv", v_xl)
                        u = q % 2
                        c.pu_guard(t, u)
                        c.pu_guard(t, 2 + u)
                        for k in range(NC):
                            pb = g.pub[u] if k < 4 else g.pub[2 + u]
                            mm = t.transpose(pb[:, (k % 4) * 128:(k % 4) * 128 + 128],
                                             xl2_rot[q % 2][:, k * 128:(k + 1) * 128], g.identb[:])
                        tr_done.append(c.inc(mm, "pe"))

                    v_tr = c.cnt["pe"]

                    @blk.vector
                    def _(v, q=q, v_tr=v_tr):
                        c.wait(v, "pe", v_tr)
                        u = q % 2
                        for k in range(NC):
                            pb = g.pub[u] if k < 4 else g.pub[2 + u]
                            mm = v.tensor_copy(
                                xl2_fm[:, k * SLAB + q * 128: k * SLAB + q * 128 + 128],
                                pb[:, (k % 4) * 128:(k % 4) * 128 + 128])
                        ev_done.append(c.inc(mm, "dv"))
                        c.pu_read(u, "dv")
                        c.pu_read(2 + u, "dv")

            v_xl2 = c.cnt["dv"]

            for th in range(2):
                u = 4 + th
                with nc.Block(f"m_rt{slab}_{th}") as blk:
                    @blk.tensor
                    def _(t, th=th, u=u):
                        if th == 0:
                            c.wait(t, "dv", v_xl2)
                        c.pu_guard(t, u)
                        for k in range(NC):
                            mm = t.matmul(g.pu[u][0:8, 0:512],
                                          g.rw_sb[:, k * 8: k * 8 + 8],
                                          xl2_fm[:, k * SLAB + th * 512: k * SLAB + th * 512 + 512],
                                          start=(k == 0), stop=(k == NC - 1))
                        c.inc(mm, "pe")

                    v_mm = c.cnt["pe"]

                    @blk.vector
                    def _(v, th=th, u=u, v_mm=v_mm):
                        c.wait(v, "pe", v_mm)
                        c.inc(v.tensor_copy(lg_sb[0:8, th * 512: th * 512 + 512],
                                            g.pu[u][0:8, 0:512]), "dv")
                        c.pu_read(u, "dv")

            v_lg = c.cnt["dv"]
            with nc.Block(f"m_rtt{slab}") as blk:
                @blk.tensor
                def _(t):
                    c.wait(t, "dv", v_lg)
                    c.pu_guard(t, 6)
                    for q in range(8):
                        mm = t.transpose(g.pu[6][:, q * 8: q * 8 + 8],
                                         lg_sb[0:8, q * 128: q * 128 + 128],
                                         g.identf[0:8, 0:8])
                    c.inc(mm, "pe")

                v_tr = c.cnt["pe"]

                @blk.vector
                def _(v, v_tr=v_tr):
                    c.wait(v, "pe", v_tr)
                    c.inc(v.tensor_copy(lg_tm[:], g.pu[6][:, 0:64]), "dv")
                    c.pu_read(6, "dv")

            for q in range(8):
                with nc.Block(f"m_g{slab}_{q}") as blk:
                    @blk.vector
                    def _(v, q=q):
                        if q == 0:
                            c.wait(v, "dv")
                        c.inc(v.tensor_reduce(gsc[:, 0:1], lg_tm[:, q * 8: q * 8 + 8],
                                              axis=AX.X, op=ALU.max, negate=True), "dv")

                    v_mx = c.cnt["dv"]

                    @blk.scalar
                    def _(s, q=q, v_mx=v_mx):
                        c.wait(s, "dv", v_mx)
                        c.inc(s.activation(lg_tm[:, q * 8: q * 8 + 8],
                                           lg_tm[:, q * 8: q * 8 + 8],
                                           AF.Exp, bias=gsc[:, 0:1]), "ac")

                    v_ex = c.cnt["ac"]

                    @blk.vector
                    def _(v, q=q, v_ex=v_ex):
                        c.wait(v, "ac", v_ex)
                        pr = lg_tm[:, q * 8: q * 8 + 8]
                        v.tensor_reduce(gsc[:, 1:2], pr, axis=AX.X, op=ALU.add)
                        v.drain()
                        v.reciprocal(gsc[:, 2:3], gsc[:, 1:2])
                        v.drain()
                        v.tensor_scalar(pr, pr, gsc[:, 2:3], None, op0=ALU.mult)
                        v.drain()
                        v.max(m8[:], pr)
                        v.drain()
                        v.tensor_scalar(e1[:], pr, m8[:, 0:1], None, op0=ALU.is_equal)
                        v.drain()
                        v.tensor_tensor_scan(cum[:], e1[:], ones8[:], 0.0,
                                             op0=ALU.add, op1=ALU.mult)
                        v.drain()
                        v.tensor_scalar(cum[:], cum[:], 1.0, None, op0=ALU.is_equal)
                        v.drain()
                        v.tensor_tensor(e1[:], e1[:], cum[:], op=ALU.mult)
                        v.drain()
                        v.tensor_scalar(g_tm[:, q * 8: q * 8 + 8], e1[:], m8[:, 0:1],
                                        None, op0=ALU.mult)
                        v.drain()
                        v.scalar_tensor_tensor(pr2[:], e1[:], -1e30, pr,
                                               op0=ALU.mult, op1=ALU.add)
                        v.drain()
                        v.tensor_scalar(e1[:], pr2[:], m8[:, 1:2], None, op0=ALU.is_equal)
                        v.drain()
                        v.tensor_tensor_scan(cum[:], e1[:], ones8[:], 0.0,
                                             op0=ALU.add, op1=ALU.mult)
                        v.drain()
                        v.tensor_scalar(cum[:], cum[:], 1.0, None, op0=ALU.is_equal)
                        v.drain()
                        v.tensor_tensor(e1[:], e1[:], cum[:], op=ALU.mult)
                        v.drain()
                        v.tensor_scalar(e1[:], e1[:], m8[:, 1:2], None, op0=ALU.mult)
                        v.drain()
                        c.inc(v.tensor_tensor(g_tm[:, q * 8: q * 8 + 8],
                                              g_tm[:, q * 8: q * 8 + 8], e1[:],
                                              op=ALU.add), "dv")

            v_gates = c.cnt["dv"]
            if slab == 0:
                c.tap("g_tm", g_tm[:], "dv")
                c.tap("xl2_fm", xl2_fm[:], "dv")
                c.flush_taps()

            for e in range(E):
                with nc.Block(f"m_s1{slab}_{e}") as blk:
                    @blk.sync
                    def _(sy, e=e):
                        c.wait(sy, "pe")
                        c.inc(sy.dma_start(out=s1e[:], in_=g.ms1_h[e:e + 1, :]), "ds1", 16)

                    v_s1d = c.cnt["ds1"]

                    @blk.tensor
                    def _(t, e=e, v_s1d=v_s1d):
                        if e == 0:
                            c.wait(t, "dv", v_gates)
                        c.wait(t, "ds1", v_s1d)
                        c.pu_guard(t, 6)
                        c.pu_guard(t, 7)
                        for hh in range(2):
                            mm = t.matmul(g.pu[6 + hh][:], g.onesrow[:],
                                          s1e[0:1, hh * 512: hh * 512 + 512],
                                          start=True, stop=True)
                        c.inc(mm, "pe")

                    v_s1 = c.cnt["pe"]

                    @blk.vector
                    def _(v, v_s1=v_s1):
                        c.wait(v, "pe", v_s1)
                        v.tensor_copy(s1b[:, 0:512], g.pu[6][:])
                        c.inc(v.tensor_copy(s1b[:, 512:1024], g.pu[7][:]), "dv")
                        c.pu_read(6, "dv")
                        c.pu_read(7, "dv")

                pe_b = c.cnt["pe"]; din_b = c.cnt["din"]
                ac_b = c.cnt["ac"]; dv_b = c.cnt["dv"]
                with nc.Block(f"m_vy{slab}_{e}") as blk:
                    wtarget = []

                    @blk.sync
                    def _(sy, e=e):
                        c.wait(sy, "dpp", g.pp_total)
                        for f in range(NF):
                            sl_ = f"dw{f % NWB}"
                            if f >= NWB - 1:
                                c.wait(sy, "pe", pe_b + 2 * (f - (NWB - 1) + 1))
                            row = (e * NF + f) * 128
                            c.inc(sy.dma_start(out=wvb[f % NWB][:],
                                               in_=g.wvt_d[row:row + 128, :]), sl_, 16)
                            c.inc(sy.dma_start(out=wb_[f % NWB][:],
                                               in_=g.wt_d[row:row + 128, :]), sl_, 16)
                            wtarget.append(c.cnt[sl_])

                    @blk.tensor
                    def _(t, e=e):
                        for f in range(NF):
                            c.wait(t, f"dw{f % NWB}", wtarget[f])
                            for th in range(2):
                                idx = 2 * f + th
                                par = idx % 2
                                ua, ub = par * 2, par * 2 + 1
                                if idx < 2:
                                    c.pu_guard(t, ua)
                                    c.pu_guard(t, ub)
                                else:
                                    c.wait(t, "ac", ac_b + idx - 1)
                                    c.wait(t, "dv", dv_b + idx - 1)
                                for k in range(NC):
                                    t.matmul(g.pu[ua][:],
                                             wvb[f % NWB][:, k * 128: k * 128 + 128],
                                             xl2_fm[:, k * SLAB + th * 512: k * SLAB + th * 512 + 512],
                                             start=(k == 0), stop=(k == NC - 1))
                                for k in range(NC):
                                    mm = t.matmul(g.pu[ub][:],
                                                  wb_[f % NWB][:, k * 128: k * 128 + 128],
                                                  xl2_fm[:, k * SLAB + th * 512: k * SLAB + th * 512 + 512],
                                                  start=(k == 0), stop=(k == NC - 1))
                                c.inc(mm, "pe")

                    @blk.scalar
                    def _(s, e=e):
                        for f in range(NF):
                            for th in range(2):
                                idx = 2 * f + th
                                par = idx % 2
                                ua, ub = par * 2, par * 2 + 1
                                c.wait(s, "pe", pe_b + idx + 1)
                                c.wait(s, "dv", dv_b + idx)
                                c.inc(s.activation(ysc[:], g.pu[ub][:], AF.Gelu_apprx_tanh,
                                                   scale=g.s_sb[:, e * NF + f: e * NF + f + 1]),
                                      "ac")

                    @blk.vector
                    def _(v, e=e):
                        for f in range(NF):
                            for th in range(2):
                                idx = 2 * f + th
                                par = idx % 2
                                ua = par * 2
                                c.wait(v, "ac", ac_b + idx + 1)
                                cp = v.scalar_tensor_tensor(
                                    h_sb[:, f * SLAB + th * 512: f * SLAB + th * 512 + 512],
                                    g.pu[ua][:],
                                    g.sv_sb[:, e * NF + f: e * NF + f + 1],
                                    ysc[:], op0=ALU.mult, op1=ALU.mult)
                                c.inc(cp, "dv")

                for u in range(4):
                    c.pu_read(u, "dv")

                for th in range(2):
                    pe_b = c.cnt["pe"]; din_b = c.cnt["din"]; dv_b = c.cnt["dv"]
                    v_s1b = c.cnt["dv"]
                    with nc.Block(f"m_o{slab}_{e}_{th}") as blk:
                        wtarget = []

                        @blk.sync
                        def _(sy, e=e, th=th):
                            c.wait(sy, "dpp", g.pp_total)
                            for f in range(NF):
                                sl_ = f"dw{f % NWB}"
                                if f >= NWB - 1:
                                    c.wait(sy, "pe", pe_b + (f - (NWB - 1) + 1))
                                row = (e * NF + f) * 128
                                c.inc(sy.dma_start(out=w1b[f % NWB][:],
                                                   in_=g.w1t_d[row:row + 128, :]), sl_, 16)
                                wtarget.append(c.cnt[sl_])

                        @blk.vector
                        def _(v, e=e, th=th):
                            for f in range(NF):
                                c.wait(v, f"dw{f % NWB}", wtarget[f])
                                if f >= NWB:
                                    c.wait(v, "pe", pe_b + (f - NWB + 1))
                                c.inc(v.tensor_tensor(w1b[f % NWB][:], w1b[f % NWB][:],
                                                      s1b[:], op=ALU.mult), "dv")

                        @blk.tensor
                        def _(t, e=e, th=th):
                            for u in range(8):
                                c.pu_guard(t, u)
                            for f in range(NF):
                                c.wait(t, "dv", dv_b + f + 1)
                                for dh in range(2):
                                    for tq in range(4):
                                        u = dh * 4 + tq
                                        mm = t.matmul(
                                            g.pu[u][:],
                                            h_sb[:, f * SLAB + th * 512 + tq * 128:
                                                 f * SLAB + th * 512 + tq * 128 + 128],
                                            w1b[f % NWB][:, dh * 512: dh * 512 + 512],
                                            start=(f == 0), stop=(f == NF - 1))
                                c.inc(mm, "pe")

                    v_o = c.cnt["pe"]
                    with nc.Block(f"m_oe{slab}_{e}_{th}") as blk:
                        @blk.vector
                        def _(v, e=e, th=th, slab=slab, v_o=v_o):
                            c.wait(v, "pe", v_o)
                            for dh in range(2):
                                for tq in range(4):
                                    u = dh * 4 + tq
                                    tl = th * 4 + tq
                                    tt = slab * 8 + tl
                                    gcol = g_tm[:, tl * 8 + e: tl * 8 + e + 1]
                                    if e == 0:
                                        base = g.x2_tm[:, tt * D + dh * 512: tt * D + dh * 512 + 512]
                                    else:
                                        base = acc[:, tl * D + dh * 512: tl * D + dh * 512 + 512]
                                    mm = v.scalar_tensor_tensor(
                                        acc[:, tl * D + dh * 512: tl * D + dh * 512 + 512],
                                        g.pu[u][:], gcol, base, op0=ALU.mult, op1=ALU.add)
                                    c.pu_read(u, "dv", c.cnt["dv"] + 1)
                            c.inc(mm, "dv")

            v_acc = c.cnt["dv"]
            with nc.Block(f"m_st{slab}") as blk:
                @blk.sync
                def _(sy, slab=slab, v_acc=v_acc):
                    c.wait(sy, "dv", v_acc)
                    c.inc(sy.dma_start(
                        out=g.out_hbm[slab * SLAB:(slab + 1) * SLAB, :].rearrange(
                            "(t p) d -> p t d", p=128),
                        in_=acc[:]), "dout", 16)

            v_store = c.cnt["dout"]


def _host_consts():
    p32 = np.arange(128)[:, None] % 32
    j32 = np.arange(32)[None, :]
    masks = np.zeros((128, 128), np.float32)
    masks[:, 0:32] = np.where(p32 < j32, -1.0, 0.0)
    masks[:, 32:64] = np.where(p32 > j32, -1.0, 0.0)
    masks[:, 64:96] = np.where(p32 <= j32, 1.0, 0.0)
    masks[:, 96:128] = np.where(p32 == j32, 1.0, 0.0)
    clamp = np.zeros((128, 4), np.float32)
    clamp[:, 0] = 0.05
    clamp[:, 1] = 0.05
    return {
        "ident_bf": np.eye(128, dtype=BF),
        "ident_f32": np.eye(128, dtype=np.float32),
        "masks_c": masks,
        "clamp": clamp,
        "ones_col": np.ones((128, 1), dtype=BF),
        "ones_row": np.ones((1, 128), dtype=np.float32),
        "ones_row_bf": np.ones((1, 128), dtype=BF),
        "cbias": np.tile(np.array([1.0, 1e-6, 1e-5, 0.0], np.float32), (128, 1)),
    }


def _chunked(w):
    cnum, j = w.shape[0] // 128, w.shape[1]
    return np.ascontiguousarray(w.reshape(cnum, 128, j).transpose(1, 0, 2).reshape(128, cnum * j))


def _expertcols(s):
    e, ff = s.shape
    return np.ascontiguousarray(s.reshape(e, ff // 128, 128).transpose(2, 0, 1).reshape(128, e * (ff // 128)))


def make_in_maps(inputs):
    f = np.float32
    a = lambda k: np.ascontiguousarray(np.asarray(inputs[k], dtype=f))
    shared = {
        "Wq": a("L0_attn_Wq"), "Wk": a("L0_attn_Wk"), "Wv": a("L0_attn_Wv"),
        "Wb": _chunked(a("L0_attn_Wb")),
        "conv_q": _chunked(a("L0_attn_conv_q")),
        "conv_k": _chunked(a("L0_attn_conv_k")),
        "conv_v": _chunked(a("L0_attn_conv_v")),
        "fir_long": _chunked(a("L0_attn_fir_long").reshape(D, 64)),
        "fir_short": _chunked(a("L0_attn_fir_short").reshape(D, 5)),
        "W1": a("L0_attn_W1"),
        "b1": np.ascontiguousarray(a("L0_attn_b1").reshape(8, 128).T),
        "W2": _chunked(a("L0_attn_W2")),
        "bbT": np.ascontiguousarray(a("L0_attn_base_bias").T),
        "b2c": a("L0_attn_b2").reshape(4, 1),
        "log_temp": a("L0_attn_log_temp").reshape(1, 4),
        "crl": a("L0_attn_conv_residual_logit").reshape(1, 4),
        "rms_w": a("L0_attn_rms_w").reshape(1, DK),
        "Wo": a("L0_attn_Wo"),
        "router_w": _chunked(a("L0_ffn_router_w")),
        "m_wv": a("L0_ffn_wv").reshape(E * D, FF),
        "m_sv": _expertcols(a("L0_ffn_sv").reshape(E, FF)),
        "m_w": a("L0_ffn_w").reshape(E * D, FF),
        "m_s": _expertcols(a("L0_ffn_s").reshape(E, FF)),
        "m_w1": a("L0_ffn_w1").reshape(E * FF, D),
        "m_s1": a("L0_ffn_s1").reshape(E, D),
    }
    shared.update(_host_consts())
    x_full = np.ascontiguousarray(np.asarray(inputs["L0_attn_x"], dtype=f))
    in_maps = []
    for core in range(8):
        m = dict(shared)
        m["x"] = np.ascontiguousarray(x_full[core])
        in_maps.append(m)
    return in_maps


def kernel(**inputs):
    in_maps = make_in_maps(inputs)
    nc = build_program()
    res = run_bass_kernel_spmd(nc, in_maps, core_ids=list(range(8)))
    out = np.stack([np.asarray(r["out"], dtype=np.float32) for r in res.results], axis=0)
    return out


# version 11 (final)
# speedup vs baseline: 1.1953x; 1.0017x over previous
import contextlib
import numpy as np
import ml_dtypes
import concourse.bass as bass
import concourse.mybir as mybir
from concourse.bass_utils import run_bass_kernel_spmd

F32 = mybir.dt.float32
BF16 = mybir.dt.bfloat16
AF = mybir.ActivationFunctionType
ALU = mybir.AluOpType
AX = mybir.AxisListType
BF = ml_dtypes.bfloat16

L = 2048
D = 1024
H = 4
DK = 256
NT = L // 128
NC = D // 128
NCH = L // 32
E = 8
FF = 4096
NF = FF // 128
SLAB = 1024
NSLAB = L // SLAB

DEBUG_TAPS = set()
STOP_AFTER = None


class _Cut(Exception):
    pass


class Ctx:
    def __init__(self, nc):
        self.nc = nc
        self.sem = {}
        self.cnt = {}
        self.dbg = []
        self.pu_last = {}
        self.stack = contextlib.ExitStack()

    def new_sem(self, name):
        self.sem[name] = self.stack.enter_context(self.nc.semaphore(name))
        self.cnt[name] = 0

    def inc(self, instr, name, n=1):
        self.cnt[name] += n
        instr.then_inc(self.sem[name], n)
        return self.cnt[name]

    def wait(self, eng, name, v=None):
        eng.wait_ge(self.sem[name], self.cnt[name] if v is None else v)

    def sb(self, name, shape, dtype):
        return self.stack.enter_context(self.nc.sbuf_tensor(name, shape, dtype))

    def pu_read(self, unit, semname, v=None):
        self.pu_last[unit] = (semname, self.cnt[semname] if v is None else v)

    def pu_guard(self, eng, unit):
        rec = self.pu_last.get(unit)
        if rec is not None:
            self.wait(eng, rec[0], rec[1])

    def tap(self, name, src_ap, semname):
        if name not in DEBUG_TAPS:
            return
        dram = self.nc.dram_tensor("dbg_" + name, list(src_ap.shape), src_ap.dtype,
                                   kind="ExternalOutput").ap()
        self.dbg.append((name, dram, src_ap, semname, self.cnt[semname]))

    def checkpoint(self, name):
        if STOP_AFTER != name:
            return False
        with self.nc.Block(f"cut_{name}") as blk:
            @blk.sync
            def _(sy):
                for s in ("pe", "dv", "ac", "gp", "din", "dwt", "dout"):
                    self.wait(sy, s)
        return True

    def flush_taps(self):
        if not self.dbg:
            return
        taps = self.dbg
        self.dbg = []
        self._tapn = getattr(self, "_tapn", 0) + 1
        with self.nc.Block(f"taps{self._tapn}") as blk:
            @blk.sync
            def _(sy):
                for name, dram, src, semname, v in taps:
                    sy.wait_ge(self.sem[semname], v)
                    self.inc(sy.dma_start(out=dram, in_=src), "dout", 16)


def build_program():
    nc = bass.Bass("TRN2", target_bir_lowering=False)
    c = Ctx(nc)

    def din(name, shape, dtype=F32):
        return nc.dram_tensor(name, list(shape), dtype, kind="ExternalInput").ap()

    x_hbm = din("x", (L, D))
    Wq_h = din("Wq", (D, D)); Wk_h = din("Wk", (D, D)); Wv_h = din("Wv", (D, D))
    Wb_h = din("Wb", (128, 32))
    cq_h = din("conv_q", (128, 32)); ck_h = din("conv_k", (128, 32)); cv_h = din("conv_v", (128, 32))
    firl_h = din("fir_long", (128, 512)); firs_h = din("fir_short", (128, 40))
    W1_h = din("W1", (1040, D)); b1_h = din("b1", (128, 8))
    W2_h = din("W2", (128, 32))
    bbT_h = din("bbT", (4, 4)); b2c_h = din("b2c", (4, 1))
    ltm_h = din("log_temp", (1, 4)); crl_h = din("crl", (1, 4))
    rmsw_h = din("rms_w", (1, DK))
    Wo_h = din("Wo", (D, D))
    rw_h = din("router_w", (128, 64))
    mwv_h = din("m_wv", (E * D, FF)); msv_h = din("m_sv", (128, E * NF))
    mw_h = din("m_w", (E * D, FF)); ms_h = din("m_s", (128, E * NF))
    mw1_h = din("m_w1", (E * FF, D)); ms1_h = din("m_s1", (E, D))
    identb_h = din("ident_bf", (128, 128), BF16)
    identf_h = din("ident_f32", (128, 128))
    masks_h = din("masks_c", (128, 128))
    clamp_h = din("clamp", (128, 4))
    onesc_h = din("ones_col", (128, 1), BF16)
    onesr_h = din("ones_row", (1, 128))
    onesrbf_h = din("ones_row_bf", (1, 128), BF16)
    cb_h = din("cbias", (128, 4))

    out_hbm = nc.dram_tensor("out", [L, D], F32, kind="ExternalOutput").ap()
    wvt_d = nc.dram_tensor("wvt_d", [E * NF * 128, NC * 128], BF16).ap()
    wt_d = nc.dram_tensor("wt_d", [E * NF * 128, NC * 128], BF16).ap()
    w1t_d = nc.dram_tensor("w1t_d", [E * NF * 128, D], BF16).ap()

    for s in ("din", "dwt", "dpp", "dout", "pe", "dv", "ac", "gp",
               "dw0", "dw1", "dw2", "dw3", "ds1", "dx0", "dx1"):
        c.new_sem(s)

    pu = [c.stack.enter_context(nc.psum_tensor(f"pu{i}", [128, 512], F32)) for i in range(8)]
    pub = [p[:].bitcast(BF16) for p in pu]

    identb = c.sb("identb", [128, 128], BF16)
    identf = c.sb("identf", [128, 128], F32)
    masks = c.sb("masks", [128, 128], F32)
    clampt = c.sb("clampt", [128, 4], F32)
    onescol = c.sb("onescol", [128, 1], BF16)
    onesrow = c.sb("onesrow", [1, 128], F32)
    cb = c.sb("cb", [128, 4], F32)
    cq_sb = c.sb("cq_sb", [128, 32], F32)
    ck_sb = c.sb("ck_sb", [128, 32], F32)
    cv_sb = c.sb("cv_sb", [128, 32], F32)
    firs_sb = c.sb("firs_sb", [128, 40], F32)
    firl_sb = c.sb("firl_sb", [128, 512], F32)
    b1_sb = c.sb("b1_sb", [128, 8], F32)
    bbT_sb = c.sb("bbT_sb", [4, 4], F32)
    b2c_sb = c.sb("b2c_sb", [4, 1], F32)
    gb_sb = c.sb("gb_sb", [4, 4], F32)
    ltm_sb = c.sb("ltm_sb", [1, 4], F32)
    crl_sb = c.sb("crl_sb", [1, 4], F32)
    rmsw_row = c.sb("rmsw_row", [1, DK], F32)
    w1s_sb = c.sb("w1s_sb", [16, D], BF16)
    w2_sb = c.sb("w2_sb", [128, 32], BF16)
    rw_sb = c.sb("rw_sb", [128, 64], BF16)
    wb_sb = c.sb("wb_sb", [128, 32], BF16)
    sv_sb = c.sb("sv_sb", [128, E * NF], F32)
    s_sb = c.sb("s_sb", [128, E * NF], F32)
    invtemp4 = c.sb("invtemp4", [4, 4], F32)
    crb = c.sb("crb", [128, 4], F32)
    rmsw_bc = c.sb("rmsw_bc", [128, DK], F32)
    onesrow_bf = c.sb("onesrow_bf", [1, 128], BF16)

    es_o = contextlib.ExitStack()
    es_x = contextlib.ExitStack()
    o_fm = es_o.enter_context(nc.sbuf_tensor("o_fm", [128, NC * L], BF16))
    x_ln_fm = es_x.enter_context(nc.sbuf_tensor("x_ln_fm", [128, NC * L], BF16))

    with nc.Block("ph0") as blk:
        @blk.sync
        def _(sy):
            for dst, src in [
                (identb[:], identb_h), (identf[:], identf_h), (masks[:], masks_h),
                (clampt[:], clamp_h), (onescol[:], onesc_h), (onesrow[:], onesr_h),
                (cb[:], cb_h),
                (cq_sb[:], cq_h), (ck_sb[:], ck_h), (cv_sb[:], cv_h),
                (firs_sb[:], firs_h), (firl_sb[:], firl_h), (b1_sb[:], b1_h),
                (bbT_sb[:], bbT_h), (b2c_sb[:], b2c_h),
                (ltm_sb[:], ltm_h), (crl_sb[:], crl_h), (rmsw_row[:], rmsw_h),
                (sv_sb[:], msv_h), (s_sb[:], ms_h),
                (onesrow_bf[:], onesrbf_h),
            ]:
                c.inc(sy.dma_start(out=dst, in_=src, allow_slow_non_contiguous=True),
                      "din", 16)

        @blk.gpsimd
        def _(g):
            c.inc(g.dma_start(out=w1s_sb[:], in_=W1_h[1024:1040, :]), "dwt", 16)
            c.inc(g.dma_start(out=w2_sb[:], in_=W2_h), "dwt", 16)
            c.inc(g.dma_start(out=rw_sb[:], in_=rw_h), "dwt", 16)
            c.inc(g.dma_start(out=wb_sb[:], in_=Wb_h), "dwt", 16)

    v_consts = c.cnt["din"]

    with nc.Block("ph0b") as blk:
        @blk.scalar
        def _(s):
            c.wait(s, "din", v_consts)
            c.inc(s.activation(ltm_sb[:], ltm_sb[:], AF.Exp), "ac")
            c.wait(s, "ac")
            c.inc(s.activation(ltm_sb[:], ltm_sb[:], AF.Ln, bias=cb[0:1, 0:1]), "ac")
            c.inc(s.activation(crl_sb[:], crl_sb[:], AF.Sigmoid), "ac")

        @blk.vector
        def _(v):
            c.wait(v, "ac")
            c.inc(v.tensor_scalar(ltm_sb[:], ltm_sb[:], 1e-4, None, op0=ALU.add), "dv")
            c.wait(v, "dv")
            c.inc(v.reciprocal(ltm_sb[:], ltm_sb[:]), "dv")
            c.wait(v, "dv")
            v.tensor_scalar(crl_sb[:], crl_sb[:], 0.5, None, op0=ALU.mult)
            c.inc(v.tensor_scalar(gb_sb[:], bbT_sb[:], b2c_sb[:, 0:1], None, op0=ALU.add), "dv")

        @blk.tensor
        def _(t):
            c.wait(t, "dv")
            t.matmul(pu[0][0:4, 0:4], onesrow[0:1, 0:4], ltm_sb[0:1, :], start=True, stop=True)
            t.matmul(pu[1][:, 0:4], onesrow[0:1, :], crl_sb[0:1, :], start=True, stop=True)
            c.inc(t.matmul(pu[2][:, 0:DK], onesrow[0:1, :], rmsw_row[0:1, :], start=True, stop=True), "pe")

        @blk.vector
        def _(v):
            c.wait(v, "pe")
            v.tensor_copy(invtemp4[:], pu[0][0:4, 0:4])
            v.tensor_copy(crb[:], pu[1][:, 0:4])
            c.inc(v.tensor_copy(rmsw_bc[:], pu[2][:, 0:DK]), "dv")
            for u in (0, 1, 2):
                c.pu_read(u, "dv")

    with contextlib.ExitStack() as ph1:
        x_rot = [ph1.enter_context(nc.sbuf_tensor(f"x_rot{i}", [128, D], F32)) for i in range(2)]
        xln_rot = [ph1.enter_context(nc.sbuf_tensor(f"xln_rot{i}", [128, D], BF16)) for i in range(2)]
        ln_scr = ph1.enter_context(nc.sbuf_tensor("ln_scr", [128, 8], F32))
        sq_junk = ph1.enter_context(nc.sbuf_tensor("sq_junk", [128, D], BF16))

        ln_done = []
        sq_done = []
        tr_done = []
        ev_done = []
        for ti in range(NT):
            with nc.Block(f"ln1_{ti}") as blk:
                @blk.sync
                def _(sy, ti=ti):
                    if ti >= 2:
                        c.wait(sy, "dv", ln_done[ti - 2])
                        c.wait(sy, "ac", sq_done[ti - 2])
                    c.inc(sy.dma_start(out=x_rot[ti % 2][:],
                                       in_=x_hbm[ti * 128:(ti + 1) * 128, :]),
                          "dx0" if ti % 2 == 0 else "dx1", 16)

                xs = "dx0" if ti % 2 == 0 else "dx1"
                v_x = c.cnt[xs]

                @blk.vector
                def _(v, ti=ti, v_x=v_x, xs=xs):
                    c.wait(v, xs, v_x)
                    c.inc(v.tensor_reduce(ln_scr[:, 0:1], x_rot[ti % 2][:], axis=AX.X, op=ALU.add), "dv")

                v_sum = c.cnt["dv"]

                @blk.scalar
                def _(s, ti=ti, v_x=v_x, xs=xs):
                    c.wait(s, xs, v_x)
                    sq_done.append(c.inc(s.activation(sq_junk[:], x_rot[ti % 2][:], AF.Square,
                                                      accum_out=ln_scr[:, 1:2]), "ac"))

                v_sq = c.cnt["ac"]

                @blk.vector
                def _(v, v_sq=v_sq, v_sum=v_sum):
                    c.wait(v, "ac", v_sq)
                    c.wait(v, "dv", v_sum)
                    v.tensor_tensor(ln_scr[:, 2:3], ln_scr[:, 0:1], ln_scr[:, 0:1], op=ALU.mult)
                    v.drain()
                    c.inc(v.scalar_tensor_tensor(ln_scr[:, 3:4], ln_scr[:, 2:3], -1.0 / D,
                                                 ln_scr[:, 1:2], op0=ALU.mult, op1=ALU.add), "dv")

                v_var = c.cnt["dv"]

                @blk.scalar
                def _(s, v_var=v_var):
                    c.wait(s, "dv", v_var)
                    c.inc(s.activation(ln_scr[:, 4:5], ln_scr[:, 3:4], AF.Sqrt,
                                       scale=1.0 / D, bias=cb[:, 2:3]), "ac")

                v_std = c.cnt["ac"]

                @blk.vector
                def _(v, ti=ti, v_std=v_std):
                    c.wait(v, "ac", v_std)
                    v.reciprocal(ln_scr[:, 5:6], ln_scr[:, 4:5])
                    v.tensor_scalar(ln_scr[:, 6:7], ln_scr[:, 0:1], -1.0 / D, None, op0=ALU.mult)
                    v.drain()
                    if ti >= 2:
                        c.wait(v, "pe", tr_done[ti - 2])
                    ln_done.append(c.inc(v.tensor_scalar(xln_rot[ti % 2][:], x_rot[ti % 2][:],
                                                         ln_scr[:, 6:7], ln_scr[:, 5:6],
                                                         op0=ALU.add, op1=ALU.mult), "dv"))

                v_xln = c.cnt["dv"]

                @blk.tensor
                def _(t, ti=ti, v_xln=v_xln):
                    c.wait(t, "dv", v_xln)
                    if ti >= 2:
                        c.wait(t, "dv", ev_done[ti - 2])
                    for k in range(NC):
                        pb = pub[ti % 2] if k < 4 else pub[2 + ti % 2]
                        mm = t.transpose(pb[:, (k % 4) * 128:(k % 4) * 128 + 128],
                                         xln_rot[ti % 2][:, k * 128:(k + 1) * 128], identb[:])
                    tr_done.append(c.inc(mm, "pe"))

                v_tr = c.cnt["pe"]

                @blk.vector
                def _(v, ti=ti, v_tr=v_tr):
                    c.wait(v, "pe", v_tr)
                    for k in range(NC):
                        pb = pub[ti % 2] if k < 4 else pub[2 + ti % 2]
                        cp = v.tensor_copy(x_ln_fm[:, k * L + ti * 128: k * L + ti * 128 + 128],
                                           pb[:, (k % 4) * 128:(k % 4) * 128 + 128])
                    ev_done.append(c.inc(cp, "dv"))

    v_xln_all = c.cnt["dv"]
    for u in range(4):
        c.pu_read(u, "dv", v_xln_all)
    c.tap("x_ln_fm", x_ln_fm[:], "dv")
    c.flush_taps()
    if c.checkpoint("ln1"):
        return nc

    class G:
        pass
    g = G()
    g.nc = nc
    _locals = dict(
        pu=pu, pub=pub, identb=identb, identf=identf, masks=masks, clampt=clampt,
        onescol=onescol, onesrow=onesrow, onesrow_bf=onesrow_bf, cq_sb=cq_sb,
        ck_sb=ck_sb, cv_sb=cv_sb, firs_sb=firs_sb, firl_sb=firl_sb, b1_sb=b1_sb,
        gb_sb=gb_sb, ltm_sb=ltm_sb, crl_sb=crl_sb, w1s_sb=w1s_sb, w2_sb=w2_sb,
        rw_sb=rw_sb, wb_sb=wb_sb, sv_sb=sv_sb, s_sb=s_sb, invtemp4=invtemp4,
        crb=crb, rmsw_bc=rmsw_bc, cb=cb, x_ln_fm=x_ln_fm, o_fm=o_fm, x_hbm=x_hbm,
        Wq_h=Wq_h, Wk_h=Wk_h, Wv_h=Wv_h, Wo_h=Wo_h, W1_h=W1_h, mwv_h=mwv_h,
        mw_h=mw_h, mw1_h=mw1_h, ms1_h=ms1_h, out_hbm=out_hbm)
    for nm, val in _locals.items():
        setattr(g, nm, val)

    g.wvt_d, g.wt_d, g.w1t_d = wvt_d, wt_d, w1t_d
    g.pp_done = {}
    g.pp_todo = [(e, f) for e in range(E) for f in range(NF)]

    g.x2_tm = c.stack.enter_context(nc.sbuf_tensor("x2_tm", [128, NT * D], BF16, side="right"))

    try:
        for head in range(H):
            if attn_head(c, g, head):
                return nc
    except _Cut:
        return nc

    es_x.close()

    wo_residual(c, g)
    c.tap("x2_tm", g.x2_tm[:], "dv")
    c.flush_taps()
    if c.checkpoint("wo"):
        return nc
    es_o.close()

    emit_prepass(c, g, len(g.pp_todo), "rest")
    g.pp_total = c.cnt["dpp"]

    moe(c, g)

    with nc.Block("fin") as blk:
        @blk.sync
        def _(sy):
            c.wait(sy, "dout")

    return nc


def emit_prepass(c, g, n, tag):
    todo = g.pp_todo[:n]
    del g.pp_todo[:n]
    if not todo:
        return
    with g.nc.Block(f"pp_{tag}") as blk:
        @blk.gpsimd
        def _(gg):
            for e, f in todo:
                row = (e * NF + f) * 128
                c.inc(gg.dma_start(
                    out=g.wvt_d[row:row + 128, :],
                    in_=g.mwv_h[e * D:(e + 1) * D, f * 128:(f + 1) * 128].rearrange(
                        "(cc p) w -> p cc w", p=128)), "dpp", 16)
                c.inc(gg.dma_start(
                    out=g.wt_d[row:row + 128, :],
                    in_=g.mw_h[e * D:(e + 1) * D, f * 128:(f + 1) * 128].rearrange(
                        "(cc p) w -> p cc w", p=128)), "dpp", 16)
                c.inc(gg.dma_start(
                    out=g.w1t_d[row:row + 128, :],
                    in_=g.mw1_h[e * FF + f * 128: e * FF + (f + 1) * 128, :]), "dpp", 16)
                g.pp_done[(e, f)] = c.cnt["dpp"]


def fence(c, nc, name):
    with nc.Block(name) as blk:
        @blk.tensor
        def _(t):
            for s in ("dv", "ac", "gp", "din", "dwt", "dout"):
                c.wait(t, s)

        @blk.vector
        def _(v):
            for s in ("pe", "ac", "gp", "din", "dwt", "dout"):
                c.wait(v, s)

        @blk.scalar
        def _(s_):
            for s in ("pe", "dv", "gp", "din", "dwt", "dout"):
                c.wait(s_, s)

        @blk.gpsimd
        def _(gg):
            for s in ("pe", "dv", "ac", "dout"):
                c.wait(gg, s)

        @blk.sync
        def _(sy):
            for s in ("pe", "dv", "ac", "gp", "dout", "dpp"):
                c.wait(sy, s)


def tr_fm_to_tm(c, g, name, src_fn, dst_fn, n=16, units=(6, 7)):
    ngrp = (n + 7) // 8
    pe_b = c.cnt["pe"]
    with g.nc.Block(f"{name}_tr") as blk:
        @blk.tensor
        def _(t):
            c.wait(t, "dv")
            c.wait(t, "ac")
            for grp in range(ngrp):
                lo, hi = grp * 8, min(grp * 8 + 8, n)
                u = units[grp % len(units)]
                c.pu_guard(t, u)
                pb = g.pub[u]
                for tt in range(lo, hi):
                    mm = t.transpose(pb[:, (tt - lo) * 128:(tt - lo + 1) * 128],
                                     src_fn(tt), g.identb[:])
                c.inc(mm, "pe")

        @blk.vector
        def _(v):
            for grp in range(ngrp):
                lo, hi = grp * 8, min(grp * 8 + 8, n)
                u = units[grp % len(units)]
                c.wait(v, "pe", pe_b + grp + 1)
                pb = g.pub[u]
                for tt in range(lo, hi):
                    cp = v.tensor_copy(dst_fn(tt), pb[:, (tt - lo) * 128:(tt - lo + 1) * 128])
                c.inc(cp, "dv")
                c.pu_read(u, "dv")
    return c.cnt["dv"]


def attn_head(c, g, head):
    nc = g.nc
    fence(c, nc, f"hd{head}_fence")

    with contextlib.ExitStack() as hs:
        def sb(name, shape, dtype):
            return hs.enter_context(nc.sbuf_tensor(f"h{head}_{name}", shape, dtype))

        q_n_fm = sb("q_n_fm", [128, 2 * L], BF16)
        k_tm = sb("k_tm", [128, NT * DK], BF16)
        v_tm = sb("v_tm", [128, NT * DK], BF16)
        fs_tm = sb("fs_tm", [128, NT * DK], BF16)
        fl_tm = sb("fl_tm", [128, NT * DK], BF16)
        delta_tm = sb("delta_tm", [128, NT * DK], BF16)
        Wt_sb = sb("Wt_sb", [128, 2 * NCH * 32], BF16)
        ALt_sb = sb("ALt_sb", [128, 512], BF16)
        iTb = sb("iTb", [128, 512], BF16)

        with contextlib.ExitStack() as pa:
            def sba(name, shape, dtype):
                return pa.enter_context(nc.sbuf_tensor(f"h{head}a_{name}", shape, dtype))

            wslab = sba("wslab", [128, NC * DK], BF16)
            P_buf = sba("P_buf", [128, 3 + L], BF16)
            cacc = sba("cacc", [128, L], BF16)
            beta_bf = sba("beta_bf", [1, L], BF16)
            rsq = sba("rsq", [1, 512], F32)
            k_n_fm = sba("k_n_fm", [128, 2 * L], BF16)
            iT = sba("iT", [128, 512], BF16)
            iT2 = sba("iT2", [128, 512], BF16)
            Pt = sba("Pt", [128, 512], BF16)
            Qt = sba("Qt", [128, 512], BF16)
            Pt2 = sba("Pt2", [128, 512], BF16)
            Qt2 = sba("Qt2", [128, 512], BF16)
            q2s = P_buf[:, 0:1024]
            b_cb = sba("b_cb", [128, 512], BF16)
            b_rb = sba("b_rb", [128, 512], BF16)
            qfac = sba("qfac", [128, 512], BF16)

            pe_b = c.cnt["pe"]
            ac_b = c.cnt["ac"]
            with nc.Block(f"hd{head}_bt") as blk:
                @blk.tensor
                def _(t):
                    for ts4 in range(4):
                        u = ts4 % 2
                        if ts4 < 2:
                            c.pu_guard(t, u)
                        else:
                            c.wait(t, "ac", ac_b + ts4 - 1)
                        for k in range(NC):
                            mm = t.matmul(g.pu[u][0:1, 0:512],
                                          g.wb_sb[:, k * 4 + head: k * 4 + head + 1],
                                          g.x_ln_fm[:, k * L + ts4 * 512: k * L + ts4 * 512 + 512],
                                          start=(k == 0), stop=(k == NC - 1))
                        c.inc(mm, "pe")

                @blk.scalar
                def _(s):
                    for ts4 in range(4):
                        u = ts4 % 2
                        c.wait(s, "pe", pe_b + ts4 + 1)
                        cp = s.activation(beta_bf[0:1, ts4 * 512: ts4 * 512 + 512],
                                          g.pu[u][0:1, 0:512], AF.Sigmoid)
                        c.inc(cp, "ac")
            for u in (0, 1):
                c.pu_read(u, "ac")

            def proj_part(part, csb, cc, dst, dst_off):
                gc = 2 * head + cc
                pe_b = c.cnt["pe"]
                ac_b = c.cnt["ac"]
                dv_b = c.cnt["dv"]
                with nc.Block(f"hd{head}_{part}{cc}") as blk:
                    @blk.tensor
                    def _(t):
                        for ts4 in range(4):
                            u = ts4 % 2
                            if ts4 < 2:
                                c.pu_guard(t, u)
                            else:
                                c.wait(t, "ac", ac_b + ts4 - 1)
                            for k in range(NC):
                                mm = t.matmul(g.pu[u][:],
                                              wslab[:, k * DK + cc * 128: k * DK + cc * 128 + 128],
                                              g.x_ln_fm[:, k * L + ts4 * 512: k * L + ts4 * 512 + 512],
                                              start=(k == 0), stop=(k == NC - 1))
                            c.inc(mm, "pe")

                    @blk.scalar
                    def _(s):
                        for ts4 in range(4):
                            u = ts4 % 2
                            c.wait(s, "pe", pe_b + ts4 + 1)
                            if ts4 == 0:
                                c.wait(s, "dv", dv_b)
                            cp = s.activation(P_buf[:, 3 + ts4 * 512: 3 + ts4 * 512 + 512],
                                              g.pu[u][:], AF.Copy)
                            c.inc(cp, "ac")
                            c.pu_read(u, "ac")
                        c.wait(s, "dv", dv_b + 1)
                        c.wait(s, "pe")
                        if part == "v":
                            cp = s.activation(v_fm[:, 63:63 + L], cacc[:], AF.Silu)
                        else:
                            cp = s.activation(dst[:, dst_off: dst_off + L], cacc[:], AF.Silu)
                        c.inc(cp, "ac")

                    @blk.vector
                    def _(v):
                        c.wait(v, "ac", ac_b + 4)
                        v.memset(P_buf[:, 0:3], 0.0)
                        v.drain()
                        v.tensor_scalar(cacc[:], P_buf[:, 0:L], csb[:, gc * 4: gc * 4 + 1],
                                        None, op0=ALU.mult)
                        for j in range(1, 4):
                            v.drain()
                            mm = v.scalar_tensor_tensor(cacc[:], P_buf[:, j: j + L],
                                                        csb[:, gc * 4 + j: gc * 4 + j + 1],
                                                        cacc[:], op0=ALU.mult, op1=ALU.add)
                        c.inc(mm, "dv")

            def l2norm(part, dst_fm):
                for ts4 in range(4):
                    ua, ub = 2 + ts4 % 2, 4 + ts4 % 2
                    with nc.Block(f"hd{head}_l2{part}_{ts4}") as blk:
                        @blk.scalar
                        def _(s, ts4=ts4):
                            c.wait(s, "ac")
                            c.wait(s, "dv")
                            c.wait(s, "pe")
                            for cc in range(2):
                                cp = s.activation(q2s[:, cc * 512: cc * 512 + 512],
                                                  dst_fm[:, cc * L + ts4 * 512: cc * L + ts4 * 512 + 512],
                                                  AF.Square)
                            c.inc(cp, "ac")

                        v_sq = c.cnt["ac"]

                        @blk.tensor
                        def _(t, ts4=ts4, ua=ua, v_sq=v_sq):
                            c.wait(t, "ac", v_sq)
                            c.pu_guard(t, ua)
                            for cc in range(2):
                                mm = t.matmul(g.pu[ua][0:1, 0:512], g.onescol[:],
                                              q2s[:, cc * 512: cc * 512 + 512],
                                              start=(cc == 0), stop=(cc == 1))
                            c.inc(mm, "pe")

                        v_ss = c.cnt["pe"]

                        @blk.scalar
                        def _(s, ts4=ts4, ua=ua, v_ss=v_ss):
                            c.wait(s, "pe", v_ss)
                            c.inc(s.activation(rsq[0:1, :], g.pu[ua][0:1, 0:512], AF.Ln,
                                               bias=g.cb[0:1, 1:2]), "ac")
                            c.wait(s, "ac")
                            cp = s.activation(rsq[0:1, :], rsq[0:1, :], AF.Exp, scale=-0.5)
                            c.inc(cp, "ac")
                            c.pu_read(ua, "ac")

                        v_rs = c.cnt["ac"]

                        @blk.tensor
                        def _(t, ts4=ts4, ub=ub, v_rs=v_rs):
                            c.wait(t, "ac", v_rs)
                            c.pu_guard(t, ub)
                            c.inc(t.matmul(g.pu[ub][:], g.onesrow[:], rsq[0:1, :],
                                           start=True, stop=True), "pe")

                        v_bc = c.cnt["pe"]

                        @blk.vector
                        def _(v, ts4=ts4, ub=ub, v_bc=v_bc):
                            c.wait(v, "pe", v_bc)
                            for cc in range(2):
                                sl = dst_fm[:, cc * L + ts4 * 512: cc * L + ts4 * 512 + 512]
                                cp = v.tensor_tensor(sl, sl, g.pu[ub][:], op=ALU.mult)
                            c.inc(cp, "dv")
                            c.pu_read(ub, "dv")

            def load_wslab(W_h, gate_pe):
                with nc.Block(f"hd{head}_wld{W_h.name}") as blk:
                    @blk.gpsimd
                    def _(gg):
                        c.wait(gg, "pe", gate_pe)
                        c.inc(gg.dma_start(
                            out=wslab[:],
                            in_=W_h[:, head * DK:(head + 1) * DK].rearrange(
                                "(c p) o -> p c o", p=128)), "dwt", 16)
                return c.cnt["dwt"]

            if c.checkpoint(f"h{head}beta"):
                return True
            emit_prepass(c, g, 48, f"h{head}q")
            v_w = load_wslab(g.Wq_h, 0)
            with nc.Block(f"hd{head}_qw") as blk:
                @blk.tensor
                def _(t, v_w=v_w):
                    c.wait(t, "dwt", v_w)
            proj_part("q", g.cq_sb, 0, q_n_fm, 0)
            proj_part("q", g.cq_sb, 1, q_n_fm, L)
            if c.checkpoint(f"h{head}qproj"):
                return True
            l2norm("q", q_n_fm)
            if c.checkpoint(f"h{head}ql2"):
                return True
            emit_prepass(c, g, 48, f"h{head}k")
            v_w = load_wslab(g.Wk_h, c.cnt["pe"])
            with nc.Block(f"hd{head}_kw") as blk:
                @blk.tensor
                def _(t, v_w=v_w):
                    c.wait(t, "dwt", v_w)
            proj_part("k", g.ck_sb, 0, k_n_fm, 0)
            proj_part("k", g.ck_sb, 1, k_n_fm, L)
            l2norm("k", k_n_fm)
            if c.checkpoint(f"h{head}kl2"):
                return True
            for cc in range(2):
                tr_fm_to_tm(c, g, f"hd{head}_ktm{cc}",
                            lambda tt, cc=cc: k_n_fm[:, cc * L + tt * 128: cc * L + tt * 128 + 128],
                            lambda tt, cc=cc: k_tm[:, tt * DK + cc * 128: tt * DK + cc * 128 + 128])

            c.tap(f"h{head}_q_n_fm", q_n_fm[:], "dv")
            c.tap(f"h{head}_k_n_fm", k_n_fm[:], "dv")
            c.flush_taps()
            if c.checkpoint(f"h{head}ktm"):
                return True

            try:
                _dp = delta_prep
            except NameError:
                pass
            delta_prep(c, g, head, dict(
                q_n_fm=q_n_fm, k_n_fm=k_n_fm, k_tm=k_tm, beta_bf=beta_bf,
                Wt_sb=Wt_sb, ALt_sb=ALt_sb, iTb=iTb,
                iT=iT, iT2=iT2, Pt=Pt, Qt=Qt, Pt2=Pt2, Qt2=Qt2,
                b_cb=b_cb, b_rb=b_rb, qfac=qfac))

            c.tap(f"h{head}_iTb", iTb[:], "dv")
            c.tap(f"h{head}_Wt_sb", Wt_sb[:], "dv")
            c.tap(f"h{head}_ALt_sb", ALt_sb[:], "dv")
            c.flush_taps()
            if c.checkpoint(f"h{head}a"):
                return True
            fence(c, nc, f"hd{head}_fa")

        with contextlib.ExitStack() as pb:
            def sbb(name, shape, dtype):
                return pb.enter_context(nc.sbuf_tensor(f"h{head}b_{name}", shape, dtype))

            wslab = sbb("wslab", [128, NC * DK], BF16)
            P_buf = sbb("P_buf", [128, 3 + L], BF16)
            cacc = sbb("cacc", [128, L], BF16)
            v_fm = sbb("v_fm", [128, 63 + L], BF16)
            flA = sbb("flA", [128, L], BF16)

            emit_prepass(c, g, 48, f"h{head}v")
            with nc.Block(f"hd{head}_vw") as blk:
                @blk.gpsimd
                def _(gg):
                    c.wait(gg, "pe")
                    c.inc(gg.dma_start(
                        out=wslab[:],
                        in_=g.Wv_h[:, head * DK:(head + 1) * DK].rearrange(
                            "(c p) o -> p c o", p=128)), "dwt", 16)

                v_w = c.cnt["dwt"]

                @blk.tensor
                def _(t, v_w=v_w):
                    c.wait(t, "dwt", v_w)

            for cc in range(2):
                gc = 2 * head + cc
                with nc.Block(f"hd{head}_v{cc}_pad") as blk:
                    @blk.vector
                    def _(v):
                        c.inc(v.memset(v_fm[:, 0:63], 0.0), "dv")

                for ts4 in range(4):
                    u = ts4 % 2
                    with nc.Block(f"hd{head}_v{cc}_mm{ts4}") as blk:
                        @blk.tensor
                        def _(t, ts4=ts4, u=u, cc=cc):
                            c.pu_guard(t, u)
                            for k in range(NC):
                                mm = t.matmul(g.pu[u][:],
                                              wslab[:, k * DK + cc * 128: k * DK + cc * 128 + 128],
                                              g.x_ln_fm[:, k * L + ts4 * 512: k * L + ts4 * 512 + 512],
                                              start=(k == 0), stop=(k == NC - 1))
                            c.inc(mm, "pe")

                        v_mm = c.cnt["pe"]

                        @blk.scalar
                        def _(s, ts4=ts4, u=u, v_mm=v_mm):
                            c.wait(s, "pe", v_mm)
                            if ts4 == 0:
                                c.wait(s, "dv")
                            cp = s.activation(P_buf[:, 3 + ts4 * 512: 3 + ts4 * 512 + 512],
                                              g.pu[u][:], AF.Copy)
                            c.inc(cp, "ac")
                            c.pu_read(u, "ac")

                v_evac = c.cnt["ac"]
                with nc.Block(f"hd{head}_v{cc}_conv") as blk:
                    @blk.vector
                    def _(v, gc=gc, v_evac=v_evac):
                        c.wait(v, "ac", v_evac)
                        v.memset(P_buf[:, 0:3], 0.0)
                        v.drain()
                        v.tensor_scalar(cacc[:], P_buf[:, 0:L], g.cv_sb[:, gc * 4: gc * 4 + 1],
                                        None, op0=ALU.mult)
                        for j in range(1, 4):
                            v.drain()
                            mm = v.scalar_tensor_tensor(cacc[:], P_buf[:, j: j + L],
                                                        g.cv_sb[:, gc * 4 + j: gc * 4 + j + 1],
                                                        cacc[:], op0=ALU.mult, op1=ALU.add)
                        c.inc(mm, "dv")

                    v_conv = c.cnt["dv"]

                    @blk.scalar
                    def _(s, v_conv=v_conv):
                        c.wait(s, "dv", v_conv)
                        c.wait(s, "pe")
                        c.inc(s.activation(v_fm[:, 63:63 + L], cacc[:], AF.Silu), "ac")

                tr_fm_to_tm(c, g, f"hd{head}_vtm{cc}",
                            lambda tt: v_fm[:, 63 + tt * 128: 63 + tt * 128 + 128],
                            lambda tt, cc=cc: v_tm[:, tt * DK + cc * 128: tt * DK + cc * 128 + 128])

                with nc.Block(f"hd{head}_fir{cc}") as blk:
                    @blk.vector
                    def _(v, gc=gc):
                        c.wait(v, "ac")
                        c.wait(v, "pe")
                        v.tensor_scalar(P_buf[:, 0:L], v_fm[:, 59:59 + L],
                                        g.firs_sb[:, gc * 5: gc * 5 + 1], None, op0=ALU.mult)
                        for j in range(1, 5):
                            v.drain()
                            mm = v.scalar_tensor_tensor(P_buf[:, 0:L], v_fm[:, 59 + j: 59 + j + L],
                                                        g.firs_sb[:, gc * 5 + j: gc * 5 + j + 1],
                                                        P_buf[:, 0:L], op0=ALU.mult, op1=ALU.add)
                        c.inc(mm, "dv")
                        for blk4 in range(4):
                            dst = flA if blk4 == 0 else cacc
                            j0 = blk4 * 16
                            v.drain()
                            v.tensor_scalar(dst[:], v_fm[:, j0: j0 + L],
                                            g.firl_sb[:, gc * 64 + j0: gc * 64 + j0 + 1],
                                            None, op0=ALU.mult)
                            for j in range(j0 + 1, j0 + 16):
                                v.drain()
                                v.scalar_tensor_tensor(dst[:], v_fm[:, j: j + L],
                                                       g.firl_sb[:, gc * 64 + j: gc * 64 + j + 1],
                                                       dst[:], op0=ALU.mult, op1=ALU.add)
                            if blk4 > 0:
                                v.drain()
                                mm = v.tensor_tensor(flA[:], flA[:], cacc[:], op=ALU.add)
                        c.inc(mm, "dv")

                tr_fm_to_tm(c, g, f"hd{head}_fstm{cc}",
                            lambda tt: P_buf[:, tt * 128: tt * 128 + 128],
                            lambda tt, cc=cc: fs_tm[:, tt * DK + cc * 128: tt * DK + cc * 128 + 128])
                tr_fm_to_tm(c, g, f"hd{head}_fltm{cc}",
                            lambda tt: flA[:, tt * 128: tt * 128 + 128],
                            lambda tt, cc=cc: fl_tm[:, tt * DK + cc * 128: tt * DK + cc * 128 + 128])

            c.tap(f"h{head}_v_tm", v_tm[:], "dv")
            c.tap(f"h{head}_fs_tm", fs_tm[:], "dv")
            c.tap(f"h{head}_fl_tm", fl_tm[:], "dv")
            c.flush_taps()
            if c.checkpoint(f"h{head}b"):
                return True
            fence(c, nc, f"hd{head}_fb")

        with contextlib.ExitStack() as sc_s:
            S_sb = sc_s.enter_context(nc.sbuf_tensor(f"h{head}_S", [128, 512], F32))
            S_bf = sc_s.enter_context(nc.sbuf_tensor(f"h{head}_Sbf", [128, 512], BF16))
            u_sc = sc_s.enter_context(nc.sbuf_tensor(f"h{head}_usc", [128, DK], BF16))
            delta_scan(c, g, head, dict(
                q_n_fm=q_n_fm, k_tm=k_tm, v_tm=v_tm, Wt_sb=Wt_sb, ALt_sb=ALt_sb,
                iTb=iTb, S_sb=S_sb, S_bf=S_bf, u_sc=u_sc, delta_tm=delta_tm))
            fence(c, nc, f"hd{head}_fs2")

        c.tap(f"h{head}_delta_tm", delta_tm[:], "dv")
        c.flush_taps()

        if c.checkpoint(f"h{head}scan"):
            return True

        attn_post(c, g, head, dict(
            v_tm=v_tm, fs_tm=fs_tm, fl_tm=fl_tm, delta_tm=delta_tm))

        if c.checkpoint(f"h{head}post"):
            return True

    fence(c, nc, f"hd{head}_fend")
    return False


def delta_prep(c, g, head, T):
    nc = g.nc
    q_n_fm, k_n_fm, k_tm = T["q_n_fm"], T["k_n_fm"], T["k_tm"]
    beta_bf = T["beta_bf"]
    Wt_sb, ALt_sb, iTb = T["Wt_sb"], T["ALt_sb"], T["iTb"]
    iT, iT2, Pt, Qt, Pt2, Qt2 = T["iT"], T["iT2"], T["Pt"], T["Qt"], T["Pt2"], T["Qt2"]
    b_cb, b_rb, qfac = T["b_cb"], T["b_rb"], T["qfac"]

    def sl(buf, i, w=32):
        j = i % 4
        return buf[32 * j:32 * j + 32, (i // 4) * w:(i // 4) * w + w]

    def mask_bc(col):
        return g.masks[:, col:col + 32].unsqueeze(1).broadcast_to([128, 16, 32])

    def r3(ap):
        return ap.rearrange("p (gr n) -> p gr n", n=32)

    with nc.Block(f"hd{head}_bfac") as blk:
        @blk.tensor
        def _(t):
            c.wait(t, "ac")
            c.pu_guard(t, 2)
            c.pu_guard(t, 3)
            for j in range(4):
                rhs = beta_bf[0:1, :].rearrange("o (gr n) -> o gr n", n=32)[:, j::4, :]
                mm = t.matmul(g.pu[2][32 * j:32 * j + 32, :], g.onesrow_bf[0:1, 0:32],
                              rhs, start=True, stop=True, tile_position=(0, 32 * j))
            c.inc(mm, "pe")
            for i in range(NCH):
                j = i % 4
                mm = t.matmul(g.pu[3][32 * j:32 * j + 32, (i // 4) * 32:(i // 4) * 32 + 32],
                              beta_bf[0:1, i * 32: i * 32 + 32], g.onesrow_bf[0:1, 0:32],
                              start=True, stop=True, tile_position=(0, 32 * j))
            c.inc(mm, "pe")

        v_b = c.cnt["pe"]

        @blk.vector
        def _(v, v_b=v_b):
            c.wait(v, "pe", v_b)
            v.tensor_copy(b_rb[:], g.pu[3][:])
            v.tensor_tensor(r3(b_cb[:]), r3(g.pu[2][:]), mask_bc(0), op=ALU.mult)
            c.inc(v.tensor_tensor(r3(qfac[:]), r3(g.pu[3][:]), mask_bc(32), op=ALU.mult), "dv")
            c.pu_read(2, "dv")
            c.pu_read(3, "dv")

    if c.checkpoint(f"h{head}bfac"):
        raise _Cut()
    with nc.Block(f"hd{head}_kkt") as blk:
        @blk.tensor
        def _(t):
            c.wait(t, "dv")
            c.pu_guard(t, 0)
            for i in range(NCH):
                j = i % 4
                for cc in range(2):
                    mm = t.matmul(sl(g.pu[0], i),
                                  k_n_fm[:, cc * L + i * 32: cc * L + i * 32 + 32],
                                  k_n_fm[:, cc * L + i * 32: cc * L + i * 32 + 32],
                                  start=(cc == 0), stop=(cc == 1), tile_position=(0, 32 * j))
            c.inc(mm, "pe")

        v_kkt = c.cnt["pe"]

        @blk.vector
        def _(v, v_kkt=v_kkt):
            c.wait(v, "pe", v_kkt)
            c.wait(v, "dv")
            v.tensor_tensor(Pt[:], g.pu[0][:], b_cb[:], op=ALU.mult)
            v.tensor_tensor(Qt[:], g.pu[0][:], qfac[:], op=ALU.mult)
            v.drain()
            c.inc(v.tensor_tensor(r3(iT[:]), r3(Pt[:]), mask_bc(96), op=ALU.add), "dv")
            c.pu_read(0, "dv")

    if c.checkpoint(f"h{head}kkt"):
        raise _Cut()
    P_cur, Q_cur, i_cur = Pt, Qt, iT
    P_nxt, Q_nxt, i_nxt = Pt2, Qt2, iT2
    for it in range(1, 5):
        with nc.Block(f"hd{head}_inv{it}") as blk:
            @blk.tensor
            def _(t, it=it, P_cur=P_cur, Q_cur=Q_cur):
                c.wait(t, "dv")
                c.pu_guard(t, 0)
                c.pu_guard(t, 1)
                if it < 4:
                    for i in range(NCH):
                        j = i % 4
                        mm = t.matmul(sl(g.pu[0], i), sl(Q_cur, i), sl(P_cur, i),
                                      start=True, stop=True, tile_position=(32 * j, 32 * j))
                    c.inc(mm, "pe")
                for i in range(NCH):
                    j = i % 4
                    mm = t.matmul(sl(g.pu[1], i), sl(P_cur, i), sl(Q_cur, i),
                                  start=True, stop=True, tile_position=(32 * j, 32 * j))
                c.inc(mm, "pe")

            v_sq = c.cnt["pe"]

            @blk.vector
            def _(v, it=it, P_nxt=P_nxt, Q_nxt=Q_nxt, v_sq=v_sq):
                c.wait(v, "pe", v_sq)
                if it < 4:
                    v.tensor_copy(P_nxt[:], g.pu[0][:])
                c.inc(v.tensor_copy(Q_nxt[:], g.pu[1][:]), "dv")
                c.pu_read(0, "dv")
                c.pu_read(1, "dv")

            v_pq = c.cnt["dv"]

            @blk.tensor
            def _(t, Q_nxt=Q_nxt, i_cur=i_cur, v_pq=v_pq):
                c.wait(t, "dv", v_pq)
                c.pu_guard(t, 2)
                for i in range(NCH):
                    j = i % 4
                    mm = t.matmul(sl(g.pu[2], i), sl(Q_nxt, i), sl(i_cur, i),
                                  start=True, stop=True, tile_position=(32 * j, 32 * j))
                c.inc(mm, "pe")

            v_up = c.cnt["pe"]

            @blk.vector
            def _(v, i_cur=i_cur, i_nxt=i_nxt, v_up=v_up):
                c.wait(v, "pe", v_up)
                c.wait(v, "dv")
                c.inc(v.tensor_tensor(i_nxt[:], g.pu[2][:], i_cur[:], op=ALU.add), "dv")
                c.pu_read(2, "dv")

        P_cur, P_nxt = P_nxt, P_cur
        Q_cur, Q_nxt = Q_nxt, Q_cur
        i_cur, i_nxt = i_nxt, i_cur

    if c.checkpoint(f"h{head}inv"):
        raise _Cut()
    with nc.Block(f"hd{head}_itb") as blk:
        @blk.vector
        def _(v, i_cur=i_cur):
            c.wait(v, "dv")
            c.inc(v.tensor_tensor(iTb[:], i_cur[:], b_rb[:], op=ALU.mult), "dv")

    v_itb = c.cnt["dv"]

    if c.checkpoint(f"h{head}itb"):
        raise _Cut()
    with nc.Block(f"hd{head}_alt") as blk:
        @blk.tensor
        def _(t):
            c.pu_guard(t, 0)
            for i in range(NCH):
                j = i % 4
                for cc in range(2):
                    mm = t.matmul(sl(g.pu[0], i),
                                  k_n_fm[:, cc * L + i * 32: cc * L + i * 32 + 32],
                                  q_n_fm[:, cc * L + i * 32: cc * L + i * 32 + 32],
                                  start=(cc == 0), stop=(cc == 1), tile_position=(0, 32 * j))
            c.inc(mm, "pe")

        v_al = c.cnt["pe"]

        @blk.vector
        def _(v, v_al=v_al):
            c.wait(v, "pe", v_al)
            c.inc(v.tensor_tensor(r3(ALt_sb[:]), r3(g.pu[0][:]), mask_bc(64), op=ALU.mult), "dv")
            c.pu_read(0, "dv")

    if c.checkpoint(f"h{head}alt"):
        raise _Cut()
    for hf in range(2):
        for sbi in range(4):
            u = 1 + (hf * 4 + sbi) % 2
            with nc.Block(f"hd{head}_wt{hf}_{sbi}") as blk:
                @blk.tensor
                def _(t, hf=hf, sbi=sbi, u=u, v_itb=v_itb):
                    if hf == 0 and sbi == 0:
                        c.wait(t, "dv", v_itb)
                    c.pu_guard(t, u)
                    prev_j = None
                    for ii in range(16):
                        i = sbi * 16 + ii
                        j = i % 4
                        if prev_j is not None and j != prev_j:
                            t.drain()
                        prev_j = j
                        mm = t.matmul(g.pu[u][:, ii * 32: ii * 32 + 32],
                                      k_tm[32 * j:32 * j + 32,
                                           (i // 4) * DK + hf * 128: (i // 4) * DK + hf * 128 + 128],
                                      sl(iTb, i),
                                      start=True, stop=True, tile_position=(32 * j, 0))
                    c.inc(mm, "pe")

                v_w = c.cnt["pe"]

                @blk.vector
                def _(v, hf=hf, sbi=sbi, u=u, v_w=v_w):
                    c.wait(v, "pe", v_w)
                    c.inc(v.tensor_scalar(
                        Wt_sb[:, hf * (NCH * 32) + sbi * 512: hf * (NCH * 32) + sbi * 512 + 512],
                        g.pu[u][:], -1.0, None, op0=ALU.mult), "dv")
                    c.pu_read(u, "dv")


def delta_scan(c, g, head, T):
    nc = g.nc
    q_n_fm, k_tm, v_tm = T["q_n_fm"], T["k_tm"], T["v_tm"]
    Wt_sb, ALt_sb, iTb = T["Wt_sb"], T["ALt_sb"], T["iTb"]
    S_sb, S_bf, u_sc, delta_tm = T["S_sb"], T["S_bf"], T["u_sc"], T["delta_tm"]

    with nc.Block(f"hd{head}_scan_init") as blk:
        @blk.vector
        def _(v):
            v.memset(S_sb[:], 0.0)
            c.inc(v.memset(S_bf[:], 0.0), "dv")

    pe_b = c.cnt["pe"]
    dv_b = c.cnt["dv"]
    with nc.Block(f"hd{head}_scan") as blk:
        @blk.tensor
        def _(t):
            c.pu_guard(t, 5)
            c.pu_guard(t, 3)
            c.pu_guard(t, 6)
            c.pu_guard(t, 7)
            for i in range(NCH):
                j = i % 4
                tt = i // 4
                uo = 6 + tt % 2
                c.wait(t, "dv", dv_b + 2 * i)
                t.matmul(g.pu[5][32 * j:32 * j + 32, 0:DK],
                         iTb[32 * j:32 * j + 32, (i // 4) * 32: (i // 4) * 32 + 32],
                         v_tm[32 * j:32 * j + 32, tt * DK: tt * DK + DK],
                         start=True, stop=False, tile_position=(32 * j, 32 * j))
                for hf in range(2):
                    mm = t.matmul(g.pu[5][32 * j:32 * j + 32, 0:DK],
                                  Wt_sb[:, hf * (NCH * 32) + i * 32: hf * (NCH * 32) + i * 32 + 32],
                                  S_bf[:, hf * DK: hf * DK + DK],
                                  start=False, stop=(hf == 1), tile_position=(0, 32 * j))
                c.inc(mm, "pe")
                c.wait(t, "dv", dv_b + 2 * i + 1)
                if j == 0 and tt >= 2:
                    c.wait(t, "dv", dv_b + 2 * (4 * (tt - 2) + 3) + 2)
                for hf in range(2):
                    t.matmul(g.pu[uo][32 * j:32 * j + 32, 0:DK],
                             q_n_fm[:, hf * L + i * 32: hf * L + i * 32 + 32],
                             S_bf[:, hf * DK: hf * DK + DK],
                             start=(hf == 0), stop=False, tile_position=(0, 32 * j))
                mm = t.matmul(g.pu[uo][32 * j:32 * j + 32, 0:DK],
                              ALt_sb[32 * j:32 * j + 32, tt * 32: tt * 32 + 32],
                              u_sc[32 * j:32 * j + 32, :],
                              start=False, stop=True, tile_position=(32 * j, 32 * j))
                c.inc(mm, "pe")
                t.drain()
                for hf in range(2):
                    mm = t.matmul(g.pu[3][:, hf * DK: hf * DK + DK],
                                  k_tm[32 * j:32 * j + 32, tt * DK + hf * 128: tt * DK + hf * 128 + 128],
                                  u_sc[32 * j:32 * j + 32, :],
                                  start=True, stop=True, tile_position=(32 * j, 0))
                c.inc(mm, "pe")

        @blk.vector
        def _(v):
            for i in range(NCH):
                j = i % 4
                tt = i // 4
                uo = 6 + tt % 2
                c.wait(v, "pe", pe_b + 3 * i + 1)
                c.inc(v.tensor_copy(u_sc[32 * j:32 * j + 32, :],
                                    g.pu[5][32 * j:32 * j + 32, 0:DK]), "dv")
                c.wait(v, "pe", pe_b + 3 * i + 3)
                v.tensor_tensor(S_sb[:], g.pu[3][:], S_sb[:], op=ALU.add)
                v.drain()
                cp = v.tensor_copy(S_bf[:], S_sb[:])
                if j == 3:
                    cp = v.tensor_copy(delta_tm[:, tt * DK: tt * DK + DK], g.pu[uo][:, 0:DK])
                c.inc(cp, "dv")

    v_sadd = c.cnt["dv"]
    c.pu_read(5, "dv", v_sadd)
    c.pu_read(3, "dv", v_sadd)


def attn_post(c, g, head, T):
    nc = g.nc
    v_tm, fs_tm, fl_tm, delta_tm = T["v_tm"], T["fs_tm"], T["fl_tm"], T["delta_tm"]
    tensors = (fs_tm, fl_tm, delta_tm, v_tm)

    with contextlib.ExitStack() as ps:
        def sb(name, shape, dtype):
            return ps.enter_context(nc.sbuf_tensor(f"h{head}p_{name}", shape, dtype))

        stats_fm = sb("stats_fm", [16, L], BF16)
        raw = sb("raw", [128, NT * 12], F32)
        stats_tm = sb("stats_tm", [128, NT * 16], F32)
        st_junk = sb("st_junk", [128, DK], BF16)
        st_scr = sb("st_scr", [128, 8], F32)
        h1h = sb("h1h", [128, NC * 512], BF16)
        w1m = sb("w1m", [128, NC * D], BF16)
        logits_sb = sb("logits_sb", [4, L], F32)
        p_scr = sb("p_scr", [128, NT * 4], F32)
        sm_scr = sb("sm_scr", [128, 8], F32)
        o_acc = sb("o_acc", [128, DK], F32)
        o_n = sb("o_n", [128, DK], BF16)

        v_scan = c.cnt["dv"]
        with nc.Block(f"hd{head}_stats_raw") as blk:
            @blk.vector
            def _(v):
                c.wait(v, "dv", v_scan)
                for ti in range(NT):
                    for x in range(4):
                        src = tensors[x][:, ti * DK: ti * DK + DK]
                        v.tensor_reduce(raw[:, ti * 12 + x: ti * 12 + x + 1], src,
                                        axis=AX.X, op=ALU.add)
                        mm = v.tensor_reduce(raw[:, ti * 12 + 8 + x: ti * 12 + 8 + x + 1], src,
                                             axis=AX.X, op=ALU.add, apply_absolute_value=True)
                c.inc(mm, "dv")

            @blk.scalar
            def _(s):
                c.wait(s, "dv", v_scan)
                for ti in range(NT):
                    for x in range(4):
                        src = tensors[x][:, ti * DK: ti * DK + DK]
                        if ti or x:
                            s.drain()
                        mm = s.activation(st_junk[:], src, AF.Square,
                                          accum_out=raw[:, ti * 12 + 4 + x: ti * 12 + 4 + x + 1])
                c.inc(mm, "ac")

        v_raw_v = c.cnt["dv"]
        v_raw_s = c.cnt["ac"]

        with nc.Block(f"hd{head}_stats_drv") as blk:
            @blk.vector
            def _(v):
                c.wait(v, "dv", v_raw_v)
                c.wait(v, "ac", v_raw_s)
                for ti in range(NT):
                    sums = raw[:, ti * 12: ti * 12 + 4]
                    sqs = raw[:, ti * 12 + 4: ti * 12 + 8]
                    abss = raw[:, ti * 12 + 8: ti * 12 + 12]
                    st = stats_tm[:, ti * 16: ti * 16 + 16].rearrange("p (x f) -> p f x", f=4)
                    v.tensor_scalar(st[:, 0:1, :], sums, 1.0 / DK, None, op0=ALU.mult)
                    mm = v.tensor_scalar(st[:, 2:3, :], abss, 1.0 / DK, None, op0=ALU.mult)
                    v.tensor_tensor(st_scr[:, 0:4], sums, sums, op=ALU.mult)
                    v.drain()
                    v.scalar_tensor_tensor(st_scr[:, 4:8], st_scr[:, 0:4], -1.0 / DK, sqs,
                                           op0=ALU.mult, op1=ALU.add)
                    v.drain()
                    mm = v.tensor_scalar(st[:, 1:2, :], st_scr[:, 4:8], 1.0 / DK, None,
                                         op0=ALU.mult)
                c.inc(mm, "dv")

            v_drv = c.cnt["dv"]

            @blk.scalar
            def _(s, v_drv=v_drv):
                c.wait(s, "dv", v_drv)
                for ti in range(NT):
                    sqs = raw[:, ti * 12 + 4: ti * 12 + 8]
                    st = stats_tm[:, ti * 16: ti * 16 + 16].rearrange("p (x f) -> p f x", f=4)
                    mm = s.activation(st[:, 3:4, :], sqs, AF.Sqrt)
                c.inc(mm, "ac")

        v_stats = c.cnt["ac"]

        for grp in range(4):
            u = grp % 2
            with nc.Block(f"hd{head}_sttr{grp}") as blk:
                @blk.tensor
                def _(t, grp=grp, u=u):
                    if grp == 0:
                        c.wait(t, "ac", v_stats)
                        c.wait(t, "dv")
                    c.pu_guard(t, u)
                    for q in range(4):
                        ti = grp * 4 + q
                        mm = t.transpose(g.pu[u][0:16, q * 128: q * 128 + 128],
                                         stats_tm[:, ti * 16: ti * 16 + 16], g.identf[:])
                    c.inc(mm, "pe")

                v_tr = c.cnt["pe"]

                @blk.vector
                def _(v, grp=grp, u=u, v_tr=v_tr):
                    c.wait(v, "pe", v_tr)
                    c.inc(v.tensor_copy(stats_fm[:, grp * 512: grp * 512 + 512],
                                        g.pu[u][0:16, :]), "dv")
                    c.pu_read(u, "dv")

        emit_prepass(c, g, 48, f"h{head}p")
        with nc.Block(f"hd{head}_w1ld") as blk:
            @blk.gpsimd
            def _(gg):
                c.wait(gg, "pe")
                c.wait(gg, "dv")
                c.inc(gg.dma_start(out=w1m[:],
                                   in_=g.W1_h[0:1024, :].rearrange("(c p) o -> p c o", p=128)),
                      "dwt", 16)

        v_w1m = c.cnt["dwt"]

        for ts4 in range(4):
            for c2 in range(NC):
                u = 2 + c2 % 2
                with nc.Block(f"hd{head}_gm{ts4}_{c2}") as blk:
                    @blk.tensor
                    def _(t, ts4=ts4, c2=c2, u=u):
                        c.pu_guard(t, u)
                        if ts4 == 0 and c2 == 0:
                            c.wait(t, "dwt", v_w1m)
                            c.wait(t, "dv")
                        t.matmul(g.pu[u][:], g.w1s_sb[:, c2 * 128: c2 * 128 + 128],
                                 stats_fm[:, ts4 * 512: ts4 * 512 + 512],
                                 start=True, stop=False)
                        for k in range(NC):
                            mm = t.matmul(g.pu[u][:],
                                          w1m[:, k * D + c2 * 128: k * D + c2 * 128 + 128],
                                          g.x_ln_fm[:, k * L + ts4 * 512: k * L + ts4 * 512 + 512],
                                          start=False, stop=(k == NC - 1))
                        c.inc(mm, "pe")

                    v_mm = c.cnt["pe"]

                    @blk.scalar
                    def _(s, ts4=ts4, c2=c2, u=u, v_mm=v_mm):
                        c.wait(s, "pe", v_mm)
                        c.inc(s.activation(h1h[:, c2 * 512: c2 * 512 + 512], g.pu[u][:],
                                           AF.Gelu, bias=g.b1_sb[:, c2: c2 + 1]), "ac")
                        c.pu_read(u, "ac")

            v_h1 = c.cnt["ac"]
            with nc.Block(f"hd{head}_gw2_{ts4}") as blk:
                @blk.tensor
                def _(t, ts4=ts4, v_h1=v_h1):
                    c.wait(t, "ac", v_h1)
                    c.pu_guard(t, 4)
                    for c2 in range(NC):
                        mm = t.matmul(g.pu[4][0:4, 0:512],
                                      g.w2_sb[:, c2 * 4: c2 * 4 + 4],
                                      h1h[:, c2 * 512: c2 * 512 + 512],
                                      start=(c2 == 0), stop=(c2 == NC - 1))
                    c.inc(mm, "pe")

                v_lg = c.cnt["pe"]

                @blk.vector
                def _(v, ts4=ts4, v_lg=v_lg):
                    c.wait(v, "pe", v_lg)
                    c.inc(v.tensor_scalar(logits_sb[0:4, ts4 * 512: ts4 * 512 + 512],
                                          g.pu[4][0:4, 0:512],
                                          g.gb_sb[:, head: head + 1],
                                          g.invtemp4[:, head: head + 1],
                                          op0=ALU.add, op1=ALU.mult), "dv")
                    c.pu_read(4, "dv")

        v_logits = c.cnt["dv"]
        with nc.Block(f"hd{head}_ptr") as blk:
            @blk.tensor
            def _(t):
                c.wait(t, "dv", v_logits)
                c.pu_guard(t, 5)
                for ti in range(NT):
                    mm = t.transpose(g.pu[5][:, ti * 4: ti * 4 + 4],
                                     logits_sb[0:4, ti * 128: ti * 128 + 128],
                                     g.identf[0:4, 0:4])
                c.inc(mm, "pe")

            v_tr = c.cnt["pe"]

            @blk.vector
            def _(v, v_tr=v_tr):
                c.wait(v, "pe", v_tr)
                c.inc(v.tensor_copy(p_scr[:], g.pu[5][:, 0:64]), "dv")
                c.pu_read(5, "dv")

        with nc.Block(f"hd{head}_sm") as blk:
            dv_b, ac_b = c.cnt["dv"], c.cnt["ac"]

            @blk.vector
            def _(v, dv_b=dv_b, ac_b=ac_b):
                for ti in range(NT):
                    if ti == 0:
                        c.wait(v, "dv", dv_b)
                    c.inc(v.tensor_reduce(sm_scr[:, 0:1], p_scr[:, ti * 4: ti * 4 + 4],
                                          axis=AX.X, op=ALU.max, negate=True), "dv")
                    c.wait(v, "ac", ac_b + ti + 1)
                    v.drain()
                    v.tensor_reduce(sm_scr[:, 1:2], p_scr[:, ti * 4: ti * 4 + 4],
                                    axis=AX.X, op=ALU.add)
                    v.drain()
                    v.reciprocal(sm_scr[:, 2:3], sm_scr[:, 1:2])
                    v.drain()
                    v.tensor_scalar(p_scr[:, ti * 4: ti * 4 + 4],
                                    p_scr[:, ti * 4: ti * 4 + 4],
                                    sm_scr[:, 2:3], None, op0=ALU.mult)
                    v.drain()
                    v.tensor_tensor(p_scr[:, ti * 4: ti * 4 + 4],
                                    p_scr[:, ti * 4: ti * 4 + 4], g.clampt[:], op=ALU.max)
                    v.drain()
                    v.tensor_reduce(sm_scr[:, 3:4], p_scr[:, ti * 4: ti * 4 + 4],
                                    axis=AX.X, op=ALU.add)
                    v.drain()
                    v.reciprocal(sm_scr[:, 4:5], sm_scr[:, 3:4])
                    v.drain()
                    v.tensor_scalar(p_scr[:, ti * 4: ti * 4 + 4],
                                    p_scr[:, ti * 4: ti * 4 + 4],
                                    sm_scr[:, 4:5], None, op0=ALU.mult)
                    v.drain()
                    c.inc(v.tensor_tensor(p_scr[:, ti * 4: ti * 4 + 1],
                                          p_scr[:, ti * 4: ti * 4 + 1],
                                          g.crb[:, head: head + 1], op=ALU.add), "dv")

            @blk.scalar
            def _(s, dv_b=dv_b):
                for ti in range(NT):
                    c.wait(s, "dv", dv_b + 2 * ti + 1)
                    c.inc(s.activation(p_scr[:, ti * 4: ti * 4 + 4],
                                       p_scr[:, ti * 4: ti * 4 + 4],
                                       AF.Exp, bias=sm_scr[:, 0:1]), "ac")

        c.tap(f"h{head}_p", p_scr[:], "dv")
        c.flush_taps()

        pre_rd = [c.pu_last.get(0), c.pu_last.get(1)]
        with nc.Block(f"hd{head}_mix") as blk:
            dv_b, ac_b, pe_b = c.cnt["dv"], c.cnt["ac"], c.cnt["pe"]

            @blk.vector
            def _(v, dv_b=dv_b, ac_b=ac_b, pe_b=pe_b):
                for ti in range(NT):
                    if ti == 0:
                        c.wait(v, "dv", dv_b)
                    c.wait(v, "ac", ac_b + 2 * ti)
                    v.tensor_scalar(o_acc[:], fs_tm[:, ti * DK: ti * DK + DK],
                                    p_scr[:, ti * 4: ti * 4 + 1], None, op0=ALU.mult)
                    for x, src in ((1, fl_tm), (2, delta_tm), (3, v_tm)):
                        v.drain()
                        mm = v.scalar_tensor_tensor(o_acc[:], src[:, ti * DK: ti * DK + DK],
                                                    p_scr[:, ti * 4 + x: ti * 4 + x + 1],
                                                    o_acc[:], op0=ALU.mult, op1=ALU.add)
                    c.inc(mm, "dv")
                    c.wait(v, "ac", ac_b + 2 * ti + 2)
                    c.wait(v, "pe", pe_b + ti)
                    v.drain()
                    v.reciprocal(sm_scr[:, 7:8], sm_scr[:, 6:7])
                    v.drain()
                    c.inc(v.scalar_tensor_tensor(o_n[:], o_acc[:], sm_scr[:, 7:8],
                                                 g.rmsw_bc[:], op0=ALU.mult, op1=ALU.mult),
                          "dv")
                    c.wait(v, "pe", pe_b + ti + 1)
                    u = ti % 2
                    for cc in range(2):
                        gc = 2 * head + cc
                        mm = v.tensor_copy(g.o_fm[:, gc * L + ti * 128: gc * L + ti * 128 + 128],
                                           g.pub[u][:, cc * 128: cc * 128 + 128])
                    c.inc(mm, "dv")
                    c.pu_read(u, "dv")

            @blk.scalar
            def _(s, dv_b=dv_b):
                for ti in range(NT):
                    c.wait(s, "dv", dv_b + 3 * ti + 1)
                    if ti > 0:
                        c.wait(s, "ac")
                    c.inc(s.activation(st_junk[:], o_acc[:], AF.Square,
                                       accum_out=sm_scr[:, 5:6]), "ac")
                    c.wait(s, "ac")
                    c.inc(s.activation(sm_scr[:, 6:7], sm_scr[:, 5:6], AF.Sqrt,
                                       scale=1.0 / DK, bias=g.cb[:, 2:3]), "ac")

            @blk.tensor
            def _(t, dv_b=dv_b, pe_b=pe_b, pre_rd=pre_rd):
                for ti in range(NT):
                    c.wait(t, "dv", dv_b + 3 * ti + 2)
                    if ti < 2 and pre_rd[ti] is not None:
                        c.wait(t, pre_rd[ti][0], pre_rd[ti][1])
                    u = ti % 2
                    for cc in range(2):
                        mm = t.transpose(g.pub[u][:, cc * 128: cc * 128 + 128],
                                         o_n[:, cc * 128: cc * 128 + 128], g.identb[:])
                    c.inc(mm, "pe")



def wo_residual(c, g):
    nc = g.nc
    fence(c, nc, "wo_fence")
    with contextlib.ExitStack() as ws:
        def sb(name, shape, dtype):
            return ws.enter_context(nc.sbuf_tensor(f"wo_{name}", shape, dtype))

        wo_sb = sb("w", [128, NC * D], BF16)
        x_slab = [sb(f"xs{i}", [128, 4 * D], F32) for i in range(2)]
        ao_fm = sb("ao", [128, 512], BF16)

        with nc.Block("wo_ld") as blk:
            @blk.gpsimd
            def _(gg):
                c.inc(gg.dma_start(out=wo_sb[:],
                                   in_=g.Wo_h.rearrange("(c p) o -> p c o", p=128)), "dwt", 16)

        v_wo = c.cnt["dwt"]
        xs_used = []
        for ts4 in range(4):
            with nc.Block(f"wo_x{ts4}") as blk:
                @blk.sync
                def _(sy, ts4=ts4):
                    if ts4 >= 2:
                        c.wait(sy, "dv", xs_used[ts4 - 2])
                    c.inc(sy.dma_start(
                        out=x_slab[ts4 % 2][:],
                        in_=g.x_hbm[ts4 * 512:(ts4 + 1) * 512, :].rearrange(
                            "(t p) d -> p t d", p=128)), "dx0" if ts4 % 2 == 0 else "dx1", 16)

            wxs = "dx0" if ts4 % 2 == 0 else "dx1"
            v_xs = c.cnt[wxs]
            for co in range(NC):
                u = co % 2
                ub = 2 + co % 2
                with nc.Block(f"wo_{ts4}_{co}") as blk:
                    @blk.tensor
                    def _(t, ts4=ts4, co=co, u=u):
                        if ts4 == 0 and co == 0:
                            c.wait(t, "dwt", v_wo)
                            c.wait(t, "dv")
                        c.pu_guard(t, u)
                        for k in range(NC):
                            mm = t.matmul(g.pu[u][:],
                                          wo_sb[:, k * D + co * 128: k * D + co * 128 + 128],
                                          g.o_fm[:, k * L + ts4 * 512: k * L + ts4 * 512 + 512],
                                          start=(k == 0), stop=(k == NC - 1))
                        c.inc(mm, "pe")

                    v_mm = c.cnt["pe"]

                    @blk.scalar
                    def _(s, u=u, v_mm=v_mm):
                        c.wait(s, "pe", v_mm)
                        c.wait(s, "dv")
                        c.inc(s.activation(ao_fm[:], g.pu[u][:], AF.Copy), "ac")
                        c.pu_read(u, "ac")

                    v_ev = c.cnt["ac"]

                    @blk.tensor
                    def _(t, ub=ub, v_ev=v_ev):
                        c.wait(t, "ac", v_ev)
                        c.pu_guard(t, ub)
                        for q in range(4):
                            mm = t.transpose(g.pub[ub][:, q * 128: q * 128 + 128],
                                             ao_fm[:, q * 128: q * 128 + 128], g.identb[:])
                        c.inc(mm, "pe")

                    v_tr = c.cnt["pe"]

                    @blk.vector
                    def _(v, ts4=ts4, co=co, ub=ub, v_tr=v_tr, v_xs=v_xs, wxs=wxs):
                        c.wait(v, "pe", v_tr)
                        c.wait(v, wxs, v_xs)
                        for q in range(4):
                            tt = ts4 * 4 + q
                            mm = v.tensor_tensor(
                                g.x2_tm[:, tt * D + co * 128: tt * D + co * 128 + 128],
                                g.pub[ub][:, q * 128: q * 128 + 128],
                                x_slab[ts4 % 2][:, q * D + co * 128: q * D + co * 128 + 128],
                                op=ALU.add)
                        c.inc(mm, "dv")
                        c.pu_read(ub, "dv")

            xs_used.append(c.cnt["dv"])


def moe(c, g):
    nc = g.nc
    fence(c, nc, "moe_fence")
    with contextlib.ExitStack() as ms:
        def sb(name, shape, dtype):
            return ms.enter_context(nc.sbuf_tensor(f"moe_{name}", shape, dtype, side="right"))

        xl2_rot = [sb(f"xl{i}", [128, D], BF16) for i in range(2)]
        ln_scr = sb("lnscr", [128, 8], F32)
        sq_junk = sb("sqj", [128, D], BF16)
        xl2_fm = sb("xfm", [128, NC * SLAB], BF16)
        lg_sb = sb("lg", [8, SLAB], F32)
        lg_tm = sb("lgtm", [128, 8 * 8], F32)
        g_tm = sb("gtm", [128, 8 * 8], F32)
        m8 = sb("m8", [128, 8], F32)
        e1 = sb("e1", [128, 8], F32)
        cum = sb("cum", [128, 8], F32)
        pr2 = sb("pr2", [128, 8], F32)
        ones8 = sb("ones8", [128, 8], F32)
        gsc = sb("gsc", [128, 8], F32)
        acc = sb("acc", [128, 8 * D], F32)
        h_sb = sb("h", [128, NF * SLAB], BF16)
        NWB = 4
        wvb = [sb(f"wv{i}", [128, NC * 128], BF16) for i in range(NWB)]
        wb_ = [sb(f"wb{i}", [128, NC * 128], BF16) for i in range(NWB)]
        w1b = [sb(f"w1b{i}", [128, D], BF16) for i in range(NWB)]
        ysc = sb("ysc", [128, 512], BF16)
        s1b = sb("s1b", [128, D], F32)
        s1e = sb("s1e", [1, D], F32)

        with nc.Block("moe_init") as blk:
            @blk.vector
            def _(v):
                c.inc(v.memset(ones8[:], 1.0), "dv")

        v_store = None
        for slab in range(NSLAB):
            ln_done = []
            tr_done = []
            ev_done = []
            for q in range(8):
                tt = slab * 8 + q
                with nc.Block(f"m_ln{slab}_{q}") as blk:
                    @blk.vector
                    def _(v, q=q, tt=tt):
                        if q == 0 and slab > 0:
                            c.wait(v, "dout", v_store)
                        c.inc(v.tensor_reduce(ln_scr[:, 0:1],
                                              g.x2_tm[:, tt * D: tt * D + D],
                                              axis=AX.X, op=ALU.add), "dv")

                    @blk.scalar
                    def _(s, q=q, tt=tt):
                        c.inc(s.activation(sq_junk[:], g.x2_tm[:, tt * D: tt * D + D],
                                           AF.Square, accum_out=ln_scr[:, 1:2]), "ac")

                    v_sq = c.cnt["ac"]

                    @blk.vector
                    def _(v, v_sq=v_sq):
                        c.wait(v, "ac", v_sq)
                        c.wait(v, "dv")
                        v.tensor_tensor(ln_scr[:, 2:3], ln_scr[:, 0:1], ln_scr[:, 0:1], op=ALU.mult)
                        v.drain()
                        c.inc(v.scalar_tensor_tensor(ln_scr[:, 3:4], ln_scr[:, 2:3], -1.0 / D,
                                                     ln_scr[:, 1:2], op0=ALU.mult, op1=ALU.add), "dv")

                    v_var = c.cnt["dv"]

                    @blk.scalar
                    def _(s, v_var=v_var):
                        c.wait(s, "dv", v_var)
                        c.inc(s.activation(ln_scr[:, 4:5], ln_scr[:, 3:4], AF.Sqrt,
                                           scale=1.0 / D, bias=g.cb[:, 2:3]), "ac")

                    v_std = c.cnt["ac"]

                    @blk.vector
                    def _(v, q=q, tt=tt, v_std=v_std):
                        c.wait(v, "ac", v_std)
                        v.reciprocal(ln_scr[:, 5:6], ln_scr[:, 4:5])
                        v.tensor_scalar(ln_scr[:, 6:7], ln_scr[:, 0:1], -1.0 / D, None, op0=ALU.mult)
                        v.drain()
                        if q >= 2:
                            c.wait(v, "pe", tr_done[q - 2])
                        ln_done.append(c.inc(v.tensor_scalar(xl2_rot[q % 2][:],
                                                             g.x2_tm[:, tt * D: tt * D + D],
                                                             ln_scr[:, 6:7], ln_scr[:, 5:6],
                                                             op0=ALU.add, op1=ALU.mult), "dv"))

                    v_xl = c.cnt["dv"]

                    @blk.tensor
                    def _(t, q=q, v_xl=v_xl):
                        c.wait(t, "dv", v_xl)
                        u = q % 2
                        c.pu_guard(t, u)
                        c.pu_guard(t, 2 + u)
                        for k in range(NC):
                            pb = g.pub[u] if k < 4 else g.pub[2 + u]
                            mm = t.transpose(pb[:, (k % 4) * 128:(k % 4) * 128 + 128],
                                             xl2_rot[q % 2][:, k * 128:(k + 1) * 128], g.identb[:])
                        tr_done.append(c.inc(mm, "pe"))

                    v_tr = c.cnt["pe"]

                    @blk.vector
                    def _(v, q=q, v_tr=v_tr):
                        c.wait(v, "pe", v_tr)
                        u = q % 2
                        for k in range(NC):
                            pb = g.pub[u] if k < 4 else g.pub[2 + u]
                            mm = v.tensor_copy(
                                xl2_fm[:, k * SLAB + q * 128: k * SLAB + q * 128 + 128],
                                pb[:, (k % 4) * 128:(k % 4) * 128 + 128])
                        ev_done.append(c.inc(mm, "dv"))
                        c.pu_read(u, "dv")
                        c.pu_read(2 + u, "dv")

            v_xl2 = c.cnt["dv"]

            for th in range(2):
                u = 4 + th
                with nc.Block(f"m_rt{slab}_{th}") as blk:
                    @blk.tensor
                    def _(t, th=th, u=u):
                        if th == 0:
                            c.wait(t, "dv", v_xl2)
                        c.pu_guard(t, u)
                        for k in range(NC):
                            mm = t.matmul(g.pu[u][0:8, 0:512],
                                          g.rw_sb[:, k * 8: k * 8 + 8],
                                          xl2_fm[:, k * SLAB + th * 512: k * SLAB + th * 512 + 512],
                                          start=(k == 0), stop=(k == NC - 1))
                        c.inc(mm, "pe")

                    v_mm = c.cnt["pe"]

                    @blk.vector
                    def _(v, th=th, u=u, v_mm=v_mm):
                        c.wait(v, "pe", v_mm)
                        c.inc(v.tensor_copy(lg_sb[0:8, th * 512: th * 512 + 512],
                                            g.pu[u][0:8, 0:512]), "dv")
                        c.pu_read(u, "dv")

            v_lg = c.cnt["dv"]
            with nc.Block(f"m_rtt{slab}") as blk:
                @blk.tensor
                def _(t):
                    c.wait(t, "dv", v_lg)
                    c.pu_guard(t, 6)
                    for q in range(8):
                        mm = t.transpose(g.pu[6][:, q * 8: q * 8 + 8],
                                         lg_sb[0:8, q * 128: q * 128 + 128],
                                         g.identf[0:8, 0:8])
                    c.inc(mm, "pe")

                v_tr = c.cnt["pe"]

                @blk.vector
                def _(v, v_tr=v_tr):
                    c.wait(v, "pe", v_tr)
                    c.inc(v.tensor_copy(lg_tm[:], g.pu[6][:, 0:64]), "dv")
                    c.pu_read(6, "dv")

            for q in range(8):
                with nc.Block(f"m_g{slab}_{q}") as blk:
                    @blk.vector
                    def _(v, q=q):
                        if q == 0:
                            c.wait(v, "dv")
                        c.inc(v.tensor_reduce(gsc[:, 0:1], lg_tm[:, q * 8: q * 8 + 8],
                                              axis=AX.X, op=ALU.max, negate=True), "dv")

                    v_mx = c.cnt["dv"]

                    @blk.scalar
                    def _(s, q=q, v_mx=v_mx):
                        c.wait(s, "dv", v_mx)
                        c.inc(s.activation(lg_tm[:, q * 8: q * 8 + 8],
                                           lg_tm[:, q * 8: q * 8 + 8],
                                           AF.Exp, bias=gsc[:, 0:1]), "ac")

                    v_ex = c.cnt["ac"]

                    @blk.vector
                    def _(v, q=q, v_ex=v_ex):
                        c.wait(v, "ac", v_ex)
                        pr = lg_tm[:, q * 8: q * 8 + 8]
                        v.tensor_reduce(gsc[:, 1:2], pr, axis=AX.X, op=ALU.add)
                        v.drain()
                        v.reciprocal(gsc[:, 2:3], gsc[:, 1:2])
                        v.drain()
                        v.tensor_scalar(pr, pr, gsc[:, 2:3], None, op0=ALU.mult)
                        v.drain()
                        v.max(m8[:], pr)
                        v.drain()
                        v.tensor_scalar(e1[:], pr, m8[:, 0:1], None, op0=ALU.is_equal)
                        v.drain()
                        v.tensor_tensor_scan(cum[:], e1[:], ones8[:], 0.0,
                                             op0=ALU.add, op1=ALU.mult)
                        v.drain()
                        v.tensor_scalar(cum[:], cum[:], 1.0, None, op0=ALU.is_equal)
                        v.drain()
                        v.tensor_tensor(e1[:], e1[:], cum[:], op=ALU.mult)
                        v.drain()
                        v.tensor_scalar(g_tm[:, q * 8: q * 8 + 8], e1[:], m8[:, 0:1],
                                        None, op0=ALU.mult)
                        v.drain()
                        v.scalar_tensor_tensor(pr2[:], e1[:], -1e30, pr,
                                               op0=ALU.mult, op1=ALU.add)
                        v.drain()
                        v.tensor_scalar(e1[:], pr2[:], m8[:, 1:2], None, op0=ALU.is_equal)
                        v.drain()
                        v.tensor_tensor_scan(cum[:], e1[:], ones8[:], 0.0,
                                             op0=ALU.add, op1=ALU.mult)
                        v.drain()
                        v.tensor_scalar(cum[:], cum[:], 1.0, None, op0=ALU.is_equal)
                        v.drain()
                        v.tensor_tensor(e1[:], e1[:], cum[:], op=ALU.mult)
                        v.drain()
                        v.tensor_scalar(e1[:], e1[:], m8[:, 1:2], None, op0=ALU.mult)
                        v.drain()
                        c.inc(v.tensor_tensor(g_tm[:, q * 8: q * 8 + 8],
                                              g_tm[:, q * 8: q * 8 + 8], e1[:],
                                              op=ALU.add), "dv")

            v_gates = c.cnt["dv"]
            if slab == 0:
                c.tap("g_tm", g_tm[:], "dv")
                c.tap("xl2_fm", xl2_fm[:], "dv")
                c.flush_taps()

            for e in range(E):
                with nc.Block(f"m_s1{slab}_{e}") as blk:
                    @blk.sync
                    def _(sy, e=e):
                        c.wait(sy, "pe")
                        c.inc(sy.dma_start(out=s1e[:], in_=g.ms1_h[e:e + 1, :]), "ds1", 16)

                    v_s1d = c.cnt["ds1"]

                    @blk.tensor
                    def _(t, e=e, v_s1d=v_s1d):
                        if e == 0:
                            c.wait(t, "dv", v_gates)
                        c.wait(t, "ds1", v_s1d)
                        c.pu_guard(t, 6)
                        c.pu_guard(t, 7)
                        for hh in range(2):
                            mm = t.matmul(g.pu[6 + hh][:], g.onesrow[:],
                                          s1e[0:1, hh * 512: hh * 512 + 512],
                                          start=True, stop=True)
                        c.inc(mm, "pe")

                    v_s1 = c.cnt["pe"]

                    @blk.vector
                    def _(v, v_s1=v_s1):
                        c.wait(v, "pe", v_s1)
                        v.tensor_copy(s1b[:, 0:512], g.pu[6][:])
                        c.inc(v.tensor_copy(s1b[:, 512:1024], g.pu[7][:]), "dv")
                        c.pu_read(6, "dv")
                        c.pu_read(7, "dv")

                pe_b = c.cnt["pe"]; din_b = c.cnt["din"]
                ac_b = c.cnt["ac"]; dv_b = c.cnt["dv"]
                with nc.Block(f"m_vy{slab}_{e}") as blk:
                    wtarget = {}

                    @blk.sync
                    def _(sy, e=e):
                        c.wait(sy, "dpp", g.pp_total)
                        for f in range(0, NF, 2):
                            sl_ = f"dw{f % NWB}"
                            if f >= NWB - 1:
                                c.wait(sy, "pe", pe_b + 2 * (f - (NWB - 1) + 1))
                            row = (e * NF + f) * 128
                            c.inc(sy.dma_start(out=wvb[f % NWB][:],
                                               in_=g.wvt_d[row:row + 128, :]), sl_, 16)
                            c.inc(sy.dma_start(out=wb_[f % NWB][:],
                                               in_=g.wt_d[row:row + 128, :]), sl_, 16)
                            wtarget[f] = c.cnt[sl_]

                    @blk.gpsimd
                    def _(gg, e=e):
                        c.wait(gg, "dpp", g.pp_total)
                        for f in range(1, NF, 2):
                            sl_ = f"dw{f % NWB}"
                            if f >= NWB - 1:
                                c.wait(gg, "pe", pe_b + 2 * (f - (NWB - 1) + 1))
                            row = (e * NF + f) * 128
                            c.inc(gg.dma_start(out=wvb[f % NWB][:],
                                               in_=g.wvt_d[row:row + 128, :]), sl_, 16)
                            c.inc(gg.dma_start(out=wb_[f % NWB][:],
                                               in_=g.wt_d[row:row + 128, :]), sl_, 16)
                            wtarget[f] = c.cnt[sl_]

                    @blk.tensor
                    def _(t, e=e):
                        for f in range(NF):
                            c.wait(t, f"dw{f % NWB}", wtarget[f])
                            for th in range(2):
                                idx = 2 * f + th
                                par = idx % 2
                                ua, ub = par * 2, par * 2 + 1
                                if idx < 2:
                                    c.pu_guard(t, ua)
                                    c.pu_guard(t, ub)
                                else:
                                    c.wait(t, "ac", ac_b + idx - 1)
                                    c.wait(t, "dv", dv_b + idx - 1)
                                for k in range(NC):
                                    t.matmul(g.pu[ua][:],
                                             wvb[f % NWB][:, k * 128: k * 128 + 128],
                                             xl2_fm[:, k * SLAB + th * 512: k * SLAB + th * 512 + 512],
                                             start=(k == 0), stop=(k == NC - 1))
                                for k in range(NC):
                                    mm = t.matmul(g.pu[ub][:],
                                                  wb_[f % NWB][:, k * 128: k * 128 + 128],
                                                  xl2_fm[:, k * SLAB + th * 512: k * SLAB + th * 512 + 512],
                                                  start=(k == 0), stop=(k == NC - 1))
                                c.inc(mm, "pe")

                    @blk.scalar
                    def _(s, e=e):
                        for f in range(NF):
                            for th in range(2):
                                idx = 2 * f + th
                                par = idx % 2
                                ua, ub = par * 2, par * 2 + 1
                                c.wait(s, "pe", pe_b + idx + 1)
                                c.wait(s, "dv", dv_b + idx)
                                c.inc(s.activation(ysc[:], g.pu[ub][:], AF.Gelu_apprx_tanh,
                                                   scale=g.s_sb[:, e * NF + f: e * NF + f + 1]),
                                      "ac")

                    @blk.vector
                    def _(v, e=e):
                        for f in range(NF):
                            for th in range(2):
                                idx = 2 * f + th
                                par = idx % 2
                                ua = par * 2
                                c.wait(v, "ac", ac_b + idx + 1)
                                cp = v.scalar_tensor_tensor(
                                    h_sb[:, f * SLAB + th * 512: f * SLAB + th * 512 + 512],
                                    g.pu[ua][:],
                                    g.sv_sb[:, e * NF + f: e * NF + f + 1],
                                    ysc[:], op0=ALU.mult, op1=ALU.mult)
                                c.inc(cp, "dv")

                for u in range(4):
                    c.pu_read(u, "dv")

                for th in range(2):
                    pe_b = c.cnt["pe"]; din_b = c.cnt["din"]; dv_b = c.cnt["dv"]
                    v_s1b = c.cnt["dv"]
                    with nc.Block(f"m_o{slab}_{e}_{th}") as blk:
                        wtarget = {}

                        @blk.sync
                        def _(sy, e=e, th=th):
                            c.wait(sy, "dpp", g.pp_total)
                            for f in range(0, NF, 2):
                                sl_ = f"dw{f % NWB}"
                                if f >= NWB - 1:
                                    c.wait(sy, "pe", pe_b + (f - (NWB - 1) + 1))
                                row = (e * NF + f) * 128
                                c.inc(sy.dma_start(out=w1b[f % NWB][:],
                                                   in_=g.w1t_d[row:row + 128, :]), sl_, 16)
                                wtarget[f] = c.cnt[sl_]

                        @blk.gpsimd
                        def _(gg, e=e, th=th):
                            c.wait(gg, "dpp", g.pp_total)
                            for f in range(1, NF, 2):
                                sl_ = f"dw{f % NWB}"
                                if f >= NWB - 1:
                                    c.wait(gg, "pe", pe_b + (f - (NWB - 1) + 1))
                                row = (e * NF + f) * 128
                                c.inc(gg.dma_start(out=w1b[f % NWB][:],
                                                   in_=g.w1t_d[row:row + 128, :]), sl_, 16)
                                wtarget[f] = c.cnt[sl_]

                        @blk.vector
                        def _(v, e=e, th=th):
                            for f in range(NF):
                                c.wait(v, f"dw{f % NWB}", wtarget[f])
                                if f >= NWB:
                                    c.wait(v, "pe", pe_b + (f - NWB + 1))
                                c.inc(v.tensor_tensor(w1b[f % NWB][:], w1b[f % NWB][:],
                                                      s1b[:], op=ALU.mult), "dv")

                        @blk.tensor
                        def _(t, e=e, th=th):
                            for u in range(8):
                                c.pu_guard(t, u)
                            for f in range(NF):
                                c.wait(t, "dv", dv_b + f + 1)
                                for dh in range(2):
                                    for tq in range(4):
                                        u = dh * 4 + tq
                                        mm = t.matmul(
                                            g.pu[u][:],
                                            h_sb[:, f * SLAB + th * 512 + tq * 128:
                                                 f * SLAB + th * 512 + tq * 128 + 128],
                                            w1b[f % NWB][:, dh * 512: dh * 512 + 512],
                                            start=(f == 0), stop=(f == NF - 1))
                                c.inc(mm, "pe")

                    v_o = c.cnt["pe"]
                    with nc.Block(f"m_oe{slab}_{e}_{th}") as blk:
                        @blk.vector
                        def _(v, e=e, th=th, slab=slab, v_o=v_o):
                            c.wait(v, "pe", v_o)
                            for dh in range(2):
                                for tq in range(4):
                                    u = dh * 4 + tq
                                    tl = th * 4 + tq
                                    tt = slab * 8 + tl
                                    gcol = g_tm[:, tl * 8 + e: tl * 8 + e + 1]
                                    if e == 0:
                                        base = g.x2_tm[:, tt * D + dh * 512: tt * D + dh * 512 + 512]
                                    else:
                                        base = acc[:, tl * D + dh * 512: tl * D + dh * 512 + 512]
                                    mm = v.scalar_tensor_tensor(
                                        acc[:, tl * D + dh * 512: tl * D + dh * 512 + 512],
                                        g.pu[u][:], gcol, base, op0=ALU.mult, op1=ALU.add)
                                    c.pu_read(u, "dv", c.cnt["dv"] + 1)
                            c.inc(mm, "dv")

            v_acc = c.cnt["dv"]
            with nc.Block(f"m_st{slab}") as blk:
                @blk.sync
                def _(sy, slab=slab, v_acc=v_acc):
                    c.wait(sy, "dv", v_acc)
                    c.inc(sy.dma_start(
                        out=g.out_hbm[slab * SLAB:(slab + 1) * SLAB, :].rearrange(
                            "(t p) d -> p t d", p=128),
                        in_=acc[:]), "dout", 16)

            v_store = c.cnt["dout"]


def _host_consts():
    p32 = np.arange(128)[:, None] % 32
    j32 = np.arange(32)[None, :]
    masks = np.zeros((128, 128), np.float32)
    masks[:, 0:32] = np.where(p32 < j32, -1.0, 0.0)
    masks[:, 32:64] = np.where(p32 > j32, -1.0, 0.0)
    masks[:, 64:96] = np.where(p32 <= j32, 1.0, 0.0)
    masks[:, 96:128] = np.where(p32 == j32, 1.0, 0.0)
    clamp = np.zeros((128, 4), np.float32)
    clamp[:, 0] = 0.05
    clamp[:, 1] = 0.05
    return {
        "ident_bf": np.eye(128, dtype=BF),
        "ident_f32": np.eye(128, dtype=np.float32),
        "masks_c": masks,
        "clamp": clamp,
        "ones_col": np.ones((128, 1), dtype=BF),
        "ones_row": np.ones((1, 128), dtype=np.float32),
        "ones_row_bf": np.ones((1, 128), dtype=BF),
        "cbias": np.tile(np.array([1.0, 1e-6, 1e-5, 0.0], np.float32), (128, 1)),
    }


def _chunked(w):
    cnum, j = w.shape[0] // 128, w.shape[1]
    return np.ascontiguousarray(w.reshape(cnum, 128, j).transpose(1, 0, 2).reshape(128, cnum * j))


def _expertcols(s):
    e, ff = s.shape
    return np.ascontiguousarray(s.reshape(e, ff // 128, 128).transpose(2, 0, 1).reshape(128, e * (ff // 128)))


def make_in_maps(inputs):
    f = np.float32
    a = lambda k: np.ascontiguousarray(np.asarray(inputs[k], dtype=f))
    shared = {
        "Wq": a("L0_attn_Wq"), "Wk": a("L0_attn_Wk"), "Wv": a("L0_attn_Wv"),
        "Wb": _chunked(a("L0_attn_Wb")),
        "conv_q": _chunked(a("L0_attn_conv_q")),
        "conv_k": _chunked(a("L0_attn_conv_k")),
        "conv_v": _chunked(a("L0_attn_conv_v")),
        "fir_long": _chunked(a("L0_attn_fir_long").reshape(D, 64)),
        "fir_short": _chunked(a("L0_attn_fir_short").reshape(D, 5)),
        "W1": a("L0_attn_W1"),
        "b1": np.ascontiguousarray(a("L0_attn_b1").reshape(8, 128).T),
        "W2": _chunked(a("L0_attn_W2")),
        "bbT": np.ascontiguousarray(a("L0_attn_base_bias").T),
        "b2c": a("L0_attn_b2").reshape(4, 1),
        "log_temp": a("L0_attn_log_temp").reshape(1, 4),
        "crl": a("L0_attn_conv_residual_logit").reshape(1, 4),
        "rms_w": a("L0_attn_rms_w").reshape(1, DK),
        "Wo": a("L0_attn_Wo"),
        "router_w": _chunked(a("L0_ffn_router_w")),
        "m_wv": a("L0_ffn_wv").reshape(E * D, FF),
        "m_sv": _expertcols(a("L0_ffn_sv").reshape(E, FF)),
        "m_w": a("L0_ffn_w").reshape(E * D, FF),
        "m_s": _expertcols(a("L0_ffn_s").reshape(E, FF)),
        "m_w1": a("L0_ffn_w1").reshape(E * FF, D),
        "m_s1": a("L0_ffn_s1").reshape(E, D),
    }
    shared.update(_host_consts())
    x_full = np.ascontiguousarray(np.asarray(inputs["L0_attn_x"], dtype=f))
    in_maps = []
    for core in range(8):
        m = dict(shared)
        m["x"] = np.ascontiguousarray(x_full[core])
        in_maps.append(m)
    return in_maps


def kernel(**inputs):
    in_maps = make_in_maps(inputs)
    nc = build_program()
    res = run_bass_kernel_spmd(nc, in_maps, core_ids=list(range(8)))
    out = np.stack([np.asarray(r["out"], dtype=np.float32) for r in res.results], axis=0)
    return out
